# version 01
# baseline (speedup 1.0000x reference)
import contextlib
import numpy as np
import ml_dtypes
import concourse.bass as bass
import concourse.mybir as mybir
from concourse.bass_utils import run_bass_kernel_spmd

F32 = mybir.dt.float32
BF16 = mybir.dt.bfloat16
AF = mybir.ActivationFunctionType
ALU = mybir.AluOpType
AXX = mybir.AxisListType.X
bfnp = ml_dtypes.bfloat16

S = 2048
D = 2048
NB = 2
NKH, NVH, DK, DV = 16, 32, 128, 128
NFO = 96
NDT = 16
NC_ = 16
FF = 8192

DEV_PHASES = 999
DEV_NB = NB
DEV_DUMP = ()

SEM_MAP = {}
def _smap(prefix, pairs):
    for a, b in pairs.items():
        SEM_MAP[prefix + a] = b
_smap('p1', dict(ld0='D0', ld1='D1', z1='V0', cA='V1', cB='V2', sq='S0',
                 sd='S1', tp='T0', tev='S2', done='done'))
_smap('p2', dict(lw0='GD0', lw1='GD1', pe='T1', evS='S3', evV='V3',
                 so0='D2', so1='D3', done='done'))
_smap('c3', dict(ld0='D0', ld1='D1', lw0='GD0', lw1='GD1', pz='G0', v='V0',
                 s='S0', sq='S1', ssmm='T0', ssev='V1', so0='D2', so1='D3',
                 sx0='D4', sx1='D5', done='done'))
_smap('r4', dict(ld='D0', s='S0', vx='V0', vb='V1', vu='V2', sr='D1',
                 ebm='T0', eb='S1', ebr='S2', ls='D2', s2='S3', v2='V3',
                 sk='D3', tp='T1', tev='S4', vc='V4', se='S5', vn='V5',
                 done='done'))
_smap('d5', dict(dl0='D0', dl1='D1', dg0='GD0', dg1='GD1', sz='G0', pga='T0',
                 evS='S0', evV='V0', pgb='T1', vm='V1', sm='S1', sq='S2',
                 ps='T2', po='T3', vo='V2', so='S3', vy='V3', py='T4', sy='S4',
                 vs='V4', sd0='D2', sd1='D3', dbg='D6', done='done'))
_smap('w6', dict(lx='D0', x2b='D3', lw0='GD0', lw1='GD1', lw2='GD2', lw3='GD3',
                 ly0='D4', ly1='D5', ly2='D7', ly3='D8', ly4='D9', ly5='D10',
                 ly6='D11', ly7='D12', pe='T0', sq='S0', lnv='V0', sd='S1',
                 lnw='V1', x2='D2', done='done'))
_smap('prA', dict(l0='D0', l1='D1', v='V0', s0='D2', s1='D3', done='done'))
_smap('z2', dict(ld0='D0', ld1='D1', z1='V0', cA='V1', cB='V2', sq='S0',
                 sd='S1', tp='T0', tev='S2', done='done'))
_smap('q5', dict(lg0='D0', lg1='D1', lv0='D4', lv1='D5', pe='T1', pv='T2',
                 evS='S3', evV='V3', eva='S4', so0='D2', so1='D3', dbg='D6',
                 on='G0', done='done'))
_smap('a5', dict(lq0='D0', lq1='D1', lq2='D4', lq3='D5', ps='T0', pt='T2',
                 pm='T3', ph='T4', pmh='T1', ex='S0', pv='S1', th='S2', me='S3',
                 mx='V0', sm='V1', mh='V2', dbg='D6', done='done'))
_smap('m6', dict(lw0='GD0', lw1='GD1', lu0='GD2', lu1='GD3', lb='D0',
                 pe='T0', pu='T1', sg='S0', mv='V0', sm_='D1', done='done'))
_smap('o6', dict(lm0='D0', lm1='D1', lm2='D4', lm3='D5', lw0='GD0', lw1='GD1',
                 lx='D7', pe='T0', sq='S0', lnv='V0', sd='S1', lnw='V1',
                 so='D2', sob='D3', ev='S2', ea='S3', pt='T1', mus='V2',
                 done='done'))
_smap('prB', dict(lq0='GD0', lq1='GD1', lk0='GD2', lk1='GD3', pe='T0', ev='V0', sg='D2',
                  lo0='GD0', lo1='GD1', lv0='GD0', lv1='GD1', sv0='D4', sv1='D6',
                  l60='GD0', l61='GD1', l62='GD2', l63='GD3',
                  zzz='D0',
                  lo='GD0', la='GD1', pw='T1', evw='S0', sw='D3',
                  lv='GD2', pv='T2', evv='S1', sv='D4',
                  lA='GD3', vA='V1', pt='T3', evt='S2', st0='D5', st1='D7', lt='GD2',
                  bc='T4', bev='V2', done='done'))

DEV_CORES = 8
DEV_DBG = False
DEV_DELTA_LIMIT = 256


class Ctx:
    def __init__(self, nc, st):
        self.nc, self.st = nc, st
        self.sems, self.cnt = {}, {}

    def sem(self, name):
        name = SEM_MAP.get(name, name)
        if name not in self.sems:
            self.sems[name] = self.st.enter_context(self.nc.semaphore(name))
            self.cnt[name] = 0
        return self.sems[name]

    def inc(self, ins, name, k=1):
        name = SEM_MAP.get(name, name)
        sem = self.sem(name)
        self.cnt[name] = self.cnt.get(name, 0) + k
        ins.then_inc(sem, k)
        return self.cnt[name]

    def wait(self, eng, name, val=None):
        name = SEM_MAP.get(name, name)
        if val is None:
            val = self.cnt.get(name, 0)
        if val > 0:
            eng.wait_ge(self.sem(name), val)

    def n(self, name):
        name = SEM_MAP.get(name, name)
        return self.cnt.get(name, 0)


def build():
    nc = bass.Bass("TRN2", target_bir_lowering=False)
    dt = nc.dram_tensor
    xin = dt("xin", [NB * S, D], F32, kind="ExternalInput").ap()
    w_qkvz = dt("w_qkvz", [D, 12288], F32, kind="ExternalInput").ap()
    w_ba = dt("w_ba", [D, 64], F32, kind="ExternalInput").ap()
    convw = dt("convw", [8192, 4], F32, kind="ExternalInput").ap()
    w_out = dt("w_out", [4096, D], F32, kind="ExternalInput").ap()
    Wq = dt("Wq", [D, D], F32, kind="ExternalInput").ap()
    Wk = dt("Wk", [D, D], F32, kind="ExternalInput").ap()
    Wv = dt("Wv", [D, D], F32, kind="ExternalInput").ap()
    Wo = dt("Wo", [D, D], F32, kind="ExternalInput").ap()
    Amat = dt("Amat", [D, 16], F32, kind="ExternalInput").ap()
    Bmat = dt("Bmat", [16, FF], F32, kind="ExternalInput").ap()
    Win = dt("Win", [FF, D], F32, kind="ExternalInput").ap()
    Wup = dt("Wup", [FF, D], F32, kind="ExternalInput").ap()
    Wdn = dt("Wdn", [D, FF], F32, kind="ExternalInput").ap()
    dtb_col = dt("dtb_col", [32, 1], F32, kind="ExternalInput").ap()
    alog_col = dt("alog_col", [32, 1], F32, kind="ExternalInput").ap()
    normw_col = dt("normw_col", [128, 1], F32, kind="ExternalInput").ap()
    ln1w_col = dt("ln1w_col", [128, 16], F32, kind="ExternalInput").ap()
    ln2w_col = dt("ln2w_col", [128, 16], F32, kind="ExternalInput").ap()
    ln1w_row = dt("ln1w_row", [1, D], F32, kind="ExternalInput").ap()
    c_identb = dt("c_identb", [128, 128], BF16, kind="ExternalInput").ap()
    c_identf = dt("c_identf", [128, 128], F32, kind="ExternalInput").ap()
    c_sel = dt("c_sel", [48, 4096], F32, kind="ExternalInput").ap()
    c_ones = dt("c_ones", [128, 1], BF16, kind="ExternalInput").ap()
    c_onesrow = dt("c_onesrow", [1, 128], F32, kind="ExternalInput").ap()
    c_maskA = dt("c_maskA", [128, 128], F32, kind="ExternalInput").ap()
    c_maskB = dt("c_maskB", [128, 128], F32, kind="ExternalInput").ap()
    c_maskW = dt("c_maskW", [128, 128], F32, kind="ExternalInput").ap()
    c_maskUb = dt("c_maskUb", [128, 128], BF16, kind="ExternalInput").ap()
    out = dt("out", [NB * S, D], F32, kind="ExternalOutput").ap()
    scratch = {}
    def scr(name, shape, dtyp):
        kind = "ExternalOutput" if name in DEV_DUMP else "Internal"
        scratch[name] = dt(name, shape, dtyp, kind=kind).ap()
        return scratch[name]
    qkvzT = scr("qkvzT", [12288, S], F32)
    ba_d = scr("ba_d", [64, S], F32)
    convq = scr("convq", [D, S], BF16)
    convk = scr("convk", [D, S], BF16)
    convv = scr("convv", [4096, S], F32)
    rows_d = scr("rows_d", [32, 3 * S], F32)
    rowsK_d = scr("rowsK_d", [16, 3 * S], F32)
    ssk_d = scr("ssk_d", [16, S], F32)
    ssq_d = scr("ssq_d", [16, S], F32)
    yT_d = scr("yT_d", [4096, S], BF16)
    x2_d = scr("x2_d", [NB * S, D], F32)
    qgt_d = scr("qgt_d", [D, S], BF16)
    m_d = scr("m_d", [FF, S], BF16)
    g_d = scr("g_d", [D, D], BF16)
    va_d = scr("va_d", [D, 256], BF16)
    wint_d = scr("wint_d", [D, FF], BF16)
    wupt_d = scr("wupt_d", [D, FF], BF16)
    wdnt_d = scr("wdnt_d", [FF, D], BF16)
    woute_d = scr("woute_d", [4096, D], BF16)
    dbgf = scr("dbgf", [128, 4096], F32)
    dbgb = scr("dbgb", [128, 4096], BF16)

    with contextlib.ExitStack() as _st:
        E = _st.enter_context
        c = Ctx(nc, _st)
        actT = E(nc.sbuf_tensor("actT", [128, NDT * S], BF16))
        wring = E(nc.sbuf_tensor("wring", [128, 2 * NDT * 128], BF16))
        obuf = E(nc.sbuf_tensor("obuf", [128, 2 * S], F32))
        stage = E(nc.sbuf_tensor("stage", [128, 2 * S], F32))
        z1tok = E(nc.sbuf_tensor("z1tok", [128, 4104], F32))
        bar = E(nc.sbuf_tensor("bar", [128, 12288], BF16))
        beta32 = E(nc.sbuf_tensor("beta32", [32, S], F32))
        cw = E(nc.sbuf_tensor("cw", [128, 8], F32))
        cols = E(nc.sbuf_tensor("cols", [128, 2048], F32))
        identb = E(nc.sbuf_tensor("identb", [128, 128], BF16))
        identf = E(nc.sbuf_tensor("identf", [128, 128], F32))
        selS = E(nc.sbuf_tensor("selS", [48, 4096], F32))
        onesS = E(nc.sbuf_tensor("onesS", [128, 1], BF16))
        onesrowS = E(nc.sbuf_tensor("onesrowS", [1, 128], F32))
        lnrow1 = E(nc.sbuf_tensor("lnrow1", [1, D], F32))
        maskA = E(nc.sbuf_tensor("maskA", [128, 128], F32))
        maskB = E(nc.sbuf_tensor("maskB", [128, 128], F32))
        maskW = E(nc.sbuf_tensor("maskW", [128, 128], F32))
        maskUbS = E(nc.sbuf_tensor("maskUbS", [128, 128], BF16))
        mhT_s = E(nc.sbuf_tensor("mhT_s", [16, 2048], BF16))
        Bs = E(nc.sbuf_tensor("Bs", [16, FF], BF16))
        smallv = E(nc.sbuf_tensor("smallv", [128, 64], F32))
        psA = E(nc.psum_tensor("psA", [128, 2048], F32))
        psB = E(nc.psum_tensor("psB", [128, 2048], F32))

        blk = E(nc.Block())
        XB = {}

        def ph_const():
            def fsync(sy):
                for dst, src in [(identb[:], c_identb), (identf[:], c_identf),
                                 (selS[:], c_sel), (onesS[:], c_ones),
                                 (maskA[:], c_maskA), (maskB[:], c_maskB),
                                 (maskW[:], c_maskW),
                                 (maskUbS[:], c_maskUb),
                                 (smallv[0:32, 0:1], dtb_col),
                                 (smallv[0:32, 1:2], alog_col),
                                 (smallv[:, 2:3], normw_col),
                                 (smallv[:, 4:20], ln1w_col),
                                 (smallv[:, 20:36], ln2w_col),
                                 (onesrowS[:], c_onesrow),
                                 (lnrow1[:], ln1w_row)]:
                    c.inc(sy.dma_start(out=dst, in_=src), 'cst', 16)
            blk.sync(fsync)
            def fscal(s):
                c.wait(s, 'cst')
                c.inc(s.activation(out=smallv[0:32, 38:39], in_=smallv[0:32, 1:2],
                                   func=AF.Exp), 'cstS')
            blk.scalar(fscal)
            def fvec(v):
                c.wait(v, 'cstS')
                c.inc(v.tensor_scalar(out=smallv[0:32, 3:4], in0=smallv[0:32, 38:39],
                                      scalar1=-1.0, scalar2=None, op0=ALU.mult),
                      'cstV')
                c.inc(v.memset(smallv[:, 40:41], 1e-5), 'cstV')
                c.inc(v.memset(smallv[:, 41:42], 1e-6), 'cstV')
                c.inc(v.memset(smallv[:, 42:43], 128.0 * 1e-6), 'cstV')
                c.inc(v.memset(smallv[:, 43:44], 1.0), 'cstV')
                c.inc(v.memset(smallv[:, 45:46], float(1.0 / np.sqrt(128.0))), 'cstV')
                c.inc(v.tensor_scalar(out=smallv[:, 44:45], in0=smallv[:, 2:3],
                                      scalar1=float(np.sqrt(128.0)), scalar2=None,
                                      op0=ALU.mult), 'cstV')
            blk.vector(fvec)
            def fgp(g):
                c.inc(g.dma_start(out=Bs[0:16, :], in_=Bmat), 'cstG', 16)
            blk.gpsimd(fgp)

        def ph_p1(b, tag, src_ap, prev_done, mode='ln'):
            B = {s: c.n(f'{tag}{s}') for s in ('ld0', 'ld1', 'z1', 'tp', 'tev',
                                               'sq', 'cA', 'sd', 'cB')}
            ncB = 2 if mode == 'ln' else 6
            nsd = 1 if mode == 'ln' else 2
            for i in range(16):
                sl = i % 2
                def fsync(sy, i=i, sl=sl):
                    if i == 0 and prev_done:
                        c.wait(sy, prev_done)
                    if i < 2:
                        c.wait(sy, f'{tag}ld{sl}', B[f'ld{sl}'])
                    if i >= 2:
                        c.wait(sy, f'{tag}z1', B['z1'] + i - 1)
                        c.wait(sy, f'{tag}cA', B['cA'] + 4 * (i - 1))
                        c.wait(sy, f'{tag}sq', B['sq'] + i - 1)
                    c.inc(sy.dma_start(out=stage[:, sl * S:(sl + 1) * S],
                                       in_=src_ap[i * 128:(i + 1) * 128, :]),
                          f'{tag}ld{sl}', 16)
                blk.sync(fsync)
                def fscal(s, i=i, sl=sl):
                    c.wait(s, f'{tag}ld{sl}', B[f'ld{sl}'] + 16 * (i // 2 + 1))
                    if i >= 2:
                        c.wait(s, f'{tag}tp', B['tp'] + 16 * (i - 1))
                    c.inc(s.activation(out=z1tok[:, sl * S:(sl + 1) * S],
                                       in_=stage[:, sl * S:(sl + 1) * S],
                                       func=AF.Square,
                                       accum_out=cols[:, 16 + i:17 + i]), f'{tag}sq', 1)
                blk.scalar(fscal)
                def fvec(v, i=i, sl=sl):
                    c.wait(v, f'{tag}ld{sl}', B[f'ld{sl}'] + 16 * (i // 2 + 1))
                    c.inc(v.reduce_sum(cols[:, i:i + 1], stage[:, sl * S:(sl + 1) * S],
                                       axis=AXX), f'{tag}cA', 1)
                    c.wait(v, f'{tag}cA')
                    c.wait(v, f'{tag}sq', B['sq'] + i + 1)
                    mu = cols[:, 32 + i:33 + i]
                    var = cols[:, 48 + i:49 + i]
                    musq = cols[:, 96:97]
                    c.inc(v.tensor_scalar(out=mu, in0=cols[:, i:i + 1], scalar1=1.0 / D,
                                          scalar2=None, op0=ALU.mult), f'{tag}cA', 1)
                    c.wait(v, f'{tag}cA')
                    c.inc(v.tensor_tensor(out=musq, in0=mu, in1=mu, op=ALU.mult),
                          f'{tag}cA', 1)
                    c.wait(v, f'{tag}cA')
                    c.inc(v.scalar_tensor_tensor(out=var, in0=cols[:, 16 + i:17 + i],
                                                 scalar=1.0 / D, in1=musq,
                                                 op0=ALU.mult, op1=ALU.subtract),
                          f'{tag}cA', 1)
                blk.vector(fvec)
                def fscal2(s, i=i):
                    if i == 0:
                        c.wait(s, 'cstV')
                    c.wait(s, f'{tag}cA', B['cA'] + 4 * (i + 1))
                    if i > 0:
                        c.wait(s, f'{tag}cB', B['cB'] + ncB * (i - 1) + 1)
                    c.inc(s.activation(out=cols[:, 97:98], in_=cols[:, 48 + i:49 + i],
                                       func=AF.Sqrt, bias=smallv[:, 40:41]), f'{tag}sd', 1)
                    if mode == 'z2':
                        if i > 0:
                            c.wait(s, f'{tag}cB', B['cB'] + ncB * (i - 1) + 4)
                        c.wait(s, f'{tag}cB', B['cB'] + ncB * i + 3)
                        c.inc(s.activation(out=cols[:, 100:101], in_=cols[:, 99:100],
                                           func=AF.Sqrt, bias=smallv[:, 41:42]),
                              f'{tag}sd', 1)
                blk.scalar(fscal2)
                def fvec2(v, i=i, sl=sl):
                    c.wait(v, f'{tag}sd', B['sd'] + nsd * i + 1)
                    mu = cols[:, 32 + i:33 + i]
                    rs = cols[:, 64 + i:65 + i]
                    ms = cols[:, 80 + i:81 + i]
                    c.inc(v.reciprocal(rs, cols[:, 97:98]), f'{tag}cB', 1)
                    c.wait(v, f'{tag}cB')
                    if mode == 'z2':
                        var = cols[:, 48 + i:49 + i]
                        c.inc(v.tensor_tensor(out=cols[:, 98:99], in0=var, in1=rs,
                                              op=ALU.mult), f'{tag}cB', 1)
                        c.wait(v, f'{tag}cB')
                        c.inc(v.tensor_tensor(out=cols[:, 99:100], in0=cols[:, 98:99],
                                              in1=rs, op=ALU.mult), f'{tag}cB', 1)
                        c.wait(v, f'{tag}cB')
                        c.wait(v, f'{tag}sd', B['sd'] + nsd * i + 2)
                        c.inc(v.reciprocal(cols[:, 101:102], cols[:, 100:101]),
                              f'{tag}cB', 1)
                        c.wait(v, f'{tag}cB')
                        c.inc(v.tensor_tensor(out=cols[:, 102:103], in0=rs,
                                              in1=cols[:, 101:102], op=ALU.mult),
                              f'{tag}cB', 1)
                        c.wait(v, f'{tag}cB')
                        sa = cols[:, 102:103]
                        c.inc(v.tensor_tensor(out=ms, in0=mu, in1=sa, op=ALU.mult),
                              f'{tag}cB', 1)
                        c.wait(v, f'{tag}cB')
                        c.inc(v.tensor_scalar(out=z1tok[:, sl * S:(sl + 1) * S],
                                              in0=stage[:, sl * S:(sl + 1) * S],
                                              scalar1=sa, scalar2=ms,
                                              op0=ALU.mult, op1=ALU.subtract),
                              f'{tag}z1', 1)
                    else:
                        c.inc(v.tensor_tensor(out=ms, in0=mu, in1=rs, op=ALU.mult),
                              f'{tag}cB', 1)
                        c.wait(v, f'{tag}cB')
                        c.inc(v.tensor_scalar(out=z1tok[:, sl * S:(sl + 1) * S],
                                              in0=stage[:, sl * S:(sl + 1) * S],
                                              scalar1=rs, scalar2=ms,
                                              op0=ALU.mult, op1=ALU.subtract),
                              f'{tag}z1', 1)
                blk.vector(fvec2)
                def ftens(t, i=i, sl=sl):
                    if i == 0:
                        c.wait(t, 'cst')
                        if prev_done:
                            c.wait(t, prev_done)
                    c.wait(t, f'{tag}z1', B['z1'] + i + 1)
                    for j in range(16):
                        k = i * 16 + j
                        if k >= 4:
                            c.wait(t, f'{tag}tev', B['tev'] + k - 3)
                        c.inc(t.transpose(psA[:, (k % 4) * 512:(k % 4) * 512 + 128],
                                          z1tok[:, sl * S + j * 128: sl * S + (j + 1) * 128],
                                          identf[:]), f'{tag}tp', 1)
                blk.tensor(ftens)
                def fscal3(s, i=i):
                    for j in range(16):
                        k = i * 16 + j
                        c.wait(s, f'{tag}tp', B['tp'] + k + 1)
                        c.inc(s.copy(actT[:, j * S + i * 128: j * S + (i + 1) * 128],
                                     psA[:, (k % 4) * 512:(k % 4) * 512 + 128]),
                              f'{tag}tev', 1)
                blk.scalar(fscal3)

        def ph_p2(b, tag, p1tag):
            tev = c.n(f'{p1tag}tev')
            B = {s: c.n(f'{tag}{s}') for s in ('pe', 'lw0', 'lw1', 'evS', 'evV',
                                               'so0', 'so1')}
            NBLK = NFO + 1
            def fgp(g):
                for fo in range(NBLK):
                    par = fo % 2
                    if fo < 2:
                        c.wait(g, f'{tag}lw{par}', B[f'lw{par}'])
                    if fo >= 2:
                        c.wait(g, f'{tag}pe', B['pe'] + (fo - 1) * NDT)
                    nco = 64 if fo == NFO else 128
                    for d in range(NDT):
                        src = (w_ba[d * 128:(d + 1) * 128, :] if fo == NFO else
                               w_qkvz[d * 128:(d + 1) * 128, fo * 128:(fo + 1) * 128])
                        c.inc(g.dma_start(
                            out=wring[:, (par * NDT + d) * 128:(par * NDT + d) * 128 + nco],
                            in_=src), f'{tag}lw{par}', 16)
            blk.gpsimd(fgp)
            def ftens(t):
                c.wait(t, f'{p1tag}tev', tev)
                for fo in range(NBLK):
                    par = fo % 2
                    ps = psA if par == 0 else psB
                    if fo >= 2:
                        c.wait(t, (f'{tag}evS' if par == 0 else f'{tag}evV'),
                               B['evS' if par == 0 else 'evV'] + 4 * ((fo - 2) // 2 + 1))
                    c.wait(t, f'{tag}lw{par}', B[f'lw{par}'] + (fo // 2 + 1) * 16 * NDT)
                    nco = 64 if fo == NFO else 128
                    for d in range(NDT):
                        wt = wring[:, (par * NDT + d) * 128:(par * NDT + d) * 128 + nco]
                        for gi in range(4):
                            ins = t.matmul(ps[0:nco, gi * 512:(gi + 1) * 512], wt,
                                           actT[:, d * S + gi * 512: d * S + (gi + 1) * 512],
                                           start=(d == 0), stop=(d == NDT - 1))
                        c.inc(ins, f'{tag}pe', 1)
            blk.tensor(ftens)
            def fscal(s):
                for fo in range(0, NBLK, 2):
                    c.wait(s, f'{tag}pe', B['pe'] + (fo + 1) * NDT)
                    if fo >= 2:
                        c.wait(s, f'{tag}so0', B['so0'] + 16 * ((fo - 2) // 2 + 1))
                    nco = 64 if fo == NFO else 128
                    for gi in range(4):
                        c.inc(s.copy(obuf[0:nco, gi * 512:(gi + 1) * 512],
                                     psA[0:nco, gi * 512:(gi + 1) * 512]), f'{tag}evS', 1)
            blk.scalar(fscal)
            def fvec(v):
                for fo in range(1, NBLK, 2):
                    c.wait(v, f'{tag}pe', B['pe'] + (fo + 1) * NDT)
                    if fo >= 2:
                        c.wait(v, f'{tag}so1', B['so1'] + 16 * ((fo - 2) // 2 + 1))
                    for gi in range(4):
                        c.inc(v.tensor_copy(obuf[:, S + gi * 512: S + (gi + 1) * 512],
                                            psB[:, gi * 512:(gi + 1) * 512]), f'{tag}evV', 1)
            blk.vector(fvec)
            def fsync(sy):
                for fo in range(NBLK):
                    par = fo % 2
                    c.wait(sy, (f'{tag}evS' if par == 0 else f'{tag}evV'),
                           B['evS' if par == 0 else 'evV'] + 4 * (fo // 2 + 1))
                    if fo == NFO:
                        dst, srcb = ba_d, obuf[0:64, par * S:(par + 1) * S]
                    else:
                        dst = qkvzT[fo * 128:(fo + 1) * 128, :]
                        srcb = obuf[:, par * S:(par + 1) * S]
                    c.inc(sy.dma_start(out=dst, in_=srcb), f'{tag}so{par}', 16)
                c.wait(sy, f'{tag}so0')
                c.wait(sy, f'{tag}so1')
                c.inc(sy.nop(), f'{tag}done', 1)
            blk.sync(fsync)


        def ph_conv(b, tag, p2tag):
            B = {s: c.n(f'{tag}{s}') for s in ('v', 'ld0', 'ld1', 'lw0', 'lw1',
                                               'pz', 's', 'sq', 'ssmm', 'ssev')}
            blocks = []
            for h in range(16):
                blocks.append((6 * h + 0, 128 * h, 'q', h))
                blocks.append((6 * h + 1, 2048 + 128 * h, 'k', h))
                blocks.append((6 * h + 2, 4096 + 256 * h, 'v', 2 * h))
                blocks.append((6 * h + 3, 4096 + 256 * h + 128, 'v', 2 * h + 1))
            nq = 0
            def cpad(sl):
                return z1tok[:, sl * 2052: sl * 2052 + 2052]
            for bi, (fo, wofs, kind, idx) in enumerate(blocks):
                sl = bi % 2
                def fgp(g, bi=bi, sl=sl, wofs=wofs):
                    if bi < 2:
                        c.wait(g, f'{p2tag}done')
                        c.wait(g, f'{tag}lw{sl}', B[f'lw{sl}'])
                        c.inc(g.memset(cpad(sl)[:, 0:4], 0.0), f'{tag}pz', 1)
                    if bi >= 2:
                        c.wait(g, f'{tag}v', B['v'] + 4 * (bi - 1))
                    c.inc(g.dma_start(out=cw[:, sl * 4:(sl + 1) * 4],
                                      in_=convw[wofs:wofs + 128, :]), f'{tag}lw{sl}', 16)
                blk.gpsimd(fgp)
                def fsync(sy, bi=bi, sl=sl, fo=fo):
                    if bi == 0:
                        c.wait(sy, f'{p2tag}done')
                    if bi < 2:
                        c.wait(sy, f'{tag}ld{sl}', B[f'ld{sl}'])
                    if bi >= 2:
                        c.wait(sy, f'{tag}v', B['v'] + 4 * (bi - 1))
                    c.inc(sy.dma_start(out=cpad(sl)[:, 4:2052],
                                       in_=qkvzT[fo * 128:(fo + 1) * 128, :]),
                          f'{tag}ld{sl}', 16)
                blk.sync(fsync)
                def fvec(v, bi=bi, sl=sl, kind=kind):
                    c.wait(v, f'{tag}ld{sl}', B[f'ld{sl}'] + 16 * (bi // 2 + 1))
                    c.wait(v, f'{tag}lw{sl}', B[f'lw{sl}'] + 16 * (bi // 2 + 1))
                    if bi < 2:
                        c.wait(v, f'{tag}pz', B['pz'] + min(bi + 1, 2))
                    if bi >= 2:
                        c.wait(v, f'{tag}s', c.n(f'{tag}s'))
                        c.wait(v, f'{tag}so{sl}', c.n(f'{tag}so{sl}'))
                        c.wait(v, f'{tag}sx{sl}', c.n(f'{tag}sx{sl}'))
                    accA = obuf[:, sl * S:(sl + 1) * S]
                    accB = stage[:, sl * S:(sl + 1) * S]
                    pj = cpad(sl)
                    c.inc(v.tensor_scalar(out=accA, in0=pj[:, 1:2049],
                                          scalar1=cw[:, sl * 4 + 0: sl * 4 + 1],
                                          scalar2=None, op0=ALU.mult), f'{tag}v', 1)
                    c.wait(v, f'{tag}v')
                    c.inc(v.scalar_tensor_tensor(out=accB, in0=pj[:, 2:2050],
                                                 scalar=cw[:, sl * 4 + 1: sl * 4 + 2],
                                                 in1=accA, op0=ALU.mult, op1=ALU.add),
                          f'{tag}v', 1)
                    c.wait(v, f'{tag}v')
                    c.inc(v.scalar_tensor_tensor(out=accA, in0=pj[:, 3:2051],
                                                 scalar=cw[:, sl * 4 + 2: sl * 4 + 3],
                                                 in1=accB, op0=ALU.mult, op1=ALU.add),
                          f'{tag}v', 1)
                    c.wait(v, f'{tag}v')
                    c.inc(v.scalar_tensor_tensor(out=accB, in0=pj[:, 4:2052],
                                                 scalar=cw[:, sl * 4 + 3: sl * 4 + 4],
                                                 in1=accA, op0=ALU.mult, op1=ALU.add),
                          f'{tag}v', 1)
                blk.vector(fvec)
                def fscal(s, bi=bi, sl=sl, kind=kind, nq=nq):
                    c.wait(s, f'{tag}v', B['v'] + 4 * (bi + 1))
                    accB = stage[:, sl * S:(sl + 1) * S]
                    if kind == 'v':
                        dst = obuf[:, sl * S:(sl + 1) * S]
                        c.inc(s.activation(out=dst, in_=accB, func=AF.Silu),
                              f'{tag}s', 1)
                    else:
                        dst = bar[:, sl * S:(sl + 1) * S]
                        c.inc(s.activation(out=dst, in_=accB, func=AF.Silu),
                              f'{tag}s', 1)
                        c.wait(s, f'{tag}s')
                        c.inc(s.activation(out=bar[:, 2 * S + sl * S: 2 * S + (sl + 1) * S],
                                           in_=dst, func=AF.Square), f'{tag}sq', 1)
                blk.scalar(fscal)
                if kind != 'v':
                    def ftens(t, bi=bi, sl=sl, nq=nq):
                        c.wait(t, f'{tag}sq', B['sq'] + nq + 1)
                        if nq >= 1:
                            c.wait(t, f'{tag}ssev', B['ssev'] + nq)
                        for gi in range(4):
                            ins = t.matmul(psA[0:1, gi * 512:(gi + 1) * 512], onesS[:],
                                           bar[:, 2 * S + sl * S + gi * 512: 2 * S + sl * S + (gi + 1) * 512],
                                           start=True, stop=True)
                        c.inc(ins, f'{tag}ssmm', 1)
                    blk.tensor(ftens)
                    def fvec2(v, bi=bi, sl=sl, nq=nq):
                        c.wait(v, f'{tag}ssmm', B['ssmm'] + nq + 1)
                        c.inc(v.tensor_copy(obuf[0:1, sl * S:(sl + 1) * S],
                                            psA[0:1, 0:2048]), f'{tag}ssev', 1)
                    blk.vector(fvec2)
                def fsync2(sy, bi=bi, sl=sl, kind=kind, idx=idx, nq=nq):
                    if kind == 'v':
                        c.wait(sy, f'{tag}s', c.n(f'{tag}s'))
                        c.inc(sy.dma_start(out=convv[idx * 128:(idx + 1) * 128, :],
                                           in_=obuf[:, sl * S:(sl + 1) * S]),
                              f'{tag}so{sl}', 16)
                    else:
                        c.wait(sy, f'{tag}s', c.n(f'{tag}s'))
                        dstqk = convq if kind == 'q' else convk
                        c.inc(sy.dma_start(out=dstqk[idx * 128:(idx + 1) * 128, :],
                                           in_=bar[:, sl * S:(sl + 1) * S]),
                              f'{tag}so{sl}', 16)
                        c.wait(sy, f'{tag}ssev', B['ssev'] + nq + 1)
                        dss = ssq_d if kind == 'q' else ssk_d
                        c.inc(sy.dma_start(out=dss[idx:idx + 1, :],
                                           in_=obuf[0:1, sl * S:(sl + 1) * S]),
                              f'{tag}sx{sl}', 16)
                blk.sync(fsync2)
                if kind != 'v':
                    nq += 1
            def fsync3(sy):
                c.wait(sy, f'{tag}so0')
                c.wait(sy, f'{tag}so1')
                c.wait(sy, f'{tag}sx0')
                c.wait(sy, f'{tag}sx1')
                c.inc(sy.nop(), f'{tag}done', 1)
            blk.sync(fsync3)

        def ph_rows(b, tag, convtag):
            B = {s: c.n(f'{tag}{s}') for s in ('ld', 's', 'vx', 'vb', 'vu', 'eb',
                                               'ebm', 'ebr', 'tp', 'tev', 'ls',
                                               's2', 'v2', 'vc', 'se', 'vn')}
            bs = stage[0:32, 0:S]
            as_ = stage[0:32, S:2 * S]
            lnb = obuf[0:32, 0:S]
            t1 = obuf[0:32, S:2 * S]
            gd = stage[0:64, 0:S]
            def fsync(sy):
                c.wait(sy, f'{convtag}done')
                c.wait(sy, f'{tag}ld', B['ld'])
                bsel = ba_d.rearrange("(h f) t -> h f t", f=4)
                c.inc(sy.dma_start(out=bs, in_=bsel[:, 0:2, :]),
                      f'{tag}ld', 16)
                c.inc(sy.dma_start(out=as_, in_=bsel[:, 2:4, :]),
                      f'{tag}ld', 16)
            blk.sync(fsync)
            def fscal(s):
                c.wait(s, f'{tag}ld', B['ld'] + 32)
                c.inc(s.activation(out=beta32[:], in_=bs, func=AF.Sigmoid), f'{tag}s', 1)
                c.wait(s, f'{tag}s')
                c.inc(s.activation(out=lnb, in_=beta32[:], func=AF.Ln), f'{tag}s', 1)
                c.wait(s, f'{tag}vx', B['vx'] + 1)
                xa = z1tok[0:32, 0:S]
                c.inc(s.activation(out=t1, in_=xa, func=AF.Abs), f'{tag}s', 1)
                c.wait(s, f'{tag}s')
                c.inc(s.activation(out=t1, in_=t1, func=AF.Exp, scale=-1.0), f'{tag}s', 1)
                c.wait(s, f'{tag}s')
                c.inc(s.activation(out=t1, in_=t1, func=AF.Ln, bias=smallv[0:32, 43:44]),
                      f'{tag}s', 1)
                c.wait(s, f'{tag}s')
                c.inc(s.activation(out=z1tok[0:32, S:2 * S], in_=xa, func=AF.Relu),
                      f'{tag}s', 1)
            blk.scalar(fscal)
            def fvec(v):
                c.wait(v, f'{tag}ld', B['ld'] + 32)
                c.inc(v.tensor_scalar(out=z1tok[0:32, 0:S], in0=as_,
                                      scalar1=smallv[0:32, 0:1], scalar2=None,
                                      op0=ALU.add), f'{tag}vx', 1)
                c.wait(v, f'{tag}s', B['s'] + 6)
                c.inc(v.tensor_tensor(out=z1tok[0:32, 0:S], in0=z1tok[0:32, S:2 * S],
                                      in1=t1, op=ALU.add), f'{tag}vx', 1)
                c.wait(v, f'{tag}vx')
                c.inc(v.tensor_scalar(out=z1tok[0:32, 0:S], in0=z1tok[0:32, 0:S],
                                      scalar1=smallv[0:32, 3:4], scalar2=None,
                                      op0=ALU.mult), f'{tag}vx', 1)
                c.wait(v, f'{tag}vx')
                c.wait(v, f'{tag}s', B['s'] + 2)
                c.inc(v.memset(stage[0:32, 0:S], 1.0), f'{tag}vx', 1)
                c.wait(v, f'{tag}vx')
                for ci in range(16):
                    c.inc(v.tensor_tensor_scan(out=obuf[0:32, S + ci * 128: S + (ci + 1) * 128],
                                               data0=stage[0:32, ci * 128:(ci + 1) * 128],
                                               data1=z1tok[0:32, ci * 128:(ci + 1) * 128],
                                               initial=0.0, op0=ALU.mult, op1=ALU.add),
                          f'{tag}vb', 1)
                c.wait(v, f'{tag}vb', B['vb'] + 16)
                c.inc(v.tensor_tensor(out=z1tok[0:32, S:2 * S], in0=obuf[0:32, S:2 * S],
                                      in1=lnb, op=ALU.add), f'{tag}vu', 1)
                c.inc(v.tensor_scalar(out=stage[0:32, 0:S], in0=obuf[0:32, S:2 * S],
                                      scalar1=-1.0, scalar2=None, op0=ALU.mult),
                      f'{tag}vu', 1)
            blk.vector(fvec)
            def fsync2(sy):
                c.wait(sy, f'{tag}vu', B['vu'] + 2)
                c.inc(sy.dma_start(out=rows_d[:, 0:S], in_=z1tok[0:32, S:2 * S]),
                      f'{tag}sr', 16)
                c.inc(sy.dma_start(out=rows_d[:, S:2 * S], in_=stage[0:32, 0:S]),
                      f'{tag}sr', 16)
                c.inc(sy.dma_start(out=rows_d[:, 2 * S:3 * S], in_=obuf[0:32, S:2 * S]),
                      f'{tag}sr', 16)
            blk.sync(fsync2)
            def ftens(t):
                c.wait(t, f'{tag}vu', B['vu'] + 2)
                c.wait(t, 'cst')
                for g in range(32):
                    if g >= 4:
                        c.wait(t, f'{tag}ebr', B['ebr'] + g - 3)
                    c.inc(t.matmul(psA[:, (g % 4) * 512:(g % 4) * 512 + 16],
                                   selS[0:32, g * 128:(g + 1) * 128],
                                   obuf[0:32, S:2 * S].rearrange("p (a o) -> p a o", o=128)[:, :, 127],
                                   start=True, stop=True), f'{tag}ebm', 1)
            blk.tensor(ftens)
            def fscal2(s):
                for g in range(32):
                    c.wait(s, f'{tag}ebm', B['ebm'] + g + 1)
                    c.inc(s.activation(out=cols[:, 1024 + g * 16: 1024 + (g + 1) * 16],
                                       in_=psA[:, (g % 4) * 512:(g % 4) * 512 + 16],
                                       func=AF.Exp), f'{tag}eb', 1)
                    c.wait(s, f'{tag}eb', B['eb'] + g + 1)
                    c.inc(s.copy(cols[:, g * 16: g * 16 + 16],
                                 psA[:, (g % 4) * 512:(g % 4) * 512 + 16]), f'{tag}ebr', 1)
            blk.scalar(fscal2)
            def ftens2(t):
                c.wait(t, f'{tag}vu', B['vu'] + 2)
                c.wait(t, f'{tag}eb', B['eb'] + 32)
                nt = 0
                for ci in range(16):
                    for srct in (z1tok[0:32, S + ci * 128: S + (ci + 1) * 128],
                                 stage[0:32, ci * 128:(ci + 1) * 128],
                                 beta32[:, ci * 128:(ci + 1) * 128],
                                 obuf[0:32, S + ci * 128: S + (ci + 1) * 128]):
                        if nt >= 4:
                            c.wait(t, f'{tag}tev', B['tev'] + nt - 3)
                        c.inc(t.transpose(psB[:, (nt % 4) * 512:(nt % 4) * 512 + 32],
                                          srct, identf[0:32, 0:32]), f'{tag}tp', 1)
                        nt += 1
            blk.tensor(ftens2)
            def fscal4(s):
                nt = 0
                for ci in range(16):
                    base = ci * 128
                    for q in range(4):
                        c.wait(s, f'{tag}tp', B['tp'] + nt + 1)
                        dst = z1tok[:, base + q * 32: base + q * 32 + 32]
                        c.inc(s.copy(dst, psB[:, (nt % 4) * 512:(nt % 4) * 512 + 32]),
                              f'{tag}tev', 1)
                        nt += 1
            blk.scalar(fscal4)
            def fsync3(sy):
                c.wait(sy, f'{tag}tev', B['tev'] + 64)
                c.wait(sy, f'{tag}sr')
                c.wait(sy, f'{tag}ls', B['ls'])
                c.inc(sy.dma_start(out=stage[0:16, 0:S], in_=ssk_d), f'{tag}ls', 16)
                c.inc(sy.dma_start(out=obuf[0:16, 0:S], in_=ssq_d), f'{tag}ls', 16)
            blk.sync(fsync3)
            def fscal3(s):
                c.wait(s, f'{tag}ls', B['ls'] + 32)
                c.inc(s.activation(out=stage[0:16, S:2 * S], in_=stage[0:16, 0:S],
                                   func=AF.Ln, bias=smallv[0:16, 41:42]), f'{tag}s2', 1)
                c.inc(s.activation(out=obuf[0:16, S:2 * S], in_=obuf[0:16, 0:S],
                                   func=AF.Ln, bias=smallv[0:16, 41:42]), f'{tag}s2', 1)
            blk.scalar(fscal3)
            def fvec2(v):
                c.wait(v, f'{tag}s2', B['s2'] + 2)
                c.inc(v.tensor_scalar(out=stage[0:16, S:2 * S], in0=stage[0:16, S:2 * S],
                                      scalar1=-0.5, scalar2=None, op0=ALU.mult),
                      f'{tag}v2', 1)
                c.inc(v.tensor_scalar(out=obuf[0:16, S:2 * S], in0=obuf[0:16, S:2 * S],
                                      scalar1=-0.5, scalar2=None, op0=ALU.mult),
                      f'{tag}v2', 1)
            blk.vector(fvec2)
            def fsync4(sy):
                c.wait(sy, f'{tag}v2', B['v2'] + 2)
                c.inc(sy.dma_start(out=rowsK_d[:, 0:S], in_=stage[0:16, S:2 * S]), f'{tag}sk', 16)
                c.inc(sy.dma_start(out=rowsK_d[:, S:2 * S], in_=stage[0:16, S:2 * S]), f'{tag}sk', 16)
                c.inc(sy.dma_start(out=rowsK_d[:, 2 * S:3 * S], in_=obuf[0:16, S:2 * S]),
                      f'{tag}sk', 16)
            blk.sync(fsync4)
            def ftens3(t):
                c.wait(t, f'{tag}v2', B['v2'] + 2)
                nt = 64
                for ci in range(16):
                    for srct in (stage[0:16, S + ci * 128: S + (ci + 1) * 128],
                                 obuf[0:16, S + ci * 128: S + (ci + 1) * 128]):
                        if True:
                            c.wait(t, f'{tag}tev', B['tev'] + nt - 3)
                        c.inc(t.transpose(psB[:, (nt % 4) * 512:(nt % 4) * 512 + 16],
                                          srct, identf[0:16, 0:16]), f'{tag}tp', 1)
                        nt += 1
            blk.tensor(ftens3)
            def fscal4b(s):
                nt = 64
                for ci in range(16):
                    for q in range(2):
                        c.wait(s, f'{tag}tp', B['tp'] + nt + 1)
                        dst = (cols[:, 512 + ci * 16: 512 + ci * 16 + 16] if q == 0 else
                               cols[:, 768 + ci * 16: 768 + ci * 16 + 16])
                        c.inc(s.copy(dst, psB[:, (nt % 4) * 512:(nt % 4) * 512 + 16]),
                              f'{tag}tev', 1)
                        nt += 1
            blk.scalar(fscal4b)
            def fvec3(v):
                for ci in range(16):
                    c.wait(v, f'{tag}tev', B['tev'] + 64 + 2 * (ci + 1))
                    base = ci * 128
                    lnrnT = cols[:, 512 + ci * 16: 512 + ci * 16 + 16]
                    lnqT = cols[:, 768 + ci * 16: 768 + ci * 16 + 16]
                    dup = lnrnT.rearrange("p (a o) -> p a o", o=1).broadcast_to([128, 16, 2])
                    dupq = lnqT.rearrange("p (a o) -> p a o", o=1).broadcast_to([128, 16, 2])
                    c.inc(v.tensor_tensor(out=z1tok[:, base:base + 32].rearrange("p (a o) -> p a o", o=2),
                                          in0=z1tok[:, base:base + 32].rearrange("p (a o) -> p a o", o=2),
                                          in1=dup, op=ALU.add), f'{tag}vc', 1)
                    c.wait(v, f'{tag}vc')
                    c.inc(v.tensor_tensor(out=z1tok[:, base + 32:base + 64].rearrange("p (a o) -> p a o", o=2),
                                          in0=z1tok[:, base + 32:base + 64].rearrange("p (a o) -> p a o", o=2),
                                          in1=dup, op=ALU.add), f'{tag}vc', 1)
                    c.wait(v, f'{tag}vc')
                    c.inc(v.scalar_tensor_tensor(out=z1tok[:, 2048 + ci * 64 + 32: 2048 + ci * 64 + 64].rearrange("p (a o) -> p a o", o=2),
                                                 in0=z1tok[:, base + 96:base + 128].rearrange("p (a o) -> p a o", o=2),
                                                 scalar=-0.5 * float(np.log(128.0)),
                                                 in1=dupq, op0=ALU.add, op1=ALU.add),
                          f'{tag}vc', 1)
            blk.vector(fvec3)
            def fscal5(s):
                for ci in range(16):
                    c.wait(s, f'{tag}vc', B['vc'] + 3 * (ci + 1))
                    c.inc(s.activation(out=z1tok[:, 2048 + ci * 64: 2048 + ci * 64 + 32],
                                       in_=z1tok[:, ci * 128: ci * 128 + 32],
                                       func=AF.Exp), f'{tag}se', 1)
                    c.wait(s, f'{tag}se')
                    c.inc(s.activation(out=z1tok[:, 2048 + ci * 64 + 32: 2048 + ci * 64 + 64],
                                       in_=z1tok[:, 2048 + ci * 64 + 32: 2048 + ci * 64 + 64],
                                       func=AF.Exp), f'{tag}se', 1)
            blk.scalar(fscal5)
            def fvec4(v):
                for ci in range(16):
                    c.wait(v, f'{tag}se', B['se'] + 2 * (ci + 1))
                    c.inc(v.tensor_scalar(out=z1tok[:, 2048 + ci * 64: 2048 + ci * 64 + 32],
                                          in0=z1tok[:, 2048 + ci * 64: 2048 + ci * 64 + 32],
                                          scalar1=-1.0, scalar2=None, op0=ALU.mult),
                          f'{tag}vn', 1)
                v2 = None
            blk.vector(fvec4)
            def fsync5(sy):
                c.wait(sy, f'{tag}vn', B['vn'] + 16)
                c.wait(sy, f'{tag}ebr', B['ebr'] + 32)
                c.wait(sy, f'{tag}sr')
                c.wait(sy, f'{tag}sk')
                c.inc(sy.nop(), f'{tag}done', 1)
            blk.sync(fsync5)


        def ph_delta(b, tag, rowstag):
            NIDX = DEV_DELTA_LIMIT
            B = {s: c.n(f'{tag}{s}') for s in ('dl0', 'dl1', 'dg0', 'dg1', 'sz',
                                               'sd0', 'sd1')}
            guards = {}
            def gwait(eng, key):
                g_ = guards.get(key)
                if g_:
                    c.wait(eng, g_[0], g_[1])
            def gset(key, sem):
                guards[key] = (sem, c.n(sem))
            for idx in range(NIDX):
                ch, h = divmod(idx, 16)
                cc = ch
                sl = idx % 2
                g0, g1 = 2 * h, 2 * h + 1
                kqo = 8192 + sl * 256
                kto = 8704 + sl * 128
                auo = 13312
                wto = 14080 + sl * 128
                yto = 14336 + sl * 128
                def col_u(g):    return z1tok[:, cc * 128 + g: cc * 128 + g + 1]
                def col_w(g):    return z1tok[:, cc * 128 + 32 + g: cc * 128 + 33 + g]
                def col_b(g):    return z1tok[:, cc * 128 + 64 + g: cc * 128 + 65 + g]
                def col_neu(g):  return z1tok[:, 2048 + cc * 64 + g: 2048 + cc * 64 + g + 1]
                def col_euq(g):  return z1tok[:, 2048 + cc * 64 + 32 + g: 2048 + cc * 64 + 33 + g]
                def col_ebc(g):  return cols[:, 1024 + g * 16 + cc: 1024 + g * 16 + cc + 1]

                def fsync(sy, idx=idx, ch=ch, h=h, sl=sl, kqo=kqo):
                    if idx == 0:
                        c.wait(sy, f'{rowstag}done')
                    if idx < 2:
                        c.wait(sy, f'{tag}dl{sl}', B[f'dl{sl}'])
                    gwait(sy, ('kq', sl))
                    c.inc(sy.dma_start(out=actT[:, kqo:kqo + 128],
                                       in_=convk[h * 128:(h + 1) * 128, ch * 128:(ch + 1) * 128]),
                          f'{tag}dl{sl}', 16)
                    c.inc(sy.dma_start(out=actT[:, kqo + 128:kqo + 256],
                                       in_=convq[h * 128:(h + 1) * 128, ch * 128:(ch + 1) * 128]),
                          f'{tag}dl{sl}', 16)
                    gwait(sy, ('zs', sl))
                    gwait(sy, ('vt', sl))
                    for j in range(2):
                        c.inc(sy.dma_start(out=stage[:, 768 + (2 * sl + j) * 128: 768 + (2 * sl + j + 1) * 128],
                                           in_=convv[(2 * h + j) * 128:(2 * h + j + 1) * 128,
                                                     ch * 128:(ch + 1) * 128]),
                              f'{tag}dl{sl}', 16)
                        c.inc(sy.dma_start(out=stage[:, 1280 + (2 * sl + j) * 128: 1280 + (2 * sl + j + 1) * 128],
                                           in_=qkvzT[(6 * h + 4 + j) * 128:(6 * h + 5 + j) * 128,
                                                     ch * 128:(ch + 1) * 128]),
                              f'{tag}dl{sl}', 16)
                blk.sync(fsync)
                def fgp(g_, idx=idx, ch=ch, sl=sl):
                    if idx == 0:
                        c.wait(g_, f'{rowstag}done')
                        c.inc(g_.memset(actT[:, 9216:13312], 0.0), f'{tag}sz', 1)
                    if idx < 2:
                        c.wait(g_, f'{tag}dg{sl}', B[f'dg{sl}'])
                    gwait(g_, ('land', sl))
                    c.inc(g_.dma_start(out=stage[0:32, sl * 384:(sl + 1) * 384],
                                       in_=rows_d.rearrange("g (q t) -> g q t", t=S)[:, :, ch * 128:(ch + 1) * 128]),
                          f'{tag}dg{sl}', 16)
                    c.inc(g_.dma_start(out=stage[32:48, sl * 384:(sl + 1) * 384],
                                       in_=rowsK_d.rearrange("g (q t) -> g q t", t=S)[:, :, ch * 128:(ch + 1) * 128]),
                          f'{tag}dg{sl}', 16)
                blk.gpsimd(fgp)

                def ftA(t, idx=idx, ch=ch, h=h, sl=sl, kqo=kqo, kto=kto):
                    if idx == 0:
                        c.wait(t, f'{rowstag}done')
                    c.wait(t, f'{tag}dl{sl}', B[f'dl{sl}'] + 16 * 6 * (idx // 2 + 1))
                    c.wait(t, f'{tag}dg{sl}', B[f'dg{sl}'] + 16 * 2 * (idx // 2 + 1))
                    KT = actT[:, kqo:kqo + 128]
                    QT = actT[:, kqo + 128:kqo + 256]
                    gwait(t, ('bank0',))
                    gwait(t, ('bank0b',))
                    c.inc(t.transpose(psA[:, 256:384], stage[:, 1280 + 2 * sl * 128: 1280 + (2 * sl + 1) * 128], identf[:]), f'{tag}pga', 1)
                    c.inc(t.transpose(psA[:, 384:512], stage[:, 1280 + (2 * sl + 1) * 128: 1280 + (2 * sl + 2) * 128], identf[:]), f'{tag}pga', 1)
                    c.inc(t.matmul(psA[:, 0:128], KT, KT, start=True, stop=True), f'{tag}pga', 1)
                    c.inc(t.matmul(psA[:, 128:256], KT, QT, start=True, stop=True), f'{tag}pga', 1)
                    gwait(t, ('bank1',))
                    gwait(t, ('bank1b',))
                    c.inc(t.matmul(psA[:, 512:640], KT, identb[:], start=True, stop=True), f'{tag}pga', 1)
                    c.inc(t.transpose(psA[:, 640:768], stage[:, 768 + 2 * sl * 128: 768 + (2 * sl + 1) * 128], identf[:]), f'{tag}pga', 1)
                    c.inc(t.transpose(psA[:, 768:896], stage[:, 768 + (2 * sl + 1) * 128: 768 + (2 * sl + 2) * 128], identf[:]), f'{tag}pga', 1)
                    gwait(t, ('bank2',))
                    c.inc(t.matmul(psA[:, 1024:1408], selS[:, 2 * h * 128:(2 * h + 1) * 128],
                                   stage[0:48, sl * 384:(sl + 1) * 384], start=True, stop=True), f'{tag}pga', 1)
                    gwait(t, ('bank3',))
                    c.inc(t.matmul(psA[:, 1536:1920], selS[:, (2 * h + 1) * 128:(2 * h + 2) * 128],
                                   stage[0:48, sl * 384:(sl + 1) * 384], start=True, stop=True), f'{tag}pga', 1)
                blk.tensor(ftA)
                pga_n = c.n(f'{tag}pga')
                gset(('land', sl), f'{tag}pga')

                def fsA(s, idx=idx, sl=sl, kto=kto, pga_n=pga_n):
                    c.wait(s, f'{tag}pga', pga_n - 4)
                    gwait(s, ('ktok', sl))
                    c.inc(s.copy(actT[:, kto:kto + 128], psA[:, 512:640]), f'{tag}evS', 1)
                    c.wait(s, f'{tag}pga', pga_n - 7)
                    gwait(s, ('zs', sl))
                    c.inc(s.activation(out=stage[:, 1280 + 2 * sl * 128: 1280 + (2 * sl + 1) * 128],
                                       in_=psA[:, 256:384], func=AF.Silu), f'{tag}evS', 1)
                    c.inc(s.activation(out=stage[:, 1280 + (2 * sl + 1) * 128: 1280 + (2 * sl + 2) * 128],
                                       in_=psA[:, 384:512], func=AF.Silu), f'{tag}evS', 1)
                blk.scalar(fsA)
                evS_A = c.n(f'{tag}evS')
                def fvA(v, idx=idx, sl=sl, pga_n=pga_n):
                    c.wait(v, f'{tag}pga', pga_n - 2)
                    gwait(v, ('vt', sl))
                    c.inc(v.tensor_copy(stage[:, 768 + 2 * sl * 128: 768 + (2 * sl + 1) * 128],
                                        psA[:, 640:768]), f'{tag}evV', 1)
                    c.inc(v.tensor_copy(stage[:, 768 + (2 * sl + 1) * 128: 768 + (2 * sl + 2) * 128],
                                        psA[:, 768:896]), f'{tag}evV', 1)
                blk.vector(fvA)
                evV_A = c.n(f'{tag}evV')

                def ftB(t, idx=idx, sl=sl, kqo=kqo, g0=g0, g1=g1):
                    KT = actT[:, kqo:kqo + 128]
                    QT = actT[:, kqo + 128:kqo + 256]
                    gwait(t, ('bank4',))
                    gwait(t, ('bank4b',))
                    if idx < 16:
                        c.wait(t, f'{tag}sz', B['sz'] + 1)
                    gwait(t, ('s', g0))
                    gwait(t, ('s', g1))
                    S0 = actT[:, 9216 + g0 * 128: 9216 + (g0 + 1) * 128]
                    S1 = actT[:, 9216 + g1 * 128: 9216 + (g1 + 1) * 128]
                    c.inc(t.matmul(psB[:, 0:128], KT, S0, start=True, stop=True), f'{tag}pgb', 1)
                    c.inc(t.matmul(psB[:, 128:256], KT, S1, start=True, stop=True), f'{tag}pgb', 1)
                    c.inc(t.matmul(psB[:, 256:384], QT, S0, start=True, stop=True), f'{tag}pgb', 1)
                    c.inc(t.matmul(psB[:, 384:512], QT, S1, start=True, stop=True), f'{tag}pgb', 1)
                blk.tensor(ftB)
                pgb_n = c.n(f'{tag}pgb')
                gset(('kq', sl), f'{tag}pgb')

                for gi, g in enumerate((g0, g1)):
                    selps = psA[:, 1024:1408] if gi == 0 else psA[:, 1536:1920]
                    arg = stage[:, 1792:1920]
                    expb = stage[:, 1920:2048]
                    au_s = (idx * 2 + gi) % 2
                    AUB = lambda s_: actT[:, auo + s_ * 384: auo + (s_ + 1) * 384]
                    def fvM(v, idx=idx, ch=ch, g=g, gi=gi, sl=sl, selps=selps, arg=arg,
                            expb=expb, au_s=au_s, pga_n=pga_n, pgb_n=pgb_n, evS_A=evS_A):
                        c.wait(v, f'{tag}pga', pga_n)
                        gwait(v, ('argbuf',))
                        c.inc(v.scalar_tensor_tensor(out=arg, in0=selps[:, 128:256],
                                                     scalar=col_u(g), in1=maskA[:],
                                                     op0=ALU.add, op1=ALU.add), f'{tag}vm', 1)
                        c.wait(v, f'{tag}sm', c.n(f'{tag}sm') + 1)
                        gwait(v, ('aub', au_s))
                        c.inc(v.scalar_tensor_tensor(out=AUB(au_s)[:, 0:128], in0=expb,
                                                     scalar=-1.0, in1=psA[:, 0:128],
                                                     op0=ALU.mult, op1=ALU.mult), f'{tag}vm', 1)
                        c.wait(v, f'{tag}vm')
                        c.inc(v.scalar_tensor_tensor(out=arg, in0=selps[:, 0:128],
                                                     scalar=col_w(g), in1=maskB[:],
                                                     op0=ALU.add, op1=ALU.add), f'{tag}vm', 1)
                        c.wait(v, f'{tag}sm', c.n(f'{tag}sm') + 2)
                        c.inc(v.scalar_tensor_tensor(out=AUB(au_s)[:, 256:384], in0=expb,
                                                     scalar=-1.0, in1=psA[:, 0:128],
                                                     op0=ALU.mult, op1=ALU.mult), f'{tag}vm', 1)
                        c.wait(v, f'{tag}vm')
                        c.inc(v.scalar_tensor_tensor(out=arg, in0=selps[:, 256:384],
                                                     scalar=col_w(g), in1=maskW[:],
                                                     op0=ALU.add, op1=ALU.add), f'{tag}vm', 1)
                        c.wait(v, f'{tag}sm', c.n(f'{tag}sm') + 3)
                        gwait(v, ('wt', sl, gi))
                        c.inc(v.tensor_tensor(out=actT[:, 14080 + sl * 128: 14208 + sl * 128],
                                              in0=expb, in1=psA[:, 128:256], op=ALU.mult),
                              f'{tag}vm', 1)
                        c.wait(v, f'{tag}evV', evV_A)
                        gwait(v, ('xg', 2 * sl + gi))
                        xgb = obuf[:, 512 + (2 * sl + gi) * 128: 512 + (2 * sl + gi + 1) * 128]
                        c.inc(v.tensor_scalar(out=xgb,
                                              in0=stage[:, 768 + (2 * sl + gi) * 128: 768 + (2 * sl + gi + 1) * 128],
                                              scalar1=col_b(g), scalar2=None, op0=ALU.mult),
                              f'{tag}vm', 1)
                        c.wait(v, f'{tag}vm')
                        c.wait(v, f'{tag}pgb', pgb_n)
                        c.inc(v.scalar_tensor_tensor(out=AUB(au_s)[:, 128:256],
                                                     in0=psB[:, gi * 128:(gi + 1) * 128],
                                                     scalar=col_neu(g), in1=xgb,
                                                     op0=ALU.mult, op1=ALU.add), f'{tag}vm', 1)
                        c.wait(v, f'{tag}vm')
                        c.inc(v.tensor_tensor(out=cols[:, 1552 + 2 * sl + gi: 1553 + 2 * sl + gi],
                                              in0=col_w(g), in1=cols[:, g * 16 + cc: g * 16 + cc + 1],
                                              op=ALU.add), f'{tag}vm', 1)
                    blk.vector(fvM)
                    if DEV_DBG and idx in (0, 16) and gi == 0:
                        def fdbg1(sy, au_s=au_s, sl=sl, idx=idx, ch=ch):
                            o = 0 if idx == 0 else 2048
                            c.wait(sy, f'{tag}vm', c.n(f'{tag}vm'))
                            c.wait(sy, f'{tag}sm', c.n(f'{tag}sm') + 4)
                            c.wait(sy, f'{tag}dg{sl}', B[f'dg{sl}'] + 16 * 2 * (idx // 2 + 1))
                            c.inc(sy.dma_start(out=dbgb[:, o:o + 384], in_=AUB(au_s)[:, 0:384]), f'{tag}dbg', 16)
                            c.inc(sy.dma_start(out=dbgb[:, o + 384:o + 512], in_=actT[:, 14080 + sl * 128: 14208 + sl * 128]), f'{tag}dbg', 16)
                            c.inc(sy.dma_start(out=dbgf[0:48, o:o + 384], in_=stage[0:48, sl * 384:(sl + 1) * 384]), f'{tag}dbg', 16)
                            c.inc(sy.dma_start(out=dbgf[:, o + 384:o + 512], in_=stage[:, 768 + 2 * sl * 128: 768 + (2 * sl + 1) * 128]), f'{tag}dbg', 16)
                            c.inc(sy.dma_start(out=dbgf[:, o + 512:o + 640], in_=stage[:, 1280 + 2 * sl * 128: 1280 + (2 * sl + 1) * 128]), f'{tag}dbg', 16)
                            c.inc(sy.dma_start(out=dbgf[:, o + 640:o + 768], in_=z1tok[:, ch * 128: ch * 128 + 128]), f'{tag}dbg', 16)
                            c.inc(sy.dma_start(out=dbgf[:, o + 768:o + 832], in_=z1tok[:, 2048 + ch * 64: 2048 + ch * 64 + 64]), f'{tag}dbg', 16)
                            c.inc(sy.dma_start(out=dbgb[:, o + 512:o + 768], in_=actT[:, 8192 + sl * 256: 8192 + sl * 256 + 256]), f'{tag}dbg', 16)
                            c.inc(sy.dma_start(out=dbgf[:, o + 832:o + 833], in_=cols[:, 1560 + 2 * sl: 1561 + 2 * sl], allow_slow_non_contiguous=True), f'{tag}dbg', 16)
                            c.inc(sy.dma_start(out=dbgf[:, o + 833:o + 834], in_=cols[:, 0 + ch: 1 + ch], allow_slow_non_contiguous=True), f'{tag}dbg', 16)
                            c.inc(sy.dma_start(out=dbgb[:, o + 896:o + 1024], in_=actT[:, 9216:9344]), f'{tag}dbg', 16)
                            c.inc(sy.dma_start(out=dbgf[:, o + 1280:o + 1792], in_=cols[:, 0:512]), f'{tag}dbg', 16)
                            c.inc(sy.dma_start(out=dbgf[0:48, o + 1792:o + 1920], in_=selS[0:48, 2048:2176]), f'{tag}dbg', 16)
                            c.inc(sy.dma_start(out=dbgf[0:48, o + 1920:o + 2048], in_=selS[0:48, 0:128]), f'{tag}dbg', 16)
                        blk.sync(fdbg1)
                        for key in (('land', sl), ('vt', sl), ('zs', sl), ('kq', sl),
                                    ('aub', au_s), ('wt', sl, gi)):
                            gset(key, f'{tag}dbg')
                    def fsM2(s, idx=idx, gi=gi, arg=arg, expb=expb):
                        base = c.n(f'{tag}sm')
                        vm_now = c.n(f'{tag}vm')
                        c.wait(s, f'{tag}vm', vm_now - 8)
                        c.inc(s.activation(out=expb, in_=arg, func=AF.Exp), f'{tag}sm', 1)
                        c.wait(s, f'{tag}vm', vm_now - 6)
                        c.inc(s.activation(out=expb, in_=arg, func=AF.Exp), f'{tag}sm', 1)
                        c.wait(s, f'{tag}vm', vm_now - 4)
                        c.inc(s.activation(out=expb, in_=arg, func=AF.Exp), f'{tag}sm', 1)
                        c.wait(s, f'{tag}vm', vm_now)
                        gwait(s, ('scc', 2 * sl + gi))
                        c.inc(s.activation(out=cols[:, 1560 + 2 * sl + gi: 1561 + 2 * sl + gi],
                                           in_=cols[:, 1552 + 2 * sl + gi: 1553 + 2 * sl + gi],
                                           func=AF.Exp), f'{tag}sm', 1)
                    blk.scalar(fsM2)
                    gset(('argbuf',), f'{tag}sm')
                    def fsQ(s, idx=idx, g=g, gi=gi, sl=sl, pgb_n=pgb_n):
                        c.wait(s, f'{tag}pgb', pgb_n)
                        gwait(s, ('qs0', 2 * sl + gi))
                        c.inc(s.activation(out=obuf[:, (2 * sl + gi) * 128:(2 * sl + gi + 1) * 128],
                                           in_=psB[:, 256 + gi * 128: 256 + (gi + 1) * 128],
                                           func=AF.Copy, scale=col_euq(g)), f'{tag}sq', 1)
                    blk.scalar(fsQ)
                    vm_done = c.n(f'{tag}vm')
                    for lev in range(7):
                        cur = (idx * 2 + gi + lev) % 2
                        nxt = 1 - cur
                        def ftS(t, lev=lev, cur=cur, nxt=nxt, au_s=au_s, idx=idx, gi=gi,
                                vm_done=vm_done):
                            A_ = AUB(cur)[:, 0:128]
                            U_ = AUB(cur)[:, 128:256]
                            B_ = AUB(cur)[:, 256:384]
                            if lev == 0:
                                c.wait(t, f'{tag}vm', vm_done)
                            gwait(t, ('bank6',))
                            if lev < 6:
                                c.inc(t.matmul(psB[:, 1024:1280], B_, AUB(cur)[:, 0:256],
                                               start=True, stop=False), f'{tag}ps', 1)
                                c.inc(t.matmul(psB[:, 1152:1280], identb[:], U_,
                                               start=False, stop=True), f'{tag}ps', 1)
                                c.inc(t.matmul(psB[:, 1280:1408], A_, B_,
                                               start=True, stop=True), f'{tag}ps', 1)
                            else:
                                c.inc(t.matmul(psB[:, 1024:1152], B_, U_,
                                               start=True, stop=False), f'{tag}ps', 1)
                                c.inc(t.matmul(psB[:, 1024:1152],
                                               identb[:], U_, start=False, stop=True), f'{tag}ps', 1)
                        blk.tensor(ftS)
                        def fevS(eng, lev=lev, cur=cur, nxt=nxt, idx=idx, gi=gi):
                            c.wait(eng, f'{tag}ps', c.n(f'{tag}ps'))
                            if DEV_DBG and c.n(f'{tag}dbg'):
                                c.wait(eng, f'{tag}dbg', c.n(f'{tag}dbg'))
                            gwait(eng, ('aub', nxt))
                            if lev < 6:
                                if isinstance(eng, type(nc.vector)):
                                    c.inc(eng.tensor_copy(AUB(nxt)[:, 0:384], psB[:, 1024:1408]),
                                          f'{tag}evV', 1)
                                else:
                                    c.inc(eng.copy(AUB(nxt)[:, 0:384], psB[:, 1024:1408]),
                                          f'{tag}evV' if False else f'{tag}evS', 1)
                            else:
                                c.inc(eng.tensor_copy(AUB(nxt)[:, 128:256], psB[:, 1024:1152]),
                                      f'{tag}evV', 1)
                                gwait(eng, ('ubar', gi))
                                c.inc(eng.tensor_scalar(
                                    out=actT[:, 14592 + gi * 128: 14720 + gi * 128],
                                    in0=psB[:, 1024:1152],
                                    scalar1=cols[:, 1560 + 2 * sl + gi: 1561 + 2 * sl + gi],
                                    scalar2=None, op0=ALU.mult), f'{tag}evV', 1)
                        if lev % 2 == 0:
                            blk.vector(fevS)
                            gset(('bank6',), f'{tag}evV')
                        else:
                            blk.scalar(fevS)
                            gset(('bank6',), f'{tag}evS')
                        gset(('aub', cur), f'{tag}ps')
                    ufin = (idx * 2 + gi + 7) % 2
                    def ftO(t, idx=idx, gi=gi, g=g, sl=sl, ufin=ufin, kqo=kqo, kto=kto):
                        U7 = AUB(ufin)[:, 128:256]
                        WT = actT[:, 14080 + sl * 128: 14208 + sl * 128]
                        gwait(t, ('bank7a', gi))
                        c.wait(t, f'{tag}evV', c.n(f'{tag}evV'))
                        c.wait(t, f'{tag}evS', c.n(f'{tag}evS'))
                        c.inc(t.matmul(psB[:, 1536 + gi * 128: 1664 + gi * 128], WT, U7,
                                       start=True, stop=True), f'{tag}po', 1)
                    blk.tensor(ftO)
                    po_n = c.n(f'{tag}po')
                    gset(('wt', sl, gi), f'{tag}po')
                    def fvO(v, idx=idx, gi=gi, g=g, sl=sl, po_n=po_n):
                        c.wait(v, f'{tag}po', po_n)
                        c.wait(v, f'{tag}sq', c.n(f'{tag}sq'))
                        xgb = obuf[:, 512 + (2 * sl + gi) * 128: 512 + (2 * sl + gi + 1) * 128]
                        qsb = obuf[:, (2 * sl + gi) * 128:(2 * sl + gi + 1) * 128]
                        c.inc(v.tensor_tensor(out=xgb, in0=psB[:, 1536 + gi * 128: 1664 + gi * 128],
                                              in1=qsb, op=ALU.add), f'{tag}vo', 1)
                        c.wait(v, f'{tag}vo')
                        c.inc(v.tensor_tensor(out=xgb, in0=xgb,
                                              in1=stage[:, 1280 + (2 * sl + gi) * 128: 1280 + (2 * sl + gi + 1) * 128],
                                              op=ALU.mult), f'{tag}vo', 1)
                    blk.vector(fvO)
                    gset(('qs0', 2 * sl + gi), f'{tag}vo')
                    def fsO(s, idx=idx, gi=gi, sl=sl):
                        c.wait(s, f'{tag}vo', c.n(f'{tag}vo'))
                        xgb = obuf[:, 512 + (2 * sl + gi) * 128: 512 + (2 * sl + gi + 1) * 128]
                        c.inc(s.activation(out=bar[:, 4 * S + (2 * sl + gi) * 128: 4 * S + (2 * sl + gi + 1) * 128],
                                           in_=xgb, func=AF.Square,
                                           accum_out=cols[:, 1536 + 2 * sl + gi: 1537 + 2 * sl + gi]),
                              f'{tag}so', 1)
                        c.wait(s, f'{tag}so')
                        c.inc(s.activation(out=cols[:, 1540 + 2 * sl + gi: 1541 + 2 * sl + gi],
                                           in_=cols[:, 1536 + 2 * sl + gi: 1537 + 2 * sl + gi],
                                           func=AF.Sqrt, bias=smallv[:, 42:43]), f'{tag}so', 1)
                    blk.scalar(fsO)
                    def fvY(v, idx=idx, gi=gi, sl=sl, g=g):
                        c.wait(v, f'{tag}so', c.n(f'{tag}so'))
                        c.inc(v.reciprocal(cols[:, 1544 + 2 * sl + gi: 1545 + 2 * sl + gi],
                                           cols[:, 1540 + 2 * sl + gi: 1541 + 2 * sl + gi]),
                              f'{tag}vy', 1)
                        c.wait(v, f'{tag}vy')
                        xgb = obuf[:, 512 + (2 * sl + gi) * 128: 512 + (2 * sl + gi + 1) * 128]
                        gwait(v, ('ybuf', gi))
                        c.inc(v.tensor_scalar(out=bar[:, 5 * S + gi * 128: 5 * S + (gi + 1) * 128],
                                              in0=xgb, scalar1=cols[:, 1544 + 2 * sl + gi: 1545 + 2 * sl + gi],
                                              scalar2=None, op0=ALU.mult), f'{tag}vy', 1)
                    blk.vector(fvY)
                    vy_n = c.n(f'{tag}vy')
                    gset(('xg', 2 * sl + gi), f'{tag}vy')
                    def ftY(t, idx=idx, gi=gi, sl=sl, g=g, vy_n=vy_n, ufin=ufin, kto=kto):
                        c.wait(t, f'{tag}vy', vy_n)
                        gwait(t, ('bank5', gi))
                        c.inc(t.matmul(psB[:, 512 + gi * 128: 640 + gi * 128],
                                       bar[:, 5 * S + gi * 128: 5 * S + (gi + 1) * 128],
                                       identb[:], start=True, stop=True), f'{tag}py', 1)
                        gwait(t, ('bank7b', gi))
                        c.inc(t.matmul(psB[:, 1792 + gi * 128: 1920 + gi * 128],
                                       actT[:, kto:kto + 128],
                                       actT[:, 14592 + gi * 128: 14720 + gi * 128],
                                       start=True, stop=True), f'{tag}py', 1)
                    blk.tensor(ftY)
                    py_n = c.n(f'{tag}py')
                    gset(('ktok', sl), f'{tag}py')
                    gset(('aub', ufin), f'{tag}py')
                    gset(('ubar', gi), f'{tag}py')
                    gset(('scc', 2 * sl + gi), f'{tag}evV')
                    def fsY(s, idx=idx, gi=gi, sl=sl, g=g, ch=ch, py_n=py_n):
                        c.wait(s, f'{tag}py', py_n - 1)
                        gwait(s, ('yslot', gi))
                        c.inc(s.copy(actT[:, 14336 + gi * 128: 14464 + gi * 128],
                                     psB[:, 512 + gi * 128: 640 + gi * 128]), f'{tag}sy', 1)
                    blk.scalar(fsY)
                    gset(('ybuf', gi), f'{tag}sy')
                    def fvS(v, idx=idx, gi=gi, g=g, py_n=py_n):
                        c.wait(v, f'{tag}py', py_n)
                        c.wait(v, f'{tag}vs', c.n(f'{tag}vs'))
                        S_g = actT[:, 9216 + g * 128: 9216 + (g + 1) * 128]
                        c.inc(v.tensor_scalar(out=obuf[:, 1024:1152], in0=S_g,
                                              scalar1=col_ebc(g), scalar2=None,
                                              op0=ALU.mult), f'{tag}vs', 1)
                        c.wait(v, f'{tag}vs')
                        c.inc(v.tensor_tensor(out=S_g,
                                              in0=psB[:, 1792 + gi * 128: 1920 + gi * 128],
                                              in1=obuf[:, 1024:1152], op=ALU.add), f'{tag}vs', 1)
                    blk.vector(fvS)
                    if DEV_DBG and idx in (0, 16) and gi == 0:
                        def fdbg2(sy, ufin=ufin, gi=gi, sl=sl, g=g, idx=idx):
                            o = 0 if idx == 0 else 2048
                            c.wait(sy, f'{tag}vs', c.n(f'{tag}vs'))
                            c.wait(sy, f'{tag}vy', c.n(f'{tag}vy'))
                            c.inc(sy.dma_start(out=dbgb[:, o + 768:o + 896], in_=AUB(ufin)[:, 128:256]), f'{tag}dbg', 16)
                            c.inc(sy.dma_start(out=dbgf[:, o + 896:o + 1024], in_=obuf[:, 512 + (2 * sl + gi) * 128: 512 + (2 * sl + gi + 1) * 128]), f'{tag}dbg', 16)
                            c.inc(sy.dma_start(out=dbgf[:, o + 1024:o + 1152], in_=obuf[:, (2 * sl + gi) * 128:(2 * sl + gi + 1) * 128]), f'{tag}dbg', 16)
                            c.inc(sy.dma_start(out=dbgb[:, o + 1024:o + 1152], in_=actT[:, 9216 + g * 128: 9216 + (g + 1) * 128]), f'{tag}dbg', 16)
                            c.inc(sy.dma_start(out=dbgb[:, o + 1152:o + 1280], in_=actT[:, 8704 + sl * 128: 8704 + sl * 128 + 128]), f'{tag}dbg', 16)
                            c.inc(sy.dma_start(out=dbgb[:, o + 1280:o + 1408], in_=actT[:, 14592 + gi * 128: 14720 + gi * 128]), f'{tag}dbg', 16)
                        blk.sync(fdbg2)
                        for key in (('aub', ufin), ('s', g), ('xg', 2 * sl + gi),
                                    ('qs0', 2 * sl + gi), ('ktok', sl)):
                            gset(key, f'{tag}dbg')
                    if not (DEV_DBG and idx == 0 and gi == 0):
                        gset(('s', g), f'{tag}vs')
                    gset(('bank7b', gi), f'{tag}vs')
                    gset(('bank5', gi), f'{tag}sy')
                    def fsyY(sy, idx=idx, gi=gi, g=g, ch=ch):
                        c.wait(sy, f'{tag}sd{gi}', c.n(f'{tag}sd{gi}'))
                        c.wait(sy, f'{tag}sy', c.n(f'{tag}sy'))
                        c.inc(sy.dma_start(out=yT_d[g * 128:(g + 1) * 128, ch * 128:(ch + 1) * 128],
                                           in_=actT[:, 14336 + gi * 128: 14464 + gi * 128]),
                              f'{tag}sd{gi}', 16)
                    blk.sync(fsyY)
                    gset(('yslot', gi), f'{tag}sd{gi}')
                    gset(('bank7a', gi), f'{tag}vs')
                gset(('bank0',), f'{tag}vm')
                gset(('bank0b',), f'{tag}evS')
                gset(('bank1',), f'{tag}evV')
                gset(('bank1b',), f'{tag}evS')
                gset(('bank2',), f'{tag}vm')
                gset(('bank3',), f'{tag}vm')
                gset(('bank4',), f'{tag}vm')
                gset(('bank4b',), f'{tag}sq')
                gset(('zs', sl), f'{tag}vo')
                gset(('vt', sl), f'{tag}vm')
            def fend(sy):
                c.wait(sy, f'{tag}sd0')
                c.wait(sy, f'{tag}sd1')
                if c.n(f'{tag}dbg'):
                    c.wait(sy, f'{tag}dbg')
                c.inc(sy.nop(), f'{tag}done', 1)
            blk.sync(fend)


        def ph_prepA(tag):
            for i in range(32):
                sl = i % 2
                def fsy(sy, i=i, sl=sl):
                    if i >= 2:
                        c.wait(sy, f'{tag}v', i - 1)
                    c.inc(sy.dma_start(out=stage[:, sl * S:(sl + 1) * S],
                                       in_=w_out[i * 128:(i + 1) * 128, :]), f'{tag}l{sl}', 16)
                blk.sync(fsy)
                def fv(v, i=i, sl=sl):
                    c.wait(v, f'{tag}l{sl}', 16 * (i // 2 + 1))
                    c.wait(v, 'cstV')
                    if i >= 2:
                        c.wait(v, f'{tag}s{sl}', 16 * (i // 2))
                    c.inc(v.tensor_scalar(out=bar[:, sl * S:(sl + 1) * S],
                                          in0=stage[:, sl * S:(sl + 1) * S],
                                          scalar1=smallv[:, 44:45], scalar2=None,
                                          op0=ALU.mult), f'{tag}v', 1)
                blk.vector(fv)
                def fsy2(sy, i=i, sl=sl):
                    c.wait(sy, f'{tag}v', i + 1)
                    c.inc(sy.dma_start(out=woute_d[i * 128:(i + 1) * 128, :],
                                       in_=bar[:, sl * S:(sl + 1) * S]), f'{tag}s{sl}', 16)
                blk.sync(fsy2)
            def fend(sy):
                c.wait(sy, f'{tag}s0')
                c.wait(sy, f'{tag}s1')
                c.inc(sy.nop(), f'{tag}done', 1)
            blk.sync(fend)


        def ph_prepB(tag):
            def f1t(t):
                c.wait(t, 'cst')
                c.wait(t, 'prAdone')
                for gi2 in range(4):
                    c.inc(t.matmul(psA[:, gi2 * 512:(gi2 + 1) * 512], onesrowS[:],
                                   lnrow1[0:1, gi2 * 512:(gi2 + 1) * 512],
                                   start=True, stop=True), f'{tag}bc', 1)
            blk.tensor(f1t)
            def f1v(v):
                c.wait(v, f'{tag}bc', c.n(f'{tag}bc'))
                for gi2 in range(4):
                    c.inc(v.tensor_copy(obuf[:, gi2 * 512:(gi2 + 1) * 512],
                                        psA[:, gi2 * 512:(gi2 + 1) * 512]), f'{tag}ev', 1)
            blk.vector(f1v)
            ev0 = c.n(f'{tag}ev')
            b_sg = c.n(f'{tag}sg')
            for dp in range(8):
                def f2g(g, dp=dp):
                    if dp == 0:
                        c.wait(g, 'prAdone')
                    for o in range(16):
                        i2 = dp * 16 + o
                        if i2 >= 2:
                            c.wait(g, f'{tag}pe', (i2 - 1) * 8)
                            c.wait(g, f'{tag}lq{i2 % 2}', 16 * (i2 // 2))
                            c.wait(g, f'{tag}lk{i2 % 2}', 16 * (i2 // 2))
                        c.inc(g.dma_start(out=bar[:, (i2 % 2) * 256:(i2 % 2) * 256 + 256],
                                          in_=Wq[o * 128:(o + 1) * 128, dp * 256:(dp + 1) * 256]),
                              f'{tag}lq{i2 % 2}', 16)
                        c.inc(g.dma_start(out=bar[:, 512 + (i2 % 2) * 2048: 512 + (i2 % 2 + 1) * 2048],
                                          in_=Wk[o * 128:(o + 1) * 128, :]),
                              f'{tag}lk{i2 % 2}', 16)
                blk.gpsimd(f2g)
                def f2t(t, dp=dp, ev0=ev0):
                    if dp == 0:
                        c.wait(t, f'{tag}ev', ev0)
                    if dp >= 1:
                        c.wait(t, f'{tag}ev', ev0 + dp * 8)
                    for o in range(16):
                        i2 = dp * 16 + o
                        c.wait(t, f'{tag}lq{i2 % 2}', 16 * (i2 // 2 + 1))
                        c.wait(t, f'{tag}lk{i2 % 2}', 16 * (i2 // 2 + 1))
                        for db2 in range(2):
                            ps = psA if db2 == 0 else psB
                            for dg in range(4):
                                ins = t.matmul(ps[:, dg * 512:(dg + 1) * 512],
                                               bar[:, (i2 % 2) * 256 + db2 * 128:(i2 % 2) * 256 + (db2 + 1) * 128],
                                               bar[:, 512 + (i2 % 2) * 2048 + dg * 512: 512 + (i2 % 2) * 2048 + (dg + 1) * 512],
                                               start=(o == 0), stop=(o == 15))
                        c.inc(ins, f'{tag}pe', 8)
                blk.tensor(f2t)
                def f2v(v, dp=dp):
                    c.wait(v, f'{tag}pe', (dp + 1) * 16 * 8)
                    if dp >= 1:
                        c.wait(v, f'{tag}sg', b_sg + 16 * 2 * dp)
                    for db2 in range(2):
                        ps = psA if db2 == 0 else psB
                        for dg in range(4):
                            c.inc(v.scalar_tensor_tensor(
                                out=bar[:, 4608 + db2 * 2048 + dg * 512: 4608 + db2 * 2048 + (dg + 1) * 512],
                                in0=ps[:, dg * 512:(dg + 1) * 512],
                                scalar=smallv[:, 4 + dp * 2 + db2: 5 + dp * 2 + db2],
                                in1=obuf[:, dg * 512:(dg + 1) * 512],
                                op0=ALU.mult, op1=ALU.mult), f'{tag}ev', 1)
                blk.vector(f2v)
                def f2s(sy, dp=dp, ev0=ev0):
                    c.wait(sy, f'{tag}ev', ev0 + (dp + 1) * 8)
                    for db2 in range(2):
                        c.inc(sy.dma_start(out=g_d[(dp * 2 + db2) * 128:(dp * 2 + db2 + 1) * 128, :],
                                           in_=bar[:, 4608 + db2 * 2048: 4608 + (db2 + 1) * 2048]),
                              f'{tag}sg', 16)
                blk.sync(f2s)
            b_lo = [c.n(f'{tag}lo0'), c.n(f'{tag}lo1')]
            def f3g(g):
                c.wait(g, f'{tag}sg', c.n(f'{tag}sg'))
                for e in range(16):
                    c.inc(g.dma_start(out=bar[:, 8704 + e * 16: 8704 + (e + 1) * 16],
                                      in_=Amat[e * 128:(e + 1) * 128, :]), f'{tag}lA', 16)
                for cb in range(16):
                    for e in range(16):
                        i3 = cb * 16 + e
                        if i3 >= 2:
                            c.wait(g, f'{tag}pw', i3 - 1)
                            c.wait(g, f'{tag}lo{i3 % 2}', b_lo[i3 % 2] + 16 * (i3 // 2))
                        c.inc(g.dma_start(out=bar[:, (i3 % 2) * 128:(i3 % 2) * 128 + 128],
                                          in_=Wo[e * 128:(e + 1) * 128, cb * 128:(cb + 1) * 128]),
                              f'{tag}lo{i3 % 2}', 16)
            blk.gpsimd(f3g)
            def f3t(t):
                c.wait(t, f'{tag}lA', c.n(f'{tag}lA'))
                c.wait(t, f'{tag}sg', c.n(f'{tag}sg'))
                for cb in range(16):
                    if cb >= 1:
                        c.wait(t, f'{tag}evw', cb)
                    for e in range(16):
                        i3 = cb * 16 + e
                        c.wait(t, f'{tag}lo{i3 % 2}', b_lo[i3 % 2] + 16 * (i3 // 2 + 1))
                        ins = t.matmul(psA[:, 0:16],
                                       bar[:, (i3 % 2) * 128:(i3 % 2) * 128 + 128],
                                       bar[:, 8704 + e * 16: 8704 + (e + 1) * 16],
                                       start=(e == 0), stop=(e == 15))
                        c.inc(ins, f'{tag}pw', 1)
                blk_ = None
            blk.tensor(f3t)
            def f3s(s):
                for cb in range(16):
                    c.wait(s, f'{tag}pw', (cb + 1) * 16)
                    c.inc(s.copy(bar[:, 8960 + cb * 16: 8960 + (cb + 1) * 16],
                                 psA[:, 0:16]), f'{tag}evw', 1)
            blk.scalar(f3s)
            b_lv = [c.n(f'{tag}lv0'), c.n(f'{tag}lv1')]
            def f4g(g):
                c.wait(g, f'{tag}evw', 16)
                for db in range(16):
                    for h in range(16):
                        i4 = db * 16 + h
                        if i4 >= 2:
                            c.wait(g, f'{tag}pv', i4 - 1)
                            c.wait(g, f'{tag}lv{i4 % 2}', b_lv[i4 % 2] + 16 * (i4 // 2))
                        c.inc(g.dma_start(out=bar[:, (i4 % 2) * 128:(i4 % 2) * 128 + 128],
                                          in_=Wv[h * 128:(h + 1) * 128, db * 128:(db + 1) * 128]),
                              f'{tag}lv{i4 % 2}', 16)
            blk.gpsimd(f4g)
            def f4t(t):
                c.wait(t, f'{tag}evw', 16)
                for db in range(16):
                    if db >= 1:
                        c.wait(t, f'{tag}evv', db)
                    for h in range(16):
                        i4 = db * 16 + h
                        c.wait(t, f'{tag}lv{i4 % 2}', b_lv[i4 % 2] + 16 * (i4 // 2 + 1))
                        ins = t.matmul(psA[:, 512 + h * 16: 512 + (h + 1) * 16],
                                       bar[:, (i4 % 2) * 128:(i4 % 2) * 128 + 128],
                                       bar[:, 8960 + h * 16: 8960 + (h + 1) * 16],
                                       start=True, stop=True)
                        c.inc(ins, f'{tag}pv', 1)
            blk.tensor(f4t)
            b_sv = [c.n(f'{tag}sv0'), c.n(f'{tag}sv1')]
            def f4s(s):
                for db in range(16):
                    c.wait(s, f'{tag}pv', (db + 1) * 16)
                    if db >= 2:
                        c.wait(s, f'{tag}sv{db % 2}', b_sv[db % 2] + 16 * ((db - 2) // 2 + 1))
                    c.inc(s.activation(out=bar[:, 2048 + (db % 2) * 256: 2048 + (db % 2 + 1) * 256],
                                       in_=psA[:, 512:768], func=AF.Copy,
                                       scale=smallv[:, 4 + db: 5 + db]), f'{tag}evv', 1)
            blk.scalar(f4s)
            def f4sy(sy):
                for db in range(16):
                    c.wait(sy, f'{tag}evv', db + 1)
                    c.inc(sy.dma_start(out=va_d[db * 128:(db + 1) * 128, :],
                                       in_=bar[:, 2048 + (db % 2) * 256: 2048 + (db % 2 + 1) * 256]),
                          f'{tag}sv{db % 2}', 16)
            blk.sync(f4sy)
            def f5g(g):
                c.wait(g, f'{tag}pv', c.n(f'{tag}pv'))
                for db in range(16):
                    c.inc(g.dma_start(out=bar[:, 8704 + db * 16: 8704 + (db + 1) * 16],
                                      in_=Amat[db * 128:(db + 1) * 128, :]), f'{tag}lt', 16)
            blk.gpsimd(f5g)
            def f5v(v):
                c.wait(v, f'{tag}lt', c.n(f'{tag}lt'))
                c.wait(v, f'{tag}pv', c.n(f'{tag}pv'))
                for db in range(16):
                    c.inc(v.tensor_scalar(out=bar[:, 9216 + db * 16: 9216 + (db + 1) * 16],
                                          in0=bar[:, 8704 + db * 16: 8704 + (db + 1) * 16],
                                          scalar1=smallv[:, 20 + db: 21 + db], scalar2=0.1,
                                          op0=ALU.mult, op1=ALU.mult), f'{tag}vA', 1)
            blk.vector(f5v)
            wlist = [(Win, wint_d, 64, True), (Wup, wupt_d, 64, True), (Wdn, wdnt_d, 16, False)]
            b_l6 = [c.n(f'{tag}l6{j}') for j in range(4)]
            b_st = [c.n(f'{tag}st0'), c.n(f'{tag}st1')]
            it5 = 0
            for (wsrc, wdst, nfb, scaled) in wlist:
                ndb = 2048 * FF // (nfb * 128) // 128
                for fb in range(nfb):
                    for db in range(ndb):
                        it = it5
                        it5 += 1
                        sl4 = it % 4
                        def f6g(g, it=it, sl4=sl4, fb=fb, db=db, wsrc=wsrc):
                            if it == 0:
                                c.wait(g, f'{tag}vA', 16)
                            if it >= 4:
                                c.wait(g, f'{tag}pt', it - 3)
                                c.wait(g, f'{tag}l6{sl4}', b_l6[sl4] + 16 * (it // 4))
                            c.inc(g.dma_start(out=bar[:, 11520 + sl4 * 128: 11520 + (sl4 + 1) * 128],
                                              in_=wsrc[fb * 128:(fb + 1) * 128, db * 128:(db + 1) * 128]),
                                  f'{tag}lw{sl4}' if False else f'{tag}l6{sl4}', 16)
                        blk.gpsimd(f6g)
                        def f6t(t, it=it, sl4=sl4):
                            if it == 0:
                                c.wait(t, f'{tag}vA', 16)
                                c.wait(t, f'{tag}sv0', c.n(f'{tag}sv0'))
                                c.wait(t, f'{tag}sv1', c.n(f'{tag}sv1'))
                            c.wait(t, f'{tag}l6{sl4}', b_l6[sl4] + 16 * (it // 4 + 1))
                            if it >= 4:
                                c.wait(t, f'{tag}evt', it - 3)
                            c.inc(t.matmul(psA[:, sl4 * 512: sl4 * 512 + 128],
                                           bar[:, 11520 + sl4 * 128: 11520 + (sl4 + 1) * 128],
                                           identb[:], start=True, stop=True), f'{tag}pt', 1)
                        blk.tensor(f6t)
                        def f6s(s, it=it, sl4=sl4, db=db, scaled=scaled):
                            c.wait(s, f'{tag}pt', it + 1)
                            if it >= 2:
                                c.wait(s, f'{tag}st{it % 2}', b_st[it % 2] + 16 * ((it - 2) // 2 + 1))
                            if scaled:
                                c.inc(s.activation(out=bar[:, 12032 + (it % 2) * 128: 12032 + (it % 2 + 1) * 128],
                                                   in_=psA[:, sl4 * 512: sl4 * 512 + 128],
                                                   func=AF.Copy, scale=smallv[:, 20 + db: 21 + db]),
                                      f'{tag}evt', 1)
                            else:
                                c.inc(s.copy(bar[:, 12032 + (it % 2) * 128: 12032 + (it % 2 + 1) * 128],
                                             psA[:, sl4 * 512: sl4 * 512 + 128]), f'{tag}evt', 1)
                        blk.scalar(f6s)
                        def f6sy(sy, it=it, fb=fb, db=db, wdst=wdst):
                            c.wait(sy, f'{tag}evt', it + 1)
                            c.inc(sy.dma_start(out=wdst[db * 128:(db + 1) * 128, fb * 128:(fb + 1) * 128],
                                               in_=bar[:, 12032 + (it % 2) * 128: 12032 + (it % 2 + 1) * 128]),
                                  f'{tag}st{it % 2}', 16)
                        blk.sync(f6sy)
            def fend(sy):
                c.wait(sy, f'{tag}st0')
                c.wait(sy, f'{tag}st1')
                c.wait(sy, f'{tag}sv0')
                c.wait(sy, f'{tag}sv1')
                c.wait(sy, f'{tag}sg')
                c.inc(sy.nop(), f'{tag}done', 1)
            blk.sync(fend)

        def ph_wout(b, tag, dtag):
            B = {s: c.n(f'{tag}{s}') for s in
                 (['pe', 'lx'] + [f'lw{i}' for i in range(4)] + [f'ly{i}' for i in range(8)])}
            guards = {}
            def gwait(eng, key):
                g_ = guards.get(key)
                if g_:
                    c.wait(eng, g_[0], g_[1])
            def gset(key, sem):
                guards[key] = (sem, c.n(sem))
            for tp in range(8):
                def fx(sy, tp=tp):
                    if tp == 0:
                        c.wait(sy, f'{dtag}done')
                        c.wait(sy, f'{tag}lx', B['lx'])
                    gwait(sy, ('xload',))
                    for tj in range(2):
                        tt = 2 * tp + tj
                        c.inc(sy.dma_start(out=z1tok[:, tj * 2052: tj * 2052 + 2048],
                                           in_=xin[b * S + tt * 128: b * S + (tt + 1) * 128, :]),
                              f'{tag}lx', 16)
                blk.sync(fx)
                def fw(gp, tp=tp):
                    if tp == 0:
                        c.wait(gp, f'{dtag}done')
                        c.wait(gp, 'prAdone', 1)
                    for vb in range(32):
                        if tp == 0 and vb < 4:
                            c.wait(gp, f'{tag}lw{vb % 4}', B[f'lw{vb % 4}'])
                        iw = tp * 32 + vb
                        if iw >= 4:
                            c.wait(gp, f'{tag}pe', B['pe'] + iw - 3)
                        c.inc(gp.dma_start(out=bar[:, (vb % 4) * 2048:(vb % 4 + 1) * 2048],
                                           in_=woute_d[vb * 128:(vb + 1) * 128, :]),
                              f'{tag}lw{vb % 4}', 16)
                blk.gpsimd(fw)
                def fy(sy, tp=tp):
                    for vb in range(32):
                        if tp == 0 and vb < 8:
                            c.wait(sy, f'{tag}ly{vb % 8}', B[f'ly{vb % 8}'])
                        iw = tp * 32 + vb
                        if iw >= 8:
                            c.wait(sy, f'{tag}pe', B['pe'] + iw - 7)
                        c.inc(sy.dma_start(out=bar[:, 10240 + (vb % 8) * 256: 10240 + (vb % 8 + 1) * 256],
                                           in_=yT_d[vb * 128:(vb + 1) * 128, tp * 256:(tp + 1) * 256]),
                              f'{tag}ly{vb % 8}', 16)
                blk.sync(fy)
                def ft(t, tp=tp):
                    if tp == 0:
                        for nm in ('d5vs', 'd5sy', 'd5evV', 'd5evS', 'd5sq',
                                   'd5so', 'd5vo', 'd5vy', 'd5vm', 'd5sm'):
                            c.wait(t, nm)
                    gwait(t, ('psA',))
                    gwait(t, ('psB',))
                    for vb in range(32):
                        c.wait(t, f'{tag}lw{vb % 4}', B[f'lw{vb % 4}'] + 16 * (tp * 8 + vb // 4 + 1))
                        c.wait(t, f'{tag}ly{vb % 8}', B[f'ly{vb % 8}'] + 16 * (tp * 4 + vb // 8 + 1))
                        for tj in range(2):
                            ps = psA if tj == 0 else psB
                            lhs = bar[:, 10240 + (vb % 8) * 256 + tj * 128: 10240 + (vb % 8) * 256 + (tj + 1) * 128]
                            for dgi in range(4):
                                ins = t.matmul(ps[:, dgi * 512:(dgi + 1) * 512], lhs,
                                               bar[:, (vb % 4) * 2048 + dgi * 512:(vb % 4) * 2048 + (dgi + 1) * 512],
                                               start=(vb == 0), stop=(vb == 31))
                        c.inc(ins, f'{tag}pe', 1)
                blk.tensor(ft)
                pe_n = c.n(f'{tag}pe')
                for tj in range(2):
                    ps = psA if tj == 0 else psB
                    def fs(s, tp=tp, tj=tj, ps=ps, pe_n=pe_n):
                        if tp == 0 and tj == 0:
                            for nm in ('d5vm', 'd5vo', 'd5vy', 'd5vs'):
                                c.wait(s, nm)
                        c.wait(s, f'{tag}pe', pe_n)
                        for bk in range(4):
                            c.inc(s.activation(out=obuf[:, tj * 2048 + bk * 512: tj * 2048 + (bk + 1) * 512],
                                               in_=ps[:, bk * 512:(bk + 1) * 512], func=AF.Square,
                                               accum_out=cols[:, 104 + tj * 4 + bk: 105 + tj * 4 + bk]),
                                  f'{tag}sq', 1)
                    blk.scalar(fs)
                    def fv(v, tp=tp, tj=tj, ps=ps, pe_n=pe_n):
                        if tp == 0 and tj == 0:
                            for nm in ('d5sm', 'd5sq', 'd5so', 'd5sy', 'd5evS'):
                                c.wait(v, nm)
                        c.wait(v, f'{tag}pe', pe_n)
                        base = c.n(f'{tag}lnv')
                        for bk in range(4):
                            c.inc(v.reduce_sum(cols[:, 96 + tj * 4 + bk: 97 + tj * 4 + bk],
                                               ps[:, bk * 512:(bk + 1) * 512], axis=AXX),
                                  f'{tag}lnv', 1)
                        c.wait(v, f'{tag}lnv')
                        c.inc(v.tensor_tensor(out=cols[:, 96 + tj * 4: 97 + tj * 4],
                                              in0=cols[:, 96 + tj * 4: 97 + tj * 4],
                                              in1=cols[:, 97 + tj * 4: 98 + tj * 4], op=ALU.add), f'{tag}lnv', 1)
                        c.wait(v, f'{tag}lnv')
                        c.inc(v.tensor_tensor(out=cols[:, 98 + tj * 4: 99 + tj * 4],
                                              in0=cols[:, 98 + tj * 4: 99 + tj * 4],
                                              in1=cols[:, 99 + tj * 4: 100 + tj * 4], op=ALU.add), f'{tag}lnv', 1)
                        c.wait(v, f'{tag}lnv')
                        c.inc(v.tensor_tensor(out=cols[:, 96 + tj * 4: 97 + tj * 4],
                                              in0=cols[:, 96 + tj * 4: 97 + tj * 4],
                                              in1=cols[:, 98 + tj * 4: 99 + tj * 4], op=ALU.add), f'{tag}lnv', 1)
                        c.wait(v, f'{tag}sq', c.n(f'{tag}sq'))
                        for bk in range(1, 4):
                            c.wait(v, f'{tag}lnv')
                            c.inc(v.tensor_tensor(out=cols[:, 104 + tj * 4: 105 + tj * 4],
                                                  in0=cols[:, 104 + tj * 4: 105 + tj * 4],
                                                  in1=cols[:, 104 + tj * 4 + bk: 105 + tj * 4 + bk],
                                                  op=ALU.add), f'{tag}lnv', 1)
                        mu = cols[:, 112 + tj: 113 + tj]
                        var = cols[:, 114 + tj: 115 + tj]
                        musq = cols[:, 116 + tj: 117 + tj]
                        c.wait(v, f'{tag}lnv')
                        c.inc(v.tensor_scalar(out=mu, in0=cols[:, 96 + tj * 4: 97 + tj * 4],
                                              scalar1=1.0 / D, scalar2=None, op0=ALU.mult), f'{tag}lnv', 1)
                        c.wait(v, f'{tag}lnv')
                        c.inc(v.tensor_tensor(out=musq, in0=mu, in1=mu, op=ALU.mult), f'{tag}lnv', 1)
                        c.wait(v, f'{tag}lnv')
                        c.inc(v.scalar_tensor_tensor(out=var, in0=cols[:, 104 + tj * 4: 105 + tj * 4],
                                                     scalar=1.0 / D, in1=musq,
                                                     op0=ALU.mult, op1=ALU.subtract), f'{tag}lnv', 1)
                    blk.vector(fv)
                    def fs2(s, tp=tp, tj=tj):
                        c.wait(s, f'{tag}lnv', c.n(f'{tag}lnv'))
                        c.inc(s.activation(out=cols[:, 118 + tj: 119 + tj],
                                           in_=cols[:, 114 + tj: 115 + tj],
                                           func=AF.Sqrt, bias=smallv[:, 40:41]), f'{tag}sd', 1)
                    blk.scalar(fs2)
                    def fv2(v, tp=tp, tj=tj, ps=ps):
                        c.wait(v, f'{tag}sd', c.n(f'{tag}sd'))
                        rs = cols[:, 120 + tj: 121 + tj]
                        mu = cols[:, 112 + tj: 113 + tj]
                        murs = cols[:, 122 + tj: 123 + tj]
                        c.inc(v.reciprocal(rs, cols[:, 118 + tj: 119 + tj]), f'{tag}lnw', 1)
                        c.wait(v, f'{tag}lnw')
                        c.inc(v.tensor_tensor(out=murs, in0=mu, in1=rs, op=ALU.mult), f'{tag}lnw', 1)
                        c.wait(v, f'{tag}lnw')
                        c.wait(v, f'{tag}lx', B['lx'] + 16 * 2 * (tp + 1))
                        gwait(v, ('stg', tj))
                        c.inc(v.tensor_scalar(out=stage[:, tj * 2048:(tj + 1) * 2048],
                                              in0=z1tok[:, tj * 2052: tj * 2052 + 2048],
                                              scalar1=1.0, scalar2=murs,
                                              op0=ALU.mult, op1=ALU.subtract), f'{tag}lnw', 1)
                        for bk in range(4):
                            c.wait(v, f'{tag}lnw')
                            c.inc(v.scalar_tensor_tensor(out=stage[:, tj * 2048 + bk * 512: tj * 2048 + (bk + 1) * 512],
                                                         in0=ps[:, bk * 512:(bk + 1) * 512],
                                                         scalar=rs,
                                                         in1=stage[:, tj * 2048 + bk * 512: tj * 2048 + (bk + 1) * 512],
                                                         op0=ALU.mult, op1=ALU.add), f'{tag}lnw', 1)
                    blk.vector(fv2)
                    def fst(sy, tp=tp, tj=tj):
                        c.wait(sy, f'{tag}lnw', c.n(f'{tag}lnw'))
                        tt = 2 * tp + tj
                        if tp == 0:
                            c.wait(sy, f'{tag}x2' if tj == 0 else f'{tag}x2b',
                                   c.n(f'{tag}x2' if tj == 0 else f'{tag}x2b'))
                        c.inc(sy.dma_start(out=x2_d[b * S + tt * 128: b * S + (tt + 1) * 128, :],
                                           in_=stage[:, tj * 2048:(tj + 1) * 2048]),
                              f'{tag}x2' if tj == 0 else f'{tag}x2b', 16)
                    blk.sync(fst)
                    gset(('stg', tj), f'{tag}x2' if tj == 0 else f'{tag}x2b')
                    gset(('psA',) if tj == 0 else ('psB',), f'{tag}lnw')
                gset(('xload',), f'{tag}lnw')
            def fend(sy):
                c.wait(sy, f'{tag}x2')
                c.wait(sy, f'{tag}x2b')
                c.inc(sy.nop(), f'{tag}done', 1)
            blk.sync(fend)

        def ph_qg(b, tag, z2tag, wtag):
            B = {s: c.n(f'{tag}{s}') for s in ('lg0', 'lg1', 'lv0', 'pe',
                                               'pv', 'evS', 'evV', 'eva',
                                               'so0', 'so1', 'on')}
            z2tevF = c.n(f'{z2tag}tev')
            def fva(sy):
                c.wait(sy, f'{tag}lv0', B['lv0'])
                c.wait(sy, 'prBdone', 1)
                c.wait(sy, f'{wtag}pe')
                for nm in ('lw0', 'lw1', 'lw2', 'lw3'):
                    c.wait(sy, f'{wtag}{nm}')
                for d in range(16):
                    c.inc(sy.dma_start(out=bar[:, 4352 + d * 256: 4352 + (d + 1) * 256],
                                       in_=va_d[d * 128:(d + 1) * 128, :]),
                          f'{tag}lv0', 16)
            blk.sync(fva)
            def fon(g):
                c.wait(g, f'{wtag}pe')
                if 'a5' in XB:
                    c.wait(g, 'a5pt', XB['a5']['pt'])
                for st5 in range(16):
                    for h in range(16):
                        c.inc(g.memset(bar[:, st5 * 272 + h * 17 + 16: st5 * 272 + h * 17 + 17], 1.0),
                              f'{tag}on', 1)
            blk.gpsimd(fon)
            def fpv(t):
                c.wait(t, f'{z2tag}tev', z2tevF)
                for nm in ('sq', 'lnv', 'lnw'):
                    c.wait(t, f'{wtag}{nm}')
                c.wait(t, f'{tag}lv0', B['lv0'] + 256)
                for st5 in range(16):
                    if st5 >= 2:
                        c.wait(t, f'{tag}eva', B['eva'] + st5 - 1)
                    for d in range(16):
                        ins = t.matmul(psA[:, (st5 % 2) * 512:(st5 % 2) * 512 + 256],
                                       actT[:, d * S + st5 * 128: d * S + (st5 + 1) * 128],
                                       bar[:, 4352 + d * 256: 4352 + (d + 1) * 256],
                                       start=(d == 0), stop=(d == 15))
                    c.inc(ins, f'{tag}pv', 1)
            blk.tensor(fpv)
            def fpa(s):
                for st5 in range(16):
                    c.wait(s, f'{tag}pv', B['pv'] + st5 + 1)
                    if st5 == 0:
                        c.wait(s, f'{wtag}pe')
                        for nm in ('lw0', 'lw1', 'lw2', 'lw3'):
                            c.wait(s, f'{wtag}{nm}')
                        if 'a5' in XB:
                            c.wait(s, 'a5pt', XB['a5']['pt'])
                    for h in range(16):
                        ins = s.copy(bar[:, st5 * 272 + h * 17: st5 * 272 + h * 17 + 16],
                                     psA[:, (st5 % 2) * 512 + h * 16:(st5 % 2) * 512 + (h + 1) * 16])
                    c.inc(ins, f'{tag}eva', 1)
            blk.scalar(fpa)
            def flgso(sy):
                c.wait(sy, f'{tag}lg0', B['lg0'])
                c.wait(sy, f'{tag}lg1', B['lg1'])
                c.wait(sy, f'{tag}pv', B['pv'] + 16)
                for dp in range(17):
                    if dp < 16:
                        par = dp % 2
                        if dp >= 2:
                            c.wait(sy, f'{tag}pe', B['pe'] + dp - 1)
                            c.wait(sy, f'{tag}lg{par}', B[f'lg{par}'] + 256 * (dp // 2))
                        for d in range(16):
                            c.inc(sy.dma_start(
                                out=bar[:, 4352 + par * 2048 + d * 128: 4352 + par * 2048 + (d + 1) * 128],
                                in_=g_d[d * 128:(d + 1) * 128, dp * 128:(dp + 1) * 128]),
                                  f'{tag}lg{par}', 16)
                    if dp >= 1:
                        j = dp - 1
                        jp = j % 2
                        c.wait(sy, f'{tag}evS' if jp == 0 else f'{tag}evV',
                               B['evS' if jp == 0 else 'evV'] + 4 * (j // 2 + 1))
                        if j < 2:
                            c.wait(sy, f'{tag}so{jp}', B[f'so{jp}'])
                            if 'a5' in XB:
                                for nm in ('lq0', 'lq1'):
                                    c.wait(sy, f'a5{nm}', XB['a5'][nm])
                        c.inc(sy.dma_start(out=qgt_d[j * 128:(j + 1) * 128, :],
                                           in_=wring[:, jp * 2048:(jp + 1) * 2048]),
                              f'{tag}so{jp}', 16)
            blk.sync(flgso)
            def fpe(t):
                for dp in range(16):
                    par = dp % 2
                    ps = psA if par == 0 else psB
                    if dp == 0:
                        c.wait(t, f'{tag}eva', B['eva'] + 16)
                    if dp >= 2:
                        c.wait(t, f'{tag}evS' if par == 0 else f'{tag}evV',
                               B['evS' if par == 0 else 'evV'] + 4 * ((dp - 2) // 2 + 1))
                    c.wait(t, f'{tag}lg{par}', B[f'lg{par}'] + 256 * (dp // 2 + 1))
                    for d in range(16):
                        gt = bar[:, 4352 + par * 2048 + d * 128: 4352 + par * 2048 + (d + 1) * 128]
                        for gi5 in range(4):
                            ins = t.matmul(ps[:, gi5 * 512:(gi5 + 1) * 512], gt,
                                           actT[:, d * S + gi5 * 512: d * S + (gi5 + 1) * 512],
                                           start=(d == 0), stop=(d == 15))
                    c.inc(ins, f'{tag}pe', 1)
            blk.tensor(fpe)
            def fes(s):
                for dp in range(0, 16, 2):
                    c.wait(s, f'{tag}pe', B['pe'] + dp + 1)
                    if dp == 0:
                        c.wait(s, 'p2lw0')
                        c.wait(s, 'p2lw1')
                    if dp >= 2:
                        c.wait(s, f'{tag}so0', B['so0'] + 16 * (dp // 2))
                    for gi5 in range(4):
                        c.inc(s.copy(wring[:, gi5 * 512:(gi5 + 1) * 512],
                                     psA[:, gi5 * 512:(gi5 + 1) * 512]), f'{tag}evS', 1)
            blk.scalar(fes)
            def fev(v):
                for dp in range(1, 16, 2):
                    c.wait(v, f'{tag}pe', B['pe'] + dp + 1)
                    if dp == 1:
                        c.wait(v, 'p2lw0')
                        c.wait(v, 'p2lw1')
                    if dp >= 2:
                        c.wait(v, f'{tag}so1', B['so1'] + 16 * (dp // 2))
                    for gi5 in range(4):
                        c.inc(v.tensor_copy(wring[:, 2048 + gi5 * 512: 2048 + (gi5 + 1) * 512],
                                            psB[:, gi5 * 512:(gi5 + 1) * 512]), f'{tag}evV', 1)
            blk.vector(fev)
            def fend(sy):
                c.wait(sy, f'{tag}so0')
                c.wait(sy, f'{tag}so1')
                c.wait(sy, f'{tag}eva', B['eva'] + 16)
                if DEV_DBG:
                    c.wait(sy, f'{tag}on', B['on'] + 256)
                    c.inc(sy.dma_start(out=dbgb[:, 0:4096], in_=bar[:, 0:4096]),
                          f'{tag}dbg', 16)
                    c.wait(sy, f'{tag}dbg')
                c.inc(sy.nop(), f'{tag}done', 1)
            blk.sync(fend)

        def ph_attn(b, tag, qtag, z2tag, wtag):
            B = {s: c.n(f'{tag}{s}') for s in
                 ('lq0', 'lq1', 'ps', 'pt', 'ph', 'pmh', 'ex', 'pv', 'th',
                  'me', 'mx', 'sm', 'mh')}
            z2tevF = c.n(f'{z2tag}tev')
            QN = {s: c.n(f'{qtag}{s}') for s in
                  ('pe', 'lg0', 'lg1', 'evS', 'evV', 'eva', 'so0', 'so1', 'on')}
            WNsq = c.n(f'{wtag}sq')
            PRg1, PRg2 = c.n('GD1'), c.n('GD2')
            s128 = smallv[:, 45:46]
            chunks = []
            group_chunks = {}
            pvg = {}
            ex_after_group = {}
            pt_after_group = {}
            exg_ptwait = {}
            ps_after_h = {}
            csl = 0; ps_n = 0; ex_n = 0; pt_n = 0
            last_pt_user = {0: 0, 1: 0}
            for h in range(16):
                for j in range(16):
                    exg_ptwait[(h, j)] = last_pt_user[j % 2]
                    gl = []
                    gi0 = j // 4
                    for gi in range(gi0, 4):
                        if gi == gi0:
                            cstart, N = j * 128, (4 - j % 4) * 128
                        else:
                            cstart, N = gi * 512, 512
                        ps_n += 1
                        ent = (h, j, cstart, N, csl % 6, gi == gi0, csl - 5, ps_n)
                        chunks.append(ent)
                        gl.append(ent)
                        csl += 1
                    group_chunks[(h, j)] = gl
                    ex_after_group[(h, j)] = ex_n + len(gl)
                    ex_n += len(gl)
                    pvg[(h, j)] = [(i, pt_n + k + 1) for k, i in enumerate(range(j, 16))]
                    pt_n += 16 - j
                    pt_after_group[(h, j)] = pt_n
                    last_pt_user[j % 2] = pt_n
                ps_after_h[h] = ps_n
            n_chunks = len(chunks)

            def cslot(slot, n0, n1, p0=0, p1=128):
                ps = psA if slot < 4 else psB
                sb = (slot % 4) * 512
                return ps[p0:p1, sb + n0: sb + n1]

            def fpe(t):
                for i in range(16):
                    if i == 0:
                        c.wait(t, f'{z2tag}tev', z2tevF)
                        c.wait(t, f'{qtag}evS', QN['evS'])
                        c.wait(t, f'{qtag}evV', QN['evV'])
                        c.wait(t, f'{qtag}eva', QN['eva'])
                        c.wait(t, f'{qtag}on', QN['on'])
                        c.wait(t, f'{tag}sm', B['sm'])
                    if i >= 4:
                        c.wait(t, f'{tag}pv', B['pv'] + i - 3)
                    for d in range(16):
                        ins = t.matmul(psA[:, (i % 4) * 512:(i % 4) * 512 + 16],
                                       actT[:, d * S + i * 128: d * S + (i + 1) * 128],
                                       bar[:, 9216 + d * 16: 9216 + (d + 1) * 16],
                                       start=(d == 0), stop=(d == 15))
                    c.inc(ins, f'{tag}ph', 1)
                for h in range(16):
                    for j in range(16):
                        if (h, j) == (0, 0):
                            c.wait(t, f'{tag}pv', B['pv'] + 16)
                        if j == 0:
                            c.wait(t, f'{tag}lq{h % 2}', B[f'lq{h % 2}'] + 16 * (h // 2 + 1))
                        for (hh, jj, cstart, N, slot, msk, exg, psn) in group_chunks[(h, j)]:
                            if exg > 0:
                                c.wait(t, f'{tag}ex', B['ex'] + exg)
                            sAP = cslot(slot, 0, N)
                            ins = t.matmul(sAP,
                                           actT[:, h * S + j * 128: h * S + (j + 1) * 128],
                                           wring[:, (h % 2) * 2048 + cstart: (h % 2) * 2048 + cstart + N],
                                           start=True, stop=not msk)
                            if msk:
                                ins = t.matmul(cslot(slot, 0, 128), identb[:],
                                               maskUbS[:], start=False, stop=True)
                            c.inc(ins, f'{tag}ps', 1)
                        pvl = [(h, j - 1)] if j >= 1 else []
                        if j == 15:
                            pvl.append((h, 15))
                        for (ph_, pj) in pvl:
                            if pj == 0 and ph_ >= 2:
                                c.wait(t, f'{tag}sm', B['sm'] + 16 * (ph_ - 1))
                            c.wait(t, f'{tag}ex', B['ex'] + ex_after_group[(ph_, pj)])
                            base = 1536 if ph_ % 2 == 0 else 1024
                            for (i, ptn) in pvg[(ph_, pj)]:
                                ins = t.matmul(psB[:, base + i * 32: base + i * 32 + 17],
                                               bar[:, 4352 + (pj % 2) * 2048 + i * 128: 4352 + (pj % 2) * 2048 + (i + 1) * 128],
                                               bar[:, pj * 272 + ph_ * 17: pj * 272 + (ph_ + 1) * 17],
                                               start=(pj == 0 and i == 0),
                                               stop=(pj == 15 and i == 15))
                                c.inc(ins, f'{tag}pt', 1)
                for i in range(16):
                    if i == 0:
                        c.wait(t, f'{tag}ex', B['ex'] + ex_n)
                    if i >= 6:
                        c.wait(t, f'{tag}me', B['me'] + 2 * (i - 6) + 2)
                    c.wait(t, f'{tag}me', B['me'] + 2 * i + 1)
                    slot = i % 6
                    ins = t.matmul(cslot(slot, 0, 128, 0, 16),
                                   bar[:, 8704 + i * 16: 8704 + (i + 1) * 16],
                                   identb[:], start=True, stop=True)
                    c.inc(ins, f'{tag}pmh', 1)
            blk.tensor(fpe)
            def fsc(s):
                for i in range(16):
                    if i == 0:
                        c.wait(s, f'{tag}mh', B['mh'])
                    c.wait(s, f'{tag}ph', B['ph'] + i + 1)
                    c.inc(s.copy(obuf[:, i * 16:(i + 1) * 16],
                                 psA[:, (i % 4) * 512:(i % 4) * 512 + 16]), f'{tag}pv', 1)
                exd = 0
                for (h, j, cstart, N, slot, msk, exg, psn) in chunks:
                    if exd == 0:
                        c.wait(s, f'{qtag}lg0', QN['lg0'])
                        c.wait(s, f'{qtag}lg1', QN['lg1'])
                        c.wait(s, f'{qtag}pe', QN['pe'])
                    if cstart == j * 128 and exg_ptwait[(h, j)] > 0:
                        c.wait(s, f'{tag}pt', B['pt'] + exg_ptwait[(h, j)])
                    c.wait(s, f'{tag}ps', B['ps'] + psn)
                    c.inc(s.activation(
                        out=bar[:, 4352 + (j % 2) * 2048 + cstart: 4352 + (j % 2) * 2048 + cstart + N],
                        in_=cslot(slot, 0, N), func=AF.Exp, scale=s128), f'{tag}ex', 1)
                    exd += 1
                for i in range(16):
                    c.wait(s, f'{tag}sm', B['sm'] + 240 + i + 1)
                    c.inc(s.activation(out=obuf[:, 512 + i * 16: 512 + (i + 1) * 16],
                                       in_=obuf[:, 256 + i * 16: 256 + (i + 1) * 16],
                                       func=AF.Tanh), f'{tag}th', 1)
                for i in range(16):
                    c.wait(s, f'{tag}mh', B['mh'] + i + 1)
                    if i == 0:
                        c.wait(s, 'GD1', PRg1)
                        c.wait(s, 'GD2', PRg2)
                    c.inc(s.copy(bar[:, 8704 + i * 16: 8704 + (i + 1) * 16],
                                 obuf[:, 768 + i * 16: 768 + (i + 1) * 16]), f'{tag}me', 1)
                    c.wait(s, f'{tag}pmh', B['pmh'] + i + 1)
                    c.inc(s.copy(mhT_s[0:16, i * 128:(i + 1) * 128],
                                 cslot(i % 6, 0, 128, 0, 16)), f'{tag}me', 1)
            blk.scalar(fsc)
            def fve(v):
                for h in range(16):
                    base = 1536 if h % 2 == 0 else 1024
                    if h == 0:
                        c.wait(v, f'{wtag}sq', WNsq)
                        c.wait(v, f'{tag}th', B['th'])
                        c.wait(v, f'{tag}me', B['me'])
                    c.wait(v, f'{tag}pt', B['pt'] + pt_after_group[(h, 15)])
                    for i in range(16):
                        rc = obuf[:, 1040:1041]
                        c.wait(v, f'{tag}sm')
                        c.inc(v.reciprocal(rc, psB[:, base + i * 32 + 16: base + i * 32 + 17]),
                              f'{tag}mx', 1)
                        c.wait(v, f'{tag}mx')
                        if h == 0:
                            c.inc(v.tensor_scalar(out=obuf[:, 256 + i * 16: 256 + (i + 1) * 16],
                                                  in0=psB[:, base + i * 32: base + i * 32 + 16],
                                                  scalar1=rc, scalar2=None, op0=ALU.mult),
                                  f'{tag}sm', 1)
                        else:
                            c.inc(v.scalar_tensor_tensor(
                                out=obuf[:, 256 + i * 16: 256 + (i + 1) * 16],
                                in0=psB[:, base + i * 32: base + i * 32 + 16],
                                scalar=rc,
                                in1=obuf[:, 256 + i * 16: 256 + (i + 1) * 16],
                                op0=ALU.mult, op1=ALU.add), f'{tag}sm', 1)
                for i in range(16):
                    c.wait(v, f'{tag}th', B['th'] + i + 1)
                    if i == 0:
                        c.wait(v, f'{tag}pv', B['pv'] + 16)
                    c.inc(v.tensor_tensor(out=obuf[:, 768 + i * 16: 768 + (i + 1) * 16],
                                          in0=obuf[:, 512 + i * 16: 512 + (i + 1) * 16],
                                          in1=obuf[:, i * 16:(i + 1) * 16],
                                          op=ALU.mult), f'{tag}mh', 1)
            blk.vector(fve)
            def fsy(sy):
                for h in range(16):
                    if h < 2:
                        c.wait(sy, f'{tag}lq{h % 2}', B[f'lq{h % 2}'])
                        c.wait(sy, f'{qtag}so0', QN['so0'])
                        c.wait(sy, f'{qtag}so1', QN['so1'])
                        c.wait(sy, f'{tag}ex', B['ex'])
                        c.wait(sy, f'{tag}mx', B['mx'])
                    else:
                        c.wait(sy, f'{tag}ps', B['ps'] + ps_after_h[h - 2])
                        c.wait(sy, f'{tag}lq{h % 2}', B[f'lq{h % 2}'] + 16 * (h // 2))
                    c.inc(sy.dma_start(out=wring[:, (h % 2) * 2048:(h % 2 + 1) * 2048],
                                       in_=qgt_d[h * 128:(h + 1) * 128, :]),
                          f'{tag}lq{h % 2}', 16)
            blk.sync(fsy)
            def fend(sy):
                for nm in ('pt', 'ph', 'pmh', 'ps'):
                    c.wait(sy, f'{tag}{nm}', B[nm] + {'pt': pt_n, 'ph': 16,
                                                      'pmh': 16, 'ps': ps_n}[nm])
                c.wait(sy, f'{tag}me', B['me'] + 32)
                c.wait(sy, f'{tag}sm', B['sm'] + 256)
                c.wait(sy, f'{tag}ex', B['ex'] + ex_n)
                if DEV_DBG:
                    c.inc(sy.dma_start(out=dbgf[:, 0:1041], in_=obuf[:, 0:1041]),
                          f'{tag}dbg', 16)
                    c.inc(sy.dma_start(out=dbgb[0:16, 0:2048], in_=mhT_s[0:16, :]),
                          f'{tag}dbg', 16)
                    c.wait(sy, f'{tag}dbg')
                c.inc(sy.nop(), f'{tag}done', 1)
            blk.sync(fend)

        def ph_mlp(b, tag, atag, qtag):
            B = {s: c.n(f'{tag}{s}') for s in
                 ('lw0', 'lw1', 'lu0', 'lu1', 'pe', 'pu', 'sg', 'mv', 'sm_')}
            AN = {s: c.n(f'{atag}{s}') for s in
                  ('pt', 'ps', 'pmh', 'ph', 'ex', 'me', 'sm', 'mx', 'pv', 'th')}
            QS = {s: c.n(f'{qtag}{s}') for s in ('so0', 'so1')}
            OL = dict(XB.get('o6', {}))
            for fb in range(64):
                par = fb % 2
                def fgp(g, fb=fb, par=par):
                    if fb == 0:
                        c.wait(g, f'{atag}pt', AN['pt'])
                        c.wait(g, f'{tag}lw0', B['lw0'])
                        c.wait(g, f'{tag}lw1', B['lw1'])
                        c.wait(g, f'{tag}lu0', B['lu0'])
                        c.wait(g, f'{tag}lu1', B['lu1'])
                    if fb >= 2:
                        c.wait(g, f'{tag}pe', B['pe'] + fb - 1)
                        c.wait(g, f'{tag}pu', B['pu'] + fb - 1)
                    for d in range(16):
                        c.inc(g.dma_start(
                            out=bar[:, par * 4096 + d * 128: par * 4096 + (d + 1) * 128],
                            in_=wint_d[d * 128:(d + 1) * 128, fb * 128:(fb + 1) * 128]),
                              f'{tag}lw{par}', 16)
                    for d in range(16):
                        c.inc(g.dma_start(
                            out=bar[:, par * 4096 + 2048 + d * 128: par * 4096 + 2048 + (d + 1) * 128],
                            in_=wupt_d[d * 128:(d + 1) * 128, fb * 128:(fb + 1) * 128]),
                              f'{tag}lu{par}', 16)
                blk.gpsimd(fgp)
                def fpe(t, fb=fb, par=par):
                    if fb == 0:
                        c.wait(t, f'{atag}ex', AN['ex'])
                        c.wait(t, f'{atag}me', AN['me'])
                        c.wait(t, f'{atag}sm', AN['sm'])
                        c.wait(t, f'{atag}mx', AN['mx'])
                        c.wait(t, f'{atag}pv', AN['pv'])
                        c.wait(t, f'{atag}th', AN['th'])
                        c.wait(t, 'cstG', 16)
                    if fb >= 1:
                        c.wait(t, f'{tag}sg', B['sg'] + 4 * fb)
                        c.wait(t, f'{tag}mv', B['mv'] + 4 * fb)
                    c.wait(t, f'{tag}lw{par}', B[f'lw{par}'] + 256 * (fb // 2 + 1))
                    for d in range(16):
                        wt = bar[:, par * 4096 + d * 128: par * 4096 + (d + 1) * 128]
                        for gi in range(4):
                            t.matmul(psA[:, gi * 512:(gi + 1) * 512], wt,
                                     actT[:, d * S + gi * 512: d * S + (gi + 1) * 512],
                                     start=(d == 0), stop=False)
                    for gi in range(4):
                        ins = t.matmul(psA[:, gi * 512:(gi + 1) * 512],
                                       Bs[0:16, fb * 128:(fb + 1) * 128],
                                       mhT_s[0:16, gi * 512:(gi + 1) * 512],
                                       start=False, stop=True)
                    c.inc(ins, f'{tag}pe', 1)
                    c.wait(t, f'{tag}lu{par}', B[f'lu{par}'] + 256 * (fb // 2 + 1))
                    for d in range(16):
                        wt = bar[:, par * 4096 + 2048 + d * 128: par * 4096 + 2048 + (d + 1) * 128]
                        for gi in range(4):
                            ins = t.matmul(psB[:, gi * 512:(gi + 1) * 512], wt,
                                           actT[:, d * S + gi * 512: d * S + (gi + 1) * 512],
                                           start=(d == 0), stop=(d == 15))
                    c.inc(ins, f'{tag}pu', 1)
                blk.tensor(fpe)
                def fsc(s, fb=fb):
                    if fb == 0:
                        c.wait(s, f'{atag}ps', AN['ps'])
                        c.wait(s, f'{qtag}so0', QS['so0'])
                        c.wait(s, f'{qtag}so1', QS['so1'])
                    c.wait(s, f'{tag}pe', B['pe'] + fb + 1)
                    if fb >= 1:
                        c.wait(s, f'{tag}mv', B['mv'] + 4 * fb)
                    for gi in range(4):
                        c.inc(s.activation(out=wring[:, gi * 512:(gi + 1) * 512],
                                           in_=psA[:, gi * 512:(gi + 1) * 512],
                                           func=AF.Silu), f'{tag}sg', 1)
                blk.scalar(fsc)
                def fve(v, fb=fb):
                    if fb == 0:
                        c.wait(v, f'{atag}ps', AN['ps'])
                    c.wait(v, f'{tag}pu', B['pu'] + fb + 1)
                    c.wait(v, f'{tag}sg', B['sg'] + 4 * (fb + 1))
                    if fb >= 1:
                        c.wait(v, f'{tag}sm_', B['sm_'] + 16 * fb)
                    for gi in range(4):
                        c.inc(v.tensor_tensor(out=wring[:, 2048 + gi * 512: 2048 + (gi + 1) * 512],
                                              in0=psB[:, gi * 512:(gi + 1) * 512],
                                              in1=wring[:, gi * 512:(gi + 1) * 512],
                                              op=ALU.mult), f'{tag}mv', 1)
                blk.vector(fve)
                def fsy(sy, fb=fb):
                    if fb == 0 and OL:
                        c.wait(sy, 'o6lm0', OL['lm0'])
                        c.wait(sy, 'o6lm1', OL['lm1'])
                    c.wait(sy, f'{tag}mv', B['mv'] + 4 * (fb + 1))
                    c.inc(sy.dma_start(out=m_d[fb * 128:(fb + 1) * 128, :],
                                       in_=wring[:, 2048:4096]), f'{tag}sm_', 16)
                blk.sync(fsy)
            def fend(sy):
                c.wait(sy, f'{tag}sm_', B['sm_'] + 64 * 16)
                c.inc(sy.nop(), f'{tag}done', 1)
            blk.sync(fend)

        def ph_out(b, tag, mtag, z2tag, wtag, atag):
            B = {s: c.n(f'{tag}{s}') for s in
                 ('lm0', 'lm1', 'lw0', 'lw1', 'pe', 'ev', 'pt', 'ea', 'sq',
                  'lnv', 'sd', 'lnw', 'mus', 'lx', 'so', 'sob')}
            MN = {s: c.n(f'{mtag}{s}') for s in ('pe', 'pu', 'mv', 'sg', 'sm_',
                                                 'lw0', 'lw1', 'lu0', 'lu1')}
            ZN = {s: c.n(f'{z2tag}{s}') for s in ('ld0', 'ld1', 'tp', 'z1')}
            WW = {s: c.n(f'{wtag}{s}') for s in ('pe', 'lnw', 'x2', 'x2b')}
            AW = {s: c.n(f'{atag}{s}') for s in ('sm', 'mh', 'pv', 'me', 'th')}
            asm = [stage[:, 0:2048], stage[:, 2048:4096],
                   obuf[:, 0:2048], obuf[:, 2048:4096]]
            for tg in range(4):
                def flm(sy, tg=tg):
                    if tg == 0:
                        c.wait(sy, f'{mtag}pe', MN['pe'])
                        c.wait(sy, f'{mtag}pu', MN['pu'])
                        c.wait(sy, f'{tag}lm0', B['lm0'])
                        c.wait(sy, f'{tag}lm1', B['lm1'])
                        c.wait(sy, f'{mtag}sm_', MN['sm_'])
                    else:
                        c.wait(sy, f'{tag}pe', B['pe'] + 32 * tg)
                        c.wait(sy, f'{tag}lm{tg % 2}', B[f'lm{tg % 2}'] + 1024 * (tg // 2))
                    for f in range(64):
                        c.inc(sy.dma_start(
                            out=actT[:, f * 512:(f + 1) * 512],
                            in_=m_d[f * 128:(f + 1) * 128, tg * 512:(tg + 1) * 512]),
                              f'{tag}lm{tg % 2}', 16)
                blk.sync(flm)
                for dt in range(16):
                    it = tg * 16 + dt
                    def fgp(g, it=it, dt=dt, tg=tg):
                        if it == 0:
                            c.wait(g, 'prBdone', 1)
                            c.wait(g, f'{mtag}pe', MN['pe'])
                            c.wait(g, f'{mtag}pu', MN['pu'])
                            c.wait(g, f'{tag}lw0', B['lw0'])
                            c.wait(g, f'{tag}lw1', B['lw1'])
                        for fh in range(2):
                            gidx = it * 2 + fh
                            par = gidx % 2
                            if gidx >= 2:
                                c.wait(g, f'{tag}pe', B['pe'] + gidx - 1)
                                c.wait(g, f'{tag}lw{par}', B[f'lw{par}'] + 512 * (gidx // 2))
                            for f in range(fh * 32, fh * 32 + 32):
                                c.inc(g.dma_start(
                                    out=bar[:, par * 4096 + (f % 32) * 128: par * 4096 + (f % 32 + 1) * 128],
                                    in_=wdnt_d[f * 128:(f + 1) * 128, dt * 128:(dt + 1) * 128]),
                                      f'{tag}lw{par}', 16)
                    blk.gpsimd(fgp)
                    def fpe(t, it=it, dt=dt, tg=tg):
                        if it == 0:
                            c.wait(t, f'{mtag}mv', MN['mv'])
                            c.wait(t, f'{mtag}sg', MN['sg'])
                        itp = it - 4
                        if itp >= 0 and itp // 16 == it // 16:
                            self_tp(t, itp)
                        c.wait(t, f'{tag}lm{tg % 2}', B[f'lm{tg % 2}'] + 1024 * (tg // 2 + 1))
                        if it >= 4:
                            c.wait(t, f'{tag}ev', B['ev'] + it - 3)
                        sl = psA[:, (dt % 4) * 512:(dt % 4 + 1) * 512]
                        for fh in range(2):
                            gidx = it * 2 + fh
                            par = gidx % 2
                            c.wait(t, f'{tag}lw{par}', B[f'lw{par}'] + 512 * (gidx // 2 + 1))
                            for f in range(fh * 32, fh * 32 + 32):
                                ins = t.matmul(sl,
                                               bar[:, par * 4096 + (f % 32) * 128: par * 4096 + (f % 32 + 1) * 128],
                                               actT[:, f * 512:(f + 1) * 512],
                                               start=(f == 0), stop=(f == 63))
                            c.inc(ins, f'{tag}pe', 1)
                        if it % 16 == 15:
                            for itp2 in range(it - 3, it + 1):
                                self_tp(t, itp2)
                    def self_tp(t, itp):
                        c.wait(t, f'{tag}ev', B['ev'] + itp + 1)
                        dtp = itp % 16
                        for q in range(4):
                            tr = (itp * 4 + q) % 4
                            if itp * 4 + q >= 4:
                                c.wait(t, f'{tag}ea', B['ea'] + itp * 4 + q - 3)
                            ins = t.matmul(psB[:, tr * 512: tr * 512 + 128],
                                           bar[:, 10240 + (dtp % 4) * 512 + q * 128: 10240 + (dtp % 4) * 512 + (q + 1) * 128],
                                           identb[:], start=True, stop=True)
                            c.inc(ins, f'{tag}pt', 1)
                    blk.tensor(fpe)
                    def fsc(s, it=it, dt=dt):
                        if it == 0:
                            c.wait(s, f'{wtag}pe', WW['pe'])
                            c.wait(s, f'{mtag}mv', MN['mv'])
                        itl = [it - 4] if (it >= 4 and (it - 4) // 16 == it // 16) else []
                        for itp in itl:
                            dtp = itp % 16
                            tgp = itp // 16
                            for q in range(4):
                                if itp == 0 and q == 0:
                                    c.wait(s, f'{atag}sm', AW['sm'])
                                    c.wait(s, f'{atag}mh', AW['mh'])
                                    c.wait(s, f'{wtag}lnw', WW['lnw'])
                                    c.wait(s, f'{wtag}x2', WW['x2'])
                                    c.wait(s, f'{wtag}x2b', WW['x2b'])
                                    c.wait(s, f'{z2tag}ld0', ZN['ld0'])
                                    c.wait(s, f'{z2tag}ld1', ZN['ld1'])
                                    c.wait(s, f'{z2tag}z1', ZN['z1'])
                                if tgp > 0 and q == 0 and dtp == 0:
                                    c.wait(s, f'{tag}so', B['so'] + 32 * tgp)
                                    c.wait(s, f'{tag}sob', B['sob'] + 32 * tgp)
                                c.wait(s, f'{tag}pt', B['pt'] + itp * 4 + q + 1)
                                tr = (itp * 4 + q) % 4
                                c.inc(s.copy(asm[q][:, dtp * 128:(dtp + 1) * 128],
                                             psB[:, tr * 512: tr * 512 + 128]),
                                      f'{tag}ea', 1)
                        c.wait(s, f'{tag}pe', B['pe'] + 2 * (it + 1))
                        if it >= 4:
                            c.wait(s, f'{tag}pt', B['pt'] + 4 * (it - 3))
                        c.inc(s.copy(bar[:, 10240 + (dt % 4) * 512: 10240 + (dt % 4 + 1) * 512],
                                     psA[:, (dt % 4) * 512:(dt % 4 + 1) * 512]),
                              f'{tag}ev', 1)
                        if it % 16 == 15:
                            for itp in range(it - 3, it + 1):
                                dtp = itp % 16
                                for q in range(4):
                                    c.wait(s, f'{tag}pt', B['pt'] + itp * 4 + q + 1)
                                    tr = (itp * 4 + q) % 4
                                    c.inc(s.copy(asm[q][:, dtp * 128:(dtp + 1) * 128],
                                                 psB[:, tr * 512: tr * 512 + 128]),
                                          f'{tag}ea', 1)
                    blk.scalar(fsc)
                for q in range(4):
                    tt = tg * 4 + q
                    def fx(sy, tg=tg, q=q, tt=tt):
                        if tg == 0 and q < 2:
                            c.wait(sy, f'{z2tag}tp', ZN['tp'])
                            c.wait(sy, f'{z2tag}z1', ZN['z1'])
                            c.wait(sy, f'{tag}lx', B['lx'])
                        else:
                            c.wait(sy, f'{tag}lnw', c.n(f'{tag}lnw'))
                        c.inc(sy.dma_start(out=z1tok[:, (q % 2) * 2052: (q % 2) * 2052 + 2048],
                                           in_=x2_d[b * S + tt * 128:b * S + (tt + 1) * 128, :]),
                              f'{tag}lx', 16)
                    blk.sync(fx)
                    def fv1(v, tg=tg, q=q, tt=tt):
                        if tg == 0 and q == 0:
                            c.wait(v, f'{atag}me', AW['me'])
                            c.wait(v, f'{atag}th', AW['th'])
                        c.wait(v, f'{tag}ea', B['ea'] + (tg * 16 + 15) * 4 + q + 1)
                        c.wait(v, f'{tag}lx', B['lx'] + 16 * (4 * tg + q + 1))
                        mus = cols[:, 1200 + tt: 1201 + tt]
                        c.inc(v.tensor_tensor(out=mus, in0=cols[:, 32 + tt: 33 + tt],
                                              in1=cols[:, 64 + tt: 65 + tt], op=ALU.mult),
                              f'{tag}mus', 1)
                        c.wait(v, f'{tag}mus')
                        c.inc(v.scalar_tensor_tensor(out=asm[q], in0=z1tok[:, (q % 2) * 2052: (q % 2) * 2052 + 2048],
                                                     scalar=cols[:, 64 + tt: 65 + tt],
                                                     in1=asm[q], op0=ALU.mult, op1=ALU.add),
                              f'{tag}lnv', 1)
                        c.wait(v, f'{tag}lnv')
                        c.inc(v.tensor_scalar(out=asm[q], in0=asm[q], scalar1=1.0,
                                              scalar2=mus, op0=ALU.mult, op1=ALU.subtract),
                              f'{tag}lnv', 1)
                        base = 1024 + tt * 8
                        for bk in range(4):
                            c.wait(v, f'{tag}lnv')
                            c.inc(v.reduce_sum(cols[:, base + bk: base + bk + 1],
                                               asm[q][:, bk * 512:(bk + 1) * 512], axis=AXX),
                                  f'{tag}lnv', 1)
                        for st in (1, 2):
                            c.wait(v, f'{tag}lnv')
                            c.inc(v.tensor_tensor(out=cols[:, base: base + 1],
                                                  in0=cols[:, base: base + 1],
                                                  in1=cols[:, base + st: base + st + 1],
                                                  op=ALU.add), f'{tag}lnv', 1)
                    blk.vector(fv1)
                    def fs1(s, q=q, tt=tt):
                        c.wait(s, f'{tag}lnv', c.n(f'{tag}lnv'))
                        base = 1024 + tt * 8
                        for bk in range(4):
                            c.wait(s, f'{tag}sq')
                            c.inc(s.activation(out=cols[:, 1536:2048],
                                               in_=asm[q][:, bk * 512:(bk + 1) * 512],
                                               func=AF.Square,
                                               accum_out=cols[:, base + 4 + bk: base + 5 + bk]),
                                  f'{tag}sq', 1)
                    blk.scalar(fs1)
                    def fv2(v, q=q, tt=tt):
                        c.wait(v, f'{tag}sq', c.n(f'{tag}sq'))
                        base = 1024 + tt * 8
                        c.inc(v.tensor_tensor(out=cols[:, base + 3: base + 4],
                                              in0=cols[:, base: base + 1],
                                              in1=cols[:, base + 3: base + 4], op=ALU.add),
                              f'{tag}lnv', 1)
                        for bk in range(1, 4):
                            c.wait(v, f'{tag}lnv')
                            c.inc(v.tensor_tensor(out=cols[:, base + 4: base + 5],
                                                  in0=cols[:, base + 4: base + 5],
                                                  in1=cols[:, base + 4 + bk: base + 5 + bk],
                                                  op=ALU.add), f'{tag}lnv', 1)
                        mu4 = cols[:, 1264 + tt: 1265 + tt]
                        musq = cols[:, 1296 + tt: 1297 + tt]
                        var4 = cols[:, 1312 + tt: 1313 + tt]
                        c.wait(v, f'{tag}lnv')
                        c.inc(v.tensor_scalar(out=mu4, in0=cols[:, base + 3: base + 4],
                                              scalar1=1.0 / D, scalar2=None, op0=ALU.mult),
                              f'{tag}lnv', 1)
                        c.wait(v, f'{tag}lnv')
                        c.inc(v.tensor_tensor(out=musq, in0=mu4, in1=mu4, op=ALU.mult),
                              f'{tag}lnv', 1)
                        c.wait(v, f'{tag}lnv')
                        c.inc(v.scalar_tensor_tensor(out=var4, in0=cols[:, base + 4: base + 5],
                                                     scalar=1.0 / D, in1=musq,
                                                     op0=ALU.mult, op1=ALU.subtract),
                              f'{tag}lnv', 1)
                    blk.vector(fv2)
                    def fs2(s, tt=tt):
                        c.wait(s, f'{tag}lnv', c.n(f'{tag}lnv'))
                        c.inc(s.activation(out=cols[:, 1328 + tt: 1329 + tt],
                                           in_=cols[:, 1312 + tt: 1313 + tt],
                                           func=AF.Sqrt, bias=smallv[:, 40:41]),
                              f'{tag}sd', 1)
                    blk.scalar(fs2)
                    def fv3(v, q=q, tt=tt):
                        c.wait(v, f'{tag}sd', c.n(f'{tag}sd'))
                        rs4 = cols[:, 1344 + tt: 1345 + tt]
                        murs = cols[:, 1360 + tt: 1361 + tt]
                        c.inc(v.reciprocal(rs4, cols[:, 1328 + tt: 1329 + tt]), f'{tag}lnw', 1)
                        c.wait(v, f'{tag}lnw')
                        c.inc(v.tensor_tensor(out=murs, in0=cols[:, 1264 + tt: 1265 + tt],
                                              in1=rs4, op=ALU.mult), f'{tag}lnw', 1)
                        c.wait(v, f'{tag}lnw')
                        c.inc(v.tensor_scalar(out=z1tok[:, (q % 2) * 2052: (q % 2) * 2052 + 2048],
                                              in0=z1tok[:, (q % 2) * 2052: (q % 2) * 2052 + 2048],
                                              scalar1=1.0, scalar2=murs,
                                              op0=ALU.mult, op1=ALU.subtract), f'{tag}lnw', 1)
                        c.wait(v, f'{tag}lnw')
                        c.inc(v.scalar_tensor_tensor(out=asm[q], in0=asm[q], scalar=rs4,
                                                     in1=z1tok[:, (q % 2) * 2052: (q % 2) * 2052 + 2048],
                                                     op0=ALU.mult, op1=ALU.add),
                              f'{tag}lnw', 1)
                    blk.vector(fv3)
                    def fst(sy, q=q, tt=tt):
                        c.wait(sy, f'{tag}lnw', c.n(f'{tag}lnw'))
                        nm = 'so' if q % 2 == 0 else 'sob'
                        c.inc(sy.dma_start(out=out[b * S + tt * 128: b * S + (tt + 1) * 128, :],
                                           in_=asm[q]), f'{tag}{nm}', 16)
                    blk.sync(fst)
            def fend(sy):
                c.wait(sy, f'{tag}so', B['so'] + 8 * 16)
                c.wait(sy, f'{tag}sob', B['sob'] + 8 * 16)
                c.inc(sy.nop(), f'{tag}done', 1)
            blk.sync(fend)

        ph_const()
        if DEV_PHASES >= 6:
            ph_prepA('prA')
            ph_prepB('prB')
        last_done = {1: 'p2', 2: 'p2', 3: 'c3', 4: 'r4', 5: 'd5', 6: 'w6',
                     7: 'q5', 8: 'a5', 9: 'm6', 10: 'o6'}
        for b in range(DEV_NB):
            lp = last_done.get(min(DEV_PHASES, max(last_done)), 'p2')
            prev = f'{lp}done' if b > 0 else ('prBdone' if DEV_PHASES >= 6 else None)
            if DEV_PHASES >= 1:
                ph_p1(b, 'p1', xin[b * S:(b + 1) * S, :], prev)
            if DEV_PHASES >= 2:
                ph_p2(b, 'p2', 'p1')
            if DEV_PHASES >= 3:
                ph_conv(b, 'c3', 'p2')
            if DEV_PHASES >= 4:
                ph_rows(b, 'r4', 'c3')
            if DEV_PHASES >= 5:
                ph_delta(b, 'd5', 'r4')
            if DEV_PHASES >= 6:
                ph_wout(b, 'w6', 'd5')
            if DEV_PHASES >= 7:
                ph_p1(b, 'z2', x2_d[b * S:(b + 1) * S, :], 'w6done', mode='z2')
                ph_qg(b, 'q5', 'z2', 'w6')
            if DEV_PHASES >= 8:
                ph_attn(b, 'a5', 'q5', 'z2', 'w6')
                XB['a5'] = {'pt': c.n('a5pt'), 'lq0': c.n('a5lq0'),
                            'lq1': c.n('a5lq1')}
            if DEV_PHASES >= 9:
                ph_mlp(b, 'm6', 'a5', 'q5')
            if DEV_PHASES >= 10:
                ph_out(b, 'o6', 'm6', 'z2', 'w6', 'a5')
                XB['o6'] = {'lm0': c.n('o6lm0'), 'lm1': c.n('o6lm1')}
        def fin(sy):
            for nm in ('prAdone', 'prBdone', 'p2done', 'c3done', 'r4done', 'd5done',
                       'w6done', 'q5done', 'a5done', 'm6done', 'o6done'):
                if c.n(nm):
                    c.wait(sy, nm)
        blk.sync(fin)
    return nc, scratch


def _consts():
    ident = np.eye(128, dtype=np.float32)
    sel = np.zeros((48, 4096), np.float32)
    for g in range(32):
        sel[g, g * 128:(g + 1) * 128] = 1.0
        sel[32 + g // 2, g * 128:(g + 1) * 128] = 1.0
    tl = np.tril(np.ones((128, 128), np.float32), -1)
    mA = np.where(tl > 0, np.float32(0.0), np.float32(-1e30))
    mB = np.where(tl.T > 0, np.float32(0.0), np.float32(-1e30))
    tu = np.triu(np.ones((128, 128), np.float32), 0)
    mW = np.where(tu > 0, np.float32(-0.5 * np.log(128.0)), np.float32(-1e30))
    mU = np.where(np.triu(np.ones((128, 128), np.float32), 0) > 0,
                  np.float32(0.0), np.float32(-1e30))
    return dict(c_identb=ident.astype(bfnp), c_identf=ident,
                c_maskUb=mU.astype(bfnp),
                c_onesrow=np.ones((1, 128), np.float32),
                c_sel=sel, c_ones=np.ones((128, 1), bfnp),
                c_maskA=mA.astype(np.float32), c_maskB=mB.astype(np.float32),
                c_maskW=mW.astype(np.float32))


def _in_map(core, inputs):
    x = inputs['L0_attn_hidden_states']
    m = dict(
        xin=np.ascontiguousarray(x[NB * core:NB * core + NB].reshape(NB * S, D)),
        w_qkvz=inputs['L0_attn_w_qkvz'],
        w_ba=inputs['L0_attn_w_ba'],
        convw=np.ascontiguousarray(inputs['L0_attn_conv_w'].reshape(8192, 4)),
        w_out=inputs['L0_attn_w_out'],
        Wq=inputs['L0_ffn_Wq'], Wk=inputs['L0_ffn_Wk'], Wv=inputs['L0_ffn_Wv'],
        Wo=inputs['L0_ffn_Wo'],
        Amat=inputs['L0_ffn_A'], Bmat=inputs['L0_ffn_B'],
        Win=inputs['L0_ffn_W_in_base'], Wup=inputs['L0_ffn_W_up_base'],
        Wdn=inputs['L0_ffn_W_down'],
        dtb_col=np.ascontiguousarray(inputs['L0_attn_dt_bias'].reshape(32, 1)),
        alog_col=np.ascontiguousarray(inputs['L0_attn_A_log'].reshape(32, 1)),
        normw_col=np.ascontiguousarray(inputs['L0_attn_norm_w'].reshape(128, 1)),
        ln1w_col=np.ascontiguousarray(inputs['L0_ffn_ln1_w'].reshape(16, 128).T),
        ln2w_col=np.ascontiguousarray(inputs['L0_ffn_ln2_w'].reshape(16, 128).T),
        ln1w_row=np.ascontiguousarray(inputs['L0_ffn_ln1_w'].reshape(1, D)),
    )
    m.update(_consts())
    return m


def kernel(**inputs):
    inputs = {k: np.asarray(v) for k, v in inputs.items()}
    nc, _ = build()
    in_maps = [_in_map(cr, inputs) for cr in range(DEV_CORES)]
    res = run_bass_kernel_spmd(nc, in_maps, core_ids=list(range(DEV_CORES)))
    outs = [np.asarray(res.results[cr]['out']).reshape(NB, S, D)
            for cr in range(DEV_CORES)]
    return np.concatenate(outs, axis=0)


# version 02
# speedup vs baseline: 1.0209x; 1.0209x over previous
import contextlib
import numpy as np
import ml_dtypes
import concourse.bass as bass
import concourse.mybir as mybir
from concourse.bass_utils import run_bass_kernel_spmd

F32 = mybir.dt.float32
BF16 = mybir.dt.bfloat16
AF = mybir.ActivationFunctionType
ALU = mybir.AluOpType
AXX = mybir.AxisListType.X
bfnp = ml_dtypes.bfloat16

S = 2048
D = 2048
NB = 2
NKH, NVH, DK, DV = 16, 32, 128, 128
NFO = 96
NDT = 16
NC_ = 16
FF = 8192

DEV_PHASES = 999
DEV_NB = NB
DEV_DUMP = ()

SEM_MAP = {}
def _smap(prefix, pairs):
    for a, b in pairs.items():
        SEM_MAP[prefix + a] = b
_smap('p1', dict(ld0='D0', ld1='D1', z1='V0', cA='V1', cB='V2', sq='S0',
                 sd='S1', tp='T0', tev='S2', done='done'))
_smap('p2', dict(lw0='GD0', lw1='GD1', pe='T1', evS='S3', evV='V3',
                 so0='D2', so1='D3', done='done'))
_smap('c3', dict(ld0='D0', ld1='D1', lw0='GD0', lw1='GD1', pz='G0', v='V0',
                 s='S0', sq='S1', ssmm='T0', ssev='V1', so0='D2', so1='D3',
                 sx0='D4', sx1='D5', done='done'))
_smap('r4', dict(ld='D0', s='S0', vx='V0', vb='V1', vu='V2', sr='D1',
                 ebm='T0', eb='S1', ebr='S2', ls='D2', s2='S3', v2='V3',
                 sk='D3', tp='T1', tev='S4', vc='V4', se='S5', vn='V5',
                 done='done'))
_smap('d5', dict(dl0='D0', dl1='D1', dg0='GD0', dg1='GD1', sz='G0', pga='T0',
                 evS='S0', evV='V0', pgb='T1', vm='V1', sm='S1', sq='S2',
                 ps='T2', po='T3', vo='V2', so='S3', vy='V3', py='T4', sy='S4',
                 vs='V4', sd0='D2', sd1='D3', dbg='D6', done='done'))
_smap('w6', dict(lx='D0', x2b='D3', lw0='GD0', lw1='GD1', lw2='GD2', lw3='GD3',
                 ly0='D4', ly1='D5', ly2='D7', ly3='D8', ly4='D9', ly5='D10',
                 ly6='D11', ly7='D12', pe='T0', sq='S0', lnv='V0', sd='S1',
                 lnw='V1', x2='D2', done='done'))
_smap('prA', dict(l0='D0', l1='D1', v='V0', s0='D2', s1='D3', done='done'))
_smap('z2', dict(ld0='D0', ld1='D1', z1='V0', cA='V1', cB='V2', sq='S0',
                 sd='S1', tp='T0', tev='S2', done='done'))
_smap('q5', dict(lg0='D0', lg1='D1', lv0='D4', lv1='D5', pe='T1', pv='T2',
                 evS='S3', evV='V3', eva='S4', so0='D2', so1='D3', dbg='D6',
                 on='G0', done='done'))
_smap('a5', dict(lq0='D0', lq1='D1', lq2='D4', lq3='D5', ps='T0', pt='T2',
                 pm='T3', ph='T4', pmh='T1', ex='S0', pv='S1', th='S2', me='S3',
                 mx='V0', sm='V1', mh='V2', dbg='D6', done='done'))
_smap('m6', dict(lw0='GD0', lw1='GD1', lu0='GD2', lu1='GD3', lb='D0',
                 pe='T0', pu='T1', sg='S0', mv='V0', sm_='D1', done='done'))
_smap('o6', dict(lm0='D0', lm1='D1', lm2='D4', lm3='D5', lw0='GD0', lw1='GD1',
                 lx='D7', pe='T0', sq='S0', lnv='V0', sd='S1', lnw='V1',
                 so='D2', sob='D3', ev='S2', ea='S3', pt='T1', mus='V2',
                 done='done'))
_smap('prB', dict(lq0='GD0', lq1='GD1', lk0='GD2', lk1='GD3', pe='T0', ev='V0', sg='D2',
                  lo0='GD0', lo1='GD1', lv0='GD0', lv1='GD1', sv0='D4', sv1='D6',
                  l60='GD0', l61='GD1', l62='GD2', l63='GD3',
                  zzz='D0',
                  lo='GD0', la='GD1', pw='T1', evw='S0', sw='D3',
                  lv='GD2', pv='T2', evv='S1', sv='D4',
                  lA='GD3', vA='V1', pt='T3', evt='S2', st0='D5', st1='D7', lt='GD2',
                  bc='T4', bev='V2', done='done'))

DEV_CORES = 8
DEV_DBG = False
DEV_DELTA_LIMIT = 256


class Ctx:
    def __init__(self, nc, st):
        self.nc, self.st = nc, st
        self.sems, self.cnt = {}, {}
        self.last_wait = {}

    def sem(self, name):
        name = SEM_MAP.get(name, name)
        if name not in self.sems:
            self.sems[name] = self.st.enter_context(self.nc.semaphore(name))
            self.cnt[name] = 0
        return self.sems[name]

    def inc(self, ins, name, k=1):
        name = SEM_MAP.get(name, name)
        sem = self.sem(name)
        self.cnt[name] = self.cnt.get(name, 0) + k
        ins.then_inc(sem, k)
        return self.cnt[name]

    def wait(self, eng, name, val=None):
        name = SEM_MAP.get(name, name)
        if val is None:
            val = self.cnt.get(name, 0)
        if val > 0:
            key = (id(eng), name)
            if val <= self.last_wait.get(key, 0):
                return
            self.last_wait[key] = val
            eng.wait_ge(self.sem(name), val)

    def n(self, name):
        name = SEM_MAP.get(name, name)
        return self.cnt.get(name, 0)


def build():
    nc = bass.Bass("TRN2", target_bir_lowering=False)
    dt = nc.dram_tensor
    xin = dt("xin", [NB * S, D], F32, kind="ExternalInput").ap()
    w_qkvz = dt("w_qkvz", [D, 12288], F32, kind="ExternalInput").ap()
    w_ba = dt("w_ba", [D, 64], F32, kind="ExternalInput").ap()
    convw = dt("convw", [8192, 4], F32, kind="ExternalInput").ap()
    w_out = dt("w_out", [4096, D], F32, kind="ExternalInput").ap()
    Wq = dt("Wq", [D, D], F32, kind="ExternalInput").ap()
    Wk = dt("Wk", [D, D], F32, kind="ExternalInput").ap()
    Wv = dt("Wv", [D, D], F32, kind="ExternalInput").ap()
    Wo = dt("Wo", [D, D], F32, kind="ExternalInput").ap()
    Amat = dt("Amat", [D, 16], F32, kind="ExternalInput").ap()
    Bmat = dt("Bmat", [16, FF], F32, kind="ExternalInput").ap()
    Win = dt("Win", [FF, D], F32, kind="ExternalInput").ap()
    Wup = dt("Wup", [FF, D], F32, kind="ExternalInput").ap()
    Wdn = dt("Wdn", [D, FF], F32, kind="ExternalInput").ap()
    dtb_col = dt("dtb_col", [32, 1], F32, kind="ExternalInput").ap()
    alog_col = dt("alog_col", [32, 1], F32, kind="ExternalInput").ap()
    normw_col = dt("normw_col", [128, 1], F32, kind="ExternalInput").ap()
    ln1w_col = dt("ln1w_col", [128, 16], F32, kind="ExternalInput").ap()
    ln2w_col = dt("ln2w_col", [128, 16], F32, kind="ExternalInput").ap()
    ln1w_row = dt("ln1w_row", [1, D], F32, kind="ExternalInput").ap()
    c_identb = dt("c_identb", [128, 128], BF16, kind="ExternalInput").ap()
    c_identf = dt("c_identf", [128, 128], F32, kind="ExternalInput").ap()
    c_sel = dt("c_sel", [48, 4096], F32, kind="ExternalInput").ap()
    c_ones = dt("c_ones", [128, 1], BF16, kind="ExternalInput").ap()
    c_onesrow = dt("c_onesrow", [1, 128], F32, kind="ExternalInput").ap()
    c_maskA = dt("c_maskA", [128, 128], F32, kind="ExternalInput").ap()
    c_maskB = dt("c_maskB", [128, 128], F32, kind="ExternalInput").ap()
    c_maskW = dt("c_maskW", [128, 128], F32, kind="ExternalInput").ap()
    c_maskUb = dt("c_maskUb", [128, 128], BF16, kind="ExternalInput").ap()
    out = dt("out", [NB * S, D], F32, kind="ExternalOutput").ap()
    scratch = {}
    def scr(name, shape, dtyp):
        kind = "ExternalOutput" if name in DEV_DUMP else "Internal"
        scratch[name] = dt(name, shape, dtyp, kind=kind).ap()
        return scratch[name]
    qkvzT = scr("qkvzT", [12288, S], F32)
    ba_d = scr("ba_d", [64, S], F32)
    convq = scr("convq", [D, S], BF16)
    convk = scr("convk", [D, S], BF16)
    convv = scr("convv", [4096, S], F32)
    rows_d = scr("rows_d", [32, 3 * S], F32)
    rowsK_d = scr("rowsK_d", [16, 3 * S], F32)
    ssk_d = scr("ssk_d", [16, S], F32)
    ssq_d = scr("ssq_d", [16, S], F32)
    yT_d = scr("yT_d", [4096, S], BF16)
    x2_d = scr("x2_d", [NB * S, D], F32)
    qgt_d = scr("qgt_d", [D, S], BF16)
    m_d = scr("m_d", [FF, S], BF16)
    g_d = scr("g_d", [D, D], BF16)
    va_d = scr("va_d", [D, 256], BF16)
    wint_d = scr("wint_d", [D, FF], BF16)
    wupt_d = scr("wupt_d", [D, FF], BF16)
    wdnt_d = scr("wdnt_d", [FF, D], BF16)
    woute_d = scr("woute_d", [4096, D], BF16)
    dbgf = scr("dbgf", [128, 4096], F32)
    dbgb = scr("dbgb", [128, 4096], BF16)

    with contextlib.ExitStack() as _st:
        E = _st.enter_context
        c = Ctx(nc, _st)
        actT = E(nc.sbuf_tensor("actT", [128, NDT * S], BF16))
        wring = E(nc.sbuf_tensor("wring", [128, 2 * NDT * 128], BF16))
        obuf = E(nc.sbuf_tensor("obuf", [128, 2 * S], F32))
        stage = E(nc.sbuf_tensor("stage", [128, 2 * S], F32))
        z1tok = E(nc.sbuf_tensor("z1tok", [128, 4104], F32))
        bar = E(nc.sbuf_tensor("bar", [128, 12288], BF16))
        beta32 = E(nc.sbuf_tensor("beta32", [32, S], F32))
        cw = E(nc.sbuf_tensor("cw", [128, 8], F32))
        cols = E(nc.sbuf_tensor("cols", [128, 2048], F32))
        identb = E(nc.sbuf_tensor("identb", [128, 128], BF16))
        identf = E(nc.sbuf_tensor("identf", [128, 128], F32))
        selS = E(nc.sbuf_tensor("selS", [48, 4096], F32))
        onesS = E(nc.sbuf_tensor("onesS", [128, 1], BF16))
        onesrowS = E(nc.sbuf_tensor("onesrowS", [1, 128], F32))
        lnrow1 = E(nc.sbuf_tensor("lnrow1", [1, D], F32))
        maskA = E(nc.sbuf_tensor("maskA", [128, 128], F32))
        maskB = E(nc.sbuf_tensor("maskB", [128, 128], F32))
        maskW = E(nc.sbuf_tensor("maskW", [128, 128], F32))
        maskUbS = E(nc.sbuf_tensor("maskUbS", [128, 128], BF16))
        mhT_s = E(nc.sbuf_tensor("mhT_s", [16, 2048], BF16))
        Bs = E(nc.sbuf_tensor("Bs", [16, FF], BF16))
        smallv = E(nc.sbuf_tensor("smallv", [128, 64], F32))
        psA = E(nc.psum_tensor("psA", [128, 2048], F32))
        psB = E(nc.psum_tensor("psB", [128, 2048], F32))

        blk = E(nc.Block())
        XB = {}

        def ph_const():
            def fsync(sy):
                for dst, src in [(identb[:], c_identb), (identf[:], c_identf),
                                 (selS[:], c_sel), (onesS[:], c_ones),
                                 (maskA[:], c_maskA), (maskB[:], c_maskB),
                                 (maskW[:], c_maskW),
                                 (maskUbS[:], c_maskUb),
                                 (smallv[0:32, 0:1], dtb_col),
                                 (smallv[0:32, 1:2], alog_col),
                                 (smallv[:, 2:3], normw_col),
                                 (smallv[:, 4:20], ln1w_col),
                                 (smallv[:, 20:36], ln2w_col),
                                 (onesrowS[:], c_onesrow),
                                 (lnrow1[:], ln1w_row)]:
                    c.inc(sy.dma_start(out=dst, in_=src), 'cst', 16)
            blk.sync(fsync)
            def fscal(s):
                c.wait(s, 'cst')
                c.inc(s.activation(out=smallv[0:32, 38:39], in_=smallv[0:32, 1:2],
                                   func=AF.Exp), 'cstS')
            blk.scalar(fscal)
            def fvec(v):
                c.wait(v, 'cstS')
                c.inc(v.tensor_scalar(out=smallv[0:32, 3:4], in0=smallv[0:32, 38:39],
                                      scalar1=-1.0, scalar2=None, op0=ALU.mult),
                      'cstV')
                c.inc(v.memset(smallv[:, 40:41], 1e-5), 'cstV')
                c.inc(v.memset(smallv[:, 41:42], 1e-6), 'cstV')
                c.inc(v.memset(smallv[:, 42:43], 128.0 * 1e-6), 'cstV')
                c.inc(v.memset(smallv[:, 43:44], 1.0), 'cstV')
                c.inc(v.memset(smallv[:, 45:46], float(1.0 / np.sqrt(128.0))), 'cstV')
                c.inc(v.tensor_scalar(out=smallv[:, 44:45], in0=smallv[:, 2:3],
                                      scalar1=float(np.sqrt(128.0)), scalar2=None,
                                      op0=ALU.mult), 'cstV')
            blk.vector(fvec)
            def fgp(g):
                c.inc(g.dma_start(out=Bs[0:16, :], in_=Bmat), 'cstG', 16)
            blk.gpsimd(fgp)

        def ph_p1(b, tag, src_ap, prev_done, mode='ln'):
            B = {s: c.n(f'{tag}{s}') for s in ('ld0', 'ld1', 'z1', 'tp', 'tev',
                                               'sq', 'cA', 'sd', 'cB')}
            ncB = 2 if mode == 'ln' else 6
            nsd = 1 if mode == 'ln' else 2
            for i in range(16):
                sl = i % 2
                def fsync(sy, i=i, sl=sl):
                    if i == 0 and prev_done:
                        c.wait(sy, prev_done)
                    if i < 2:
                        c.wait(sy, f'{tag}ld{sl}', B[f'ld{sl}'])
                    if i >= 2:
                        c.wait(sy, f'{tag}z1', B['z1'] + i - 1)
                        c.wait(sy, f'{tag}cA', B['cA'] + 4 * (i - 1))
                        c.wait(sy, f'{tag}sq', B['sq'] + i - 1)
                    c.inc(sy.dma_start(out=stage[:, sl * S:(sl + 1) * S],
                                       in_=src_ap[i * 128:(i + 1) * 128, :]),
                          f'{tag}ld{sl}', 16)
                blk.sync(fsync)
                def fscal(s, i=i, sl=sl):
                    c.wait(s, f'{tag}ld{sl}', B[f'ld{sl}'] + 16 * (i // 2 + 1))
                    if i >= 2:
                        c.wait(s, f'{tag}tp', B['tp'] + 16 * (i - 1))
                    c.inc(s.activation(out=z1tok[:, sl * S:(sl + 1) * S],
                                       in_=stage[:, sl * S:(sl + 1) * S],
                                       func=AF.Square,
                                       accum_out=cols[:, 16 + i:17 + i]), f'{tag}sq', 1)
                blk.scalar(fscal)
                def fvec(v, i=i, sl=sl):
                    c.wait(v, f'{tag}ld{sl}', B[f'ld{sl}'] + 16 * (i // 2 + 1))
                    c.inc(v.reduce_sum(cols[:, i:i + 1], stage[:, sl * S:(sl + 1) * S],
                                       axis=AXX), f'{tag}cA', 1)
                    c.wait(v, f'{tag}cA')
                    c.wait(v, f'{tag}sq', B['sq'] + i + 1)
                    mu = cols[:, 32 + i:33 + i]
                    var = cols[:, 48 + i:49 + i]
                    musq = cols[:, 96:97]
                    c.inc(v.tensor_scalar(out=mu, in0=cols[:, i:i + 1], scalar1=1.0 / D,
                                          scalar2=None, op0=ALU.mult), f'{tag}cA', 1)
                    c.wait(v, f'{tag}cA')
                    c.inc(v.tensor_tensor(out=musq, in0=mu, in1=mu, op=ALU.mult),
                          f'{tag}cA', 1)
                    c.wait(v, f'{tag}cA')
                    c.inc(v.scalar_tensor_tensor(out=var, in0=cols[:, 16 + i:17 + i],
                                                 scalar=1.0 / D, in1=musq,
                                                 op0=ALU.mult, op1=ALU.subtract),
                          f'{tag}cA', 1)
                blk.vector(fvec)
                def fscal2(s, i=i):
                    if i == 0:
                        c.wait(s, 'cstV')
                    c.wait(s, f'{tag}cA', B['cA'] + 4 * (i + 1))
                    if i > 0:
                        c.wait(s, f'{tag}cB', B['cB'] + ncB * (i - 1) + 1)
                    c.inc(s.activation(out=cols[:, 97:98], in_=cols[:, 48 + i:49 + i],
                                       func=AF.Sqrt, bias=smallv[:, 40:41]), f'{tag}sd', 1)
                    if mode == 'z2':
                        if i > 0:
                            c.wait(s, f'{tag}cB', B['cB'] + ncB * (i - 1) + 4)
                        c.wait(s, f'{tag}cB', B['cB'] + ncB * i + 3)
                        c.inc(s.activation(out=cols[:, 100:101], in_=cols[:, 99:100],
                                           func=AF.Sqrt, bias=smallv[:, 41:42]),
                              f'{tag}sd', 1)
                blk.scalar(fscal2)
                def fvec2(v, i=i, sl=sl):
                    c.wait(v, f'{tag}sd', B['sd'] + nsd * i + 1)
                    mu = cols[:, 32 + i:33 + i]
                    rs = cols[:, 64 + i:65 + i]
                    ms = cols[:, 80 + i:81 + i]
                    c.inc(v.reciprocal(rs, cols[:, 97:98]), f'{tag}cB', 1)
                    c.wait(v, f'{tag}cB')
                    if mode == 'z2':
                        var = cols[:, 48 + i:49 + i]
                        c.inc(v.tensor_tensor(out=cols[:, 98:99], in0=var, in1=rs,
                                              op=ALU.mult), f'{tag}cB', 1)
                        c.wait(v, f'{tag}cB')
                        c.inc(v.tensor_tensor(out=cols[:, 99:100], in0=cols[:, 98:99],
                                              in1=rs, op=ALU.mult), f'{tag}cB', 1)
                        c.wait(v, f'{tag}cB')
                        c.wait(v, f'{tag}sd', B['sd'] + nsd * i + 2)
                        c.inc(v.reciprocal(cols[:, 101:102], cols[:, 100:101]),
                              f'{tag}cB', 1)
                        c.wait(v, f'{tag}cB')
                        c.inc(v.tensor_tensor(out=cols[:, 102:103], in0=rs,
                                              in1=cols[:, 101:102], op=ALU.mult),
                              f'{tag}cB', 1)
                        c.wait(v, f'{tag}cB')
                        sa = cols[:, 102:103]
                        c.inc(v.tensor_tensor(out=ms, in0=mu, in1=sa, op=ALU.mult),
                              f'{tag}cB', 1)
                        c.wait(v, f'{tag}cB')
                        c.inc(v.tensor_scalar(out=z1tok[:, sl * S:(sl + 1) * S],
                                              in0=stage[:, sl * S:(sl + 1) * S],
                                              scalar1=sa, scalar2=ms,
                                              op0=ALU.mult, op1=ALU.subtract),
                              f'{tag}z1', 1)
                    else:
                        c.inc(v.tensor_tensor(out=ms, in0=mu, in1=rs, op=ALU.mult),
                              f'{tag}cB', 1)
                        c.wait(v, f'{tag}cB')
                        c.inc(v.tensor_scalar(out=z1tok[:, sl * S:(sl + 1) * S],
                                              in0=stage[:, sl * S:(sl + 1) * S],
                                              scalar1=rs, scalar2=ms,
                                              op0=ALU.mult, op1=ALU.subtract),
                              f'{tag}z1', 1)
                blk.vector(fvec2)
                def ftens(t, i=i, sl=sl):
                    if i == 0:
                        c.wait(t, 'cst')
                        if prev_done:
                            c.wait(t, prev_done)
                    c.wait(t, f'{tag}z1', B['z1'] + i + 1)
                    for j in range(16):
                        k = i * 16 + j
                        if k >= 4:
                            c.wait(t, f'{tag}tev', B['tev'] + k - 3)
                        c.inc(t.transpose(psA[:, (k % 4) * 512:(k % 4) * 512 + 128],
                                          z1tok[:, sl * S + j * 128: sl * S + (j + 1) * 128],
                                          identf[:]), f'{tag}tp', 1)
                blk.tensor(ftens)
                def fscal3(s, i=i):
                    for j in range(16):
                        k = i * 16 + j
                        c.wait(s, f'{tag}tp', B['tp'] + k + 1)
                        c.inc(s.copy(actT[:, j * S + i * 128: j * S + (i + 1) * 128],
                                     psA[:, (k % 4) * 512:(k % 4) * 512 + 128]),
                              f'{tag}tev', 1)
                blk.scalar(fscal3)

        def ph_p2(b, tag, p1tag):
            tev = c.n(f'{p1tag}tev')
            B = {s: c.n(f'{tag}{s}') for s in ('pe', 'lw0', 'lw1', 'evS', 'evV',
                                               'so0', 'so1')}
            NBLK = NFO + 1
            def fgp(g):
                for fo in range(NBLK):
                    par = fo % 2
                    if fo < 2:
                        c.wait(g, f'{tag}lw{par}', B[f'lw{par}'])
                    if fo >= 2:
                        c.wait(g, f'{tag}pe', B['pe'] + (fo - 1) * NDT)
                    nco = 64 if fo == NFO else 128
                    for d in range(NDT):
                        src = (w_ba[d * 128:(d + 1) * 128, :] if fo == NFO else
                               w_qkvz[d * 128:(d + 1) * 128, fo * 128:(fo + 1) * 128])
                        c.inc(g.dma_start(
                            out=wring[:, (par * NDT + d) * 128:(par * NDT + d) * 128 + nco],
                            in_=src), f'{tag}lw{par}', 16)
            blk.gpsimd(fgp)
            def ftens(t):
                c.wait(t, f'{p1tag}tev', tev)
                for fo in range(NBLK):
                    par = fo % 2
                    ps = psA if par == 0 else psB
                    if fo >= 2:
                        c.wait(t, (f'{tag}evS' if par == 0 else f'{tag}evV'),
                               B['evS' if par == 0 else 'evV'] + 4 * ((fo - 2) // 2 + 1))
                    c.wait(t, f'{tag}lw{par}', B[f'lw{par}'] + (fo // 2 + 1) * 16 * NDT)
                    nco = 64 if fo == NFO else 128
                    for d in range(NDT):
                        wt = wring[:, (par * NDT + d) * 128:(par * NDT + d) * 128 + nco]
                        for gi in range(4):
                            ins = t.matmul(ps[0:nco, gi * 512:(gi + 1) * 512], wt,
                                           actT[:, d * S + gi * 512: d * S + (gi + 1) * 512],
                                           start=(d == 0), stop=(d == NDT - 1))
                        c.inc(ins, f'{tag}pe', 1)
            blk.tensor(ftens)
            def fscal(s):
                for fo in range(0, NBLK, 2):
                    c.wait(s, f'{tag}pe', B['pe'] + (fo + 1) * NDT)
                    if fo >= 2:
                        c.wait(s, f'{tag}so0', B['so0'] + 16 * ((fo - 2) // 2 + 1))
                    nco = 64 if fo == NFO else 128
                    for gi in range(4):
                        c.inc(s.copy(obuf[0:nco, gi * 512:(gi + 1) * 512],
                                     psA[0:nco, gi * 512:(gi + 1) * 512]), f'{tag}evS', 1)
            blk.scalar(fscal)
            def fvec(v):
                for fo in range(1, NBLK, 2):
                    c.wait(v, f'{tag}pe', B['pe'] + (fo + 1) * NDT)
                    if fo >= 2:
                        c.wait(v, f'{tag}so1', B['so1'] + 16 * ((fo - 2) // 2 + 1))
                    for gi in range(4):
                        c.inc(v.tensor_copy(obuf[:, S + gi * 512: S + (gi + 1) * 512],
                                            psB[:, gi * 512:(gi + 1) * 512]), f'{tag}evV', 1)
            blk.vector(fvec)
            def fsync(sy):
                for fo in range(NBLK):
                    par = fo % 2
                    c.wait(sy, (f'{tag}evS' if par == 0 else f'{tag}evV'),
                           B['evS' if par == 0 else 'evV'] + 4 * (fo // 2 + 1))
                    if fo == NFO:
                        dst, srcb = ba_d, obuf[0:64, par * S:(par + 1) * S]
                    else:
                        dst = qkvzT[fo * 128:(fo + 1) * 128, :]
                        srcb = obuf[:, par * S:(par + 1) * S]
                    c.inc(sy.dma_start(out=dst, in_=srcb), f'{tag}so{par}', 16)
                c.wait(sy, f'{tag}so0')
                c.wait(sy, f'{tag}so1')
                c.inc(sy.nop(), f'{tag}done', 1)
            blk.sync(fsync)


        def ph_conv(b, tag, p2tag):
            B = {s: c.n(f'{tag}{s}') for s in ('v', 'ld0', 'ld1', 'lw0', 'lw1',
                                               'pz', 's', 'sq', 'ssmm', 'ssev')}
            blocks = []
            for h in range(16):
                blocks.append((6 * h + 0, 128 * h, 'q', h))
                blocks.append((6 * h + 1, 2048 + 128 * h, 'k', h))
                blocks.append((6 * h + 2, 4096 + 256 * h, 'v', 2 * h))
                blocks.append((6 * h + 3, 4096 + 256 * h + 128, 'v', 2 * h + 1))
            nq = 0
            def cpad(sl):
                return z1tok[:, sl * 2052: sl * 2052 + 2052]
            for bi, (fo, wofs, kind, idx) in enumerate(blocks):
                sl = bi % 2
                def fgp(g, bi=bi, sl=sl, wofs=wofs):
                    if bi < 2:
                        c.wait(g, f'{p2tag}done')
                        c.wait(g, f'{tag}lw{sl}', B[f'lw{sl}'])
                        c.inc(g.memset(cpad(sl)[:, 0:4], 0.0), f'{tag}pz', 1)
                    if bi >= 2:
                        c.wait(g, f'{tag}v', B['v'] + 4 * (bi - 1))
                    c.inc(g.dma_start(out=cw[:, sl * 4:(sl + 1) * 4],
                                      in_=convw[wofs:wofs + 128, :]), f'{tag}lw{sl}', 16)
                blk.gpsimd(fgp)
                def fsync(sy, bi=bi, sl=sl, fo=fo):
                    if bi == 0:
                        c.wait(sy, f'{p2tag}done')
                    if bi < 2:
                        c.wait(sy, f'{tag}ld{sl}', B[f'ld{sl}'])
                    if bi >= 2:
                        c.wait(sy, f'{tag}v', B['v'] + 4 * (bi - 1))
                    c.inc(sy.dma_start(out=cpad(sl)[:, 4:2052],
                                       in_=qkvzT[fo * 128:(fo + 1) * 128, :]),
                          f'{tag}ld{sl}', 16)
                blk.sync(fsync)
                def fvec(v, bi=bi, sl=sl, kind=kind):
                    c.wait(v, f'{tag}ld{sl}', B[f'ld{sl}'] + 16 * (bi // 2 + 1))
                    c.wait(v, f'{tag}lw{sl}', B[f'lw{sl}'] + 16 * (bi // 2 + 1))
                    if bi < 2:
                        c.wait(v, f'{tag}pz', B['pz'] + min(bi + 1, 2))
                    if bi >= 2:
                        c.wait(v, f'{tag}s', c.n(f'{tag}s'))
                        c.wait(v, f'{tag}so{sl}', c.n(f'{tag}so{sl}'))
                        c.wait(v, f'{tag}sx{sl}', c.n(f'{tag}sx{sl}'))
                    accA = obuf[:, sl * S:(sl + 1) * S]
                    accB = stage[:, sl * S:(sl + 1) * S]
                    pj = cpad(sl)
                    c.inc(v.tensor_scalar(out=accA, in0=pj[:, 1:2049],
                                          scalar1=cw[:, sl * 4 + 0: sl * 4 + 1],
                                          scalar2=None, op0=ALU.mult), f'{tag}v', 1)
                    c.wait(v, f'{tag}v')
                    c.inc(v.scalar_tensor_tensor(out=accB, in0=pj[:, 2:2050],
                                                 scalar=cw[:, sl * 4 + 1: sl * 4 + 2],
                                                 in1=accA, op0=ALU.mult, op1=ALU.add),
                          f'{tag}v', 1)
                    c.wait(v, f'{tag}v')
                    c.inc(v.scalar_tensor_tensor(out=accA, in0=pj[:, 3:2051],
                                                 scalar=cw[:, sl * 4 + 2: sl * 4 + 3],
                                                 in1=accB, op0=ALU.mult, op1=ALU.add),
                          f'{tag}v', 1)
                    c.wait(v, f'{tag}v')
                    c.inc(v.scalar_tensor_tensor(out=accB, in0=pj[:, 4:2052],
                                                 scalar=cw[:, sl * 4 + 3: sl * 4 + 4],
                                                 in1=accA, op0=ALU.mult, op1=ALU.add),
                          f'{tag}v', 1)
                blk.vector(fvec)
                def fscal(s, bi=bi, sl=sl, kind=kind, nq=nq):
                    c.wait(s, f'{tag}v', B['v'] + 4 * (bi + 1))
                    accB = stage[:, sl * S:(sl + 1) * S]
                    if kind == 'v':
                        dst = obuf[:, sl * S:(sl + 1) * S]
                        c.inc(s.activation(out=dst, in_=accB, func=AF.Silu),
                              f'{tag}s', 1)
                    else:
                        dst = bar[:, sl * S:(sl + 1) * S]
                        c.inc(s.activation(out=dst, in_=accB, func=AF.Silu),
                              f'{tag}s', 1)
                        c.wait(s, f'{tag}s')
                        c.inc(s.activation(out=bar[:, 2 * S + sl * S: 2 * S + (sl + 1) * S],
                                           in_=dst, func=AF.Square), f'{tag}sq', 1)
                blk.scalar(fscal)
                if kind != 'v':
                    def ftens(t, bi=bi, sl=sl, nq=nq):
                        c.wait(t, f'{tag}sq', B['sq'] + nq + 1)
                        if nq >= 1:
                            c.wait(t, f'{tag}ssev', B['ssev'] + nq)
                        for gi in range(4):
                            ins = t.matmul(psA[0:1, gi * 512:(gi + 1) * 512], onesS[:],
                                           bar[:, 2 * S + sl * S + gi * 512: 2 * S + sl * S + (gi + 1) * 512],
                                           start=True, stop=True)
                        c.inc(ins, f'{tag}ssmm', 1)
                    blk.tensor(ftens)
                    def fvec2(v, bi=bi, sl=sl, nq=nq):
                        c.wait(v, f'{tag}ssmm', B['ssmm'] + nq + 1)
                        c.inc(v.tensor_copy(obuf[0:1, sl * S:(sl + 1) * S],
                                            psA[0:1, 0:2048]), f'{tag}ssev', 1)
                    blk.vector(fvec2)
                def fsync2(sy, bi=bi, sl=sl, kind=kind, idx=idx, nq=nq):
                    if kind == 'v':
                        c.wait(sy, f'{tag}s', c.n(f'{tag}s'))
                        c.inc(sy.dma_start(out=convv[idx * 128:(idx + 1) * 128, :],
                                           in_=obuf[:, sl * S:(sl + 1) * S]),
                              f'{tag}so{sl}', 16)
                    else:
                        c.wait(sy, f'{tag}s', c.n(f'{tag}s'))
                        dstqk = convq if kind == 'q' else convk
                        c.inc(sy.dma_start(out=dstqk[idx * 128:(idx + 1) * 128, :],
                                           in_=bar[:, sl * S:(sl + 1) * S]),
                              f'{tag}so{sl}', 16)
                        c.wait(sy, f'{tag}ssev', B['ssev'] + nq + 1)
                        dss = ssq_d if kind == 'q' else ssk_d
                        c.inc(sy.dma_start(out=dss[idx:idx + 1, :],
                                           in_=obuf[0:1, sl * S:(sl + 1) * S]),
                              f'{tag}sx{sl}', 16)
                blk.sync(fsync2)
                if kind != 'v':
                    nq += 1
            def fsync3(sy):
                c.wait(sy, f'{tag}so0')
                c.wait(sy, f'{tag}so1')
                c.wait(sy, f'{tag}sx0')
                c.wait(sy, f'{tag}sx1')
                c.inc(sy.nop(), f'{tag}done', 1)
            blk.sync(fsync3)

        def ph_rows(b, tag, convtag):
            B = {s: c.n(f'{tag}{s}') for s in ('ld', 's', 'vx', 'vb', 'vu', 'eb',
                                               'ebm', 'ebr', 'tp', 'tev', 'ls',
                                               's2', 'v2', 'vc', 'se', 'vn')}
            bs = stage[0:32, 0:S]
            as_ = stage[0:32, S:2 * S]
            lnb = obuf[0:32, 0:S]
            t1 = obuf[0:32, S:2 * S]
            gd = stage[0:64, 0:S]
            def fsync(sy):
                c.wait(sy, f'{convtag}done')
                c.wait(sy, f'{tag}ld', B['ld'])
                bsel = ba_d.rearrange("(h f) t -> h f t", f=4)
                c.inc(sy.dma_start(out=bs, in_=bsel[:, 0:2, :]),
                      f'{tag}ld', 16)
                c.inc(sy.dma_start(out=as_, in_=bsel[:, 2:4, :]),
                      f'{tag}ld', 16)
            blk.sync(fsync)
            def fscal(s):
                c.wait(s, f'{tag}ld', B['ld'] + 32)
                c.inc(s.activation(out=beta32[:], in_=bs, func=AF.Sigmoid), f'{tag}s', 1)
                c.wait(s, f'{tag}s')
                c.inc(s.activation(out=lnb, in_=beta32[:], func=AF.Ln), f'{tag}s', 1)
                c.wait(s, f'{tag}vx', B['vx'] + 1)
                xa = z1tok[0:32, 0:S]
                c.inc(s.activation(out=t1, in_=xa, func=AF.Abs), f'{tag}s', 1)
                c.wait(s, f'{tag}s')
                c.inc(s.activation(out=t1, in_=t1, func=AF.Exp, scale=-1.0), f'{tag}s', 1)
                c.wait(s, f'{tag}s')
                c.inc(s.activation(out=t1, in_=t1, func=AF.Ln, bias=smallv[0:32, 43:44]),
                      f'{tag}s', 1)
                c.wait(s, f'{tag}s')
                c.inc(s.activation(out=z1tok[0:32, S:2 * S], in_=xa, func=AF.Relu),
                      f'{tag}s', 1)
            blk.scalar(fscal)
            def fvec(v):
                c.wait(v, f'{tag}ld', B['ld'] + 32)
                c.inc(v.tensor_scalar(out=z1tok[0:32, 0:S], in0=as_,
                                      scalar1=smallv[0:32, 0:1], scalar2=None,
                                      op0=ALU.add), f'{tag}vx', 1)
                c.wait(v, f'{tag}s', B['s'] + 6)
                c.inc(v.tensor_tensor(out=z1tok[0:32, 0:S], in0=z1tok[0:32, S:2 * S],
                                      in1=t1, op=ALU.add), f'{tag}vx', 1)
                c.wait(v, f'{tag}vx')
                c.inc(v.tensor_scalar(out=z1tok[0:32, 0:S], in0=z1tok[0:32, 0:S],
                                      scalar1=smallv[0:32, 3:4], scalar2=None,
                                      op0=ALU.mult), f'{tag}vx', 1)
                c.wait(v, f'{tag}vx')
                c.wait(v, f'{tag}s', B['s'] + 2)
                c.inc(v.memset(stage[0:32, 0:S], 1.0), f'{tag}vx', 1)
                c.wait(v, f'{tag}vx')
                for ci in range(16):
                    c.inc(v.tensor_tensor_scan(out=obuf[0:32, S + ci * 128: S + (ci + 1) * 128],
                                               data0=stage[0:32, ci * 128:(ci + 1) * 128],
                                               data1=z1tok[0:32, ci * 128:(ci + 1) * 128],
                                               initial=0.0, op0=ALU.mult, op1=ALU.add),
                          f'{tag}vb', 1)
                c.wait(v, f'{tag}vb', B['vb'] + 16)
                c.inc(v.tensor_tensor(out=z1tok[0:32, S:2 * S], in0=obuf[0:32, S:2 * S],
                                      in1=lnb, op=ALU.add), f'{tag}vu', 1)
                c.inc(v.tensor_scalar(out=stage[0:32, 0:S], in0=obuf[0:32, S:2 * S],
                                      scalar1=-1.0, scalar2=None, op0=ALU.mult),
                      f'{tag}vu', 1)
            blk.vector(fvec)
            def fsync2(sy):
                c.wait(sy, f'{tag}vu', B['vu'] + 2)
                c.inc(sy.dma_start(out=rows_d[:, 0:S], in_=z1tok[0:32, S:2 * S]),
                      f'{tag}sr', 16)
                c.inc(sy.dma_start(out=rows_d[:, S:2 * S], in_=stage[0:32, 0:S]),
                      f'{tag}sr', 16)
                c.inc(sy.dma_start(out=rows_d[:, 2 * S:3 * S], in_=obuf[0:32, S:2 * S]),
                      f'{tag}sr', 16)
            blk.sync(fsync2)
            def ftens(t):
                c.wait(t, f'{tag}vu', B['vu'] + 2)
                c.wait(t, 'cst')
                for g in range(32):
                    if g >= 4:
                        c.wait(t, f'{tag}ebr', B['ebr'] + g - 3)
                    c.inc(t.matmul(psA[:, (g % 4) * 512:(g % 4) * 512 + 16],
                                   selS[0:32, g * 128:(g + 1) * 128],
                                   obuf[0:32, S:2 * S].rearrange("p (a o) -> p a o", o=128)[:, :, 127],
                                   start=True, stop=True), f'{tag}ebm', 1)
            blk.tensor(ftens)
            def fscal2(s):
                for g in range(32):
                    c.wait(s, f'{tag}ebm', B['ebm'] + g + 1)
                    c.inc(s.activation(out=cols[:, 1024 + g * 16: 1024 + (g + 1) * 16],
                                       in_=psA[:, (g % 4) * 512:(g % 4) * 512 + 16],
                                       func=AF.Exp), f'{tag}eb', 1)
                    c.wait(s, f'{tag}eb', B['eb'] + g + 1)
                    c.inc(s.copy(cols[:, g * 16: g * 16 + 16],
                                 psA[:, (g % 4) * 512:(g % 4) * 512 + 16]), f'{tag}ebr', 1)
            blk.scalar(fscal2)
            def ftens2(t):
                c.wait(t, f'{tag}vu', B['vu'] + 2)
                c.wait(t, f'{tag}eb', B['eb'] + 32)
                nt = 0
                for ci in range(16):
                    for srct in (z1tok[0:32, S + ci * 128: S + (ci + 1) * 128],
                                 stage[0:32, ci * 128:(ci + 1) * 128],
                                 beta32[:, ci * 128:(ci + 1) * 128],
                                 obuf[0:32, S + ci * 128: S + (ci + 1) * 128]):
                        if nt >= 4:
                            c.wait(t, f'{tag}tev', B['tev'] + nt - 3)
                        c.inc(t.transpose(psB[:, (nt % 4) * 512:(nt % 4) * 512 + 32],
                                          srct, identf[0:32, 0:32]), f'{tag}tp', 1)
                        nt += 1
            blk.tensor(ftens2)
            def fscal4(s):
                nt = 0
                for ci in range(16):
                    base = ci * 128
                    for q in range(4):
                        c.wait(s, f'{tag}tp', B['tp'] + nt + 1)
                        dst = z1tok[:, base + q * 32: base + q * 32 + 32]
                        c.inc(s.copy(dst, psB[:, (nt % 4) * 512:(nt % 4) * 512 + 32]),
                              f'{tag}tev', 1)
                        nt += 1
            blk.scalar(fscal4)
            def fsync3(sy):
                c.wait(sy, f'{tag}tev', B['tev'] + 64)
                c.wait(sy, f'{tag}sr')
                c.wait(sy, f'{tag}ls', B['ls'])
                c.inc(sy.dma_start(out=stage[0:16, 0:S], in_=ssk_d), f'{tag}ls', 16)
                c.inc(sy.dma_start(out=obuf[0:16, 0:S], in_=ssq_d), f'{tag}ls', 16)
            blk.sync(fsync3)
            def fscal3(s):
                c.wait(s, f'{tag}ls', B['ls'] + 32)
                c.inc(s.activation(out=stage[0:16, S:2 * S], in_=stage[0:16, 0:S],
                                   func=AF.Ln, bias=smallv[0:16, 41:42]), f'{tag}s2', 1)
                c.inc(s.activation(out=obuf[0:16, S:2 * S], in_=obuf[0:16, 0:S],
                                   func=AF.Ln, bias=smallv[0:16, 41:42]), f'{tag}s2', 1)
            blk.scalar(fscal3)
            def fvec2(v):
                c.wait(v, f'{tag}s2', B['s2'] + 2)
                c.inc(v.tensor_scalar(out=stage[0:16, S:2 * S], in0=stage[0:16, S:2 * S],
                                      scalar1=-0.5, scalar2=None, op0=ALU.mult),
                      f'{tag}v2', 1)
                c.inc(v.tensor_scalar(out=obuf[0:16, S:2 * S], in0=obuf[0:16, S:2 * S],
                                      scalar1=-0.5, scalar2=None, op0=ALU.mult),
                      f'{tag}v2', 1)
            blk.vector(fvec2)
            def fsync4(sy):
                c.wait(sy, f'{tag}v2', B['v2'] + 2)
                c.inc(sy.dma_start(out=rowsK_d[:, 0:S], in_=stage[0:16, S:2 * S]), f'{tag}sk', 16)
                c.inc(sy.dma_start(out=rowsK_d[:, S:2 * S], in_=stage[0:16, S:2 * S]), f'{tag}sk', 16)
                c.inc(sy.dma_start(out=rowsK_d[:, 2 * S:3 * S], in_=obuf[0:16, S:2 * S]),
                      f'{tag}sk', 16)
            blk.sync(fsync4)
            def ftens3(t):
                c.wait(t, f'{tag}v2', B['v2'] + 2)
                nt = 64
                for ci in range(16):
                    for srct in (stage[0:16, S + ci * 128: S + (ci + 1) * 128],
                                 obuf[0:16, S + ci * 128: S + (ci + 1) * 128]):
                        if True:
                            c.wait(t, f'{tag}tev', B['tev'] + nt - 3)
                        c.inc(t.transpose(psB[:, (nt % 4) * 512:(nt % 4) * 512 + 16],
                                          srct, identf[0:16, 0:16]), f'{tag}tp', 1)
                        nt += 1
            blk.tensor(ftens3)
            def fscal4b(s):
                nt = 64
                for ci in range(16):
                    for q in range(2):
                        c.wait(s, f'{tag}tp', B['tp'] + nt + 1)
                        dst = (cols[:, 512 + ci * 16: 512 + ci * 16 + 16] if q == 0 else
                               cols[:, 768 + ci * 16: 768 + ci * 16 + 16])
                        c.inc(s.copy(dst, psB[:, (nt % 4) * 512:(nt % 4) * 512 + 16]),
                              f'{tag}tev', 1)
                        nt += 1
            blk.scalar(fscal4b)
            def fvec3(v):
                for ci in range(16):
                    c.wait(v, f'{tag}tev', B['tev'] + 64 + 2 * (ci + 1))
                    base = ci * 128
                    lnrnT = cols[:, 512 + ci * 16: 512 + ci * 16 + 16]
                    lnqT = cols[:, 768 + ci * 16: 768 + ci * 16 + 16]
                    dup = lnrnT.rearrange("p (a o) -> p a o", o=1).broadcast_to([128, 16, 2])
                    dupq = lnqT.rearrange("p (a o) -> p a o", o=1).broadcast_to([128, 16, 2])
                    c.inc(v.tensor_tensor(out=z1tok[:, base:base + 32].rearrange("p (a o) -> p a o", o=2),
                                          in0=z1tok[:, base:base + 32].rearrange("p (a o) -> p a o", o=2),
                                          in1=dup, op=ALU.add), f'{tag}vc', 1)
                    c.wait(v, f'{tag}vc')
                    c.inc(v.tensor_tensor(out=z1tok[:, base + 32:base + 64].rearrange("p (a o) -> p a o", o=2),
                                          in0=z1tok[:, base + 32:base + 64].rearrange("p (a o) -> p a o", o=2),
                                          in1=dup, op=ALU.add), f'{tag}vc', 1)
                    c.wait(v, f'{tag}vc')
                    c.inc(v.scalar_tensor_tensor(out=z1tok[:, 2048 + ci * 64 + 32: 2048 + ci * 64 + 64].rearrange("p (a o) -> p a o", o=2),
                                                 in0=z1tok[:, base + 96:base + 128].rearrange("p (a o) -> p a o", o=2),
                                                 scalar=-0.5 * float(np.log(128.0)),
                                                 in1=dupq, op0=ALU.add, op1=ALU.add),
                          f'{tag}vc', 1)
            blk.vector(fvec3)
            def fscal5(s):
                for ci in range(16):
                    c.wait(s, f'{tag}vc', B['vc'] + 3 * (ci + 1))
                    c.inc(s.activation(out=z1tok[:, 2048 + ci * 64: 2048 + ci * 64 + 32],
                                       in_=z1tok[:, ci * 128: ci * 128 + 32],
                                       func=AF.Exp), f'{tag}se', 1)
                    c.wait(s, f'{tag}se')
                    c.inc(s.activation(out=z1tok[:, 2048 + ci * 64 + 32: 2048 + ci * 64 + 64],
                                       in_=z1tok[:, 2048 + ci * 64 + 32: 2048 + ci * 64 + 64],
                                       func=AF.Exp), f'{tag}se', 1)
            blk.scalar(fscal5)
            def fvec4(v):
                for ci in range(16):
                    c.wait(v, f'{tag}se', B['se'] + 2 * (ci + 1))
                    c.inc(v.tensor_scalar(out=z1tok[:, 2048 + ci * 64: 2048 + ci * 64 + 32],
                                          in0=z1tok[:, 2048 + ci * 64: 2048 + ci * 64 + 32],
                                          scalar1=-1.0, scalar2=None, op0=ALU.mult),
                          f'{tag}vn', 1)
                v2 = None
            blk.vector(fvec4)
            def fsync5(sy):
                c.wait(sy, f'{tag}vn', B['vn'] + 16)
                c.wait(sy, f'{tag}ebr', B['ebr'] + 32)
                c.wait(sy, f'{tag}sr')
                c.wait(sy, f'{tag}sk')
                c.inc(sy.nop(), f'{tag}done', 1)
            blk.sync(fsync5)


        def ph_delta(b, tag, rowstag):
            NIDX = DEV_DELTA_LIMIT
            B = {s: c.n(f'{tag}{s}') for s in ('dl0', 'dl1', 'dg0', 'dg1', 'sz',
                                               'sd0', 'sd1')}
            guards = {}
            def gwait(eng, key):
                g_ = guards.get(key)
                if g_:
                    c.wait(eng, g_[0], g_[1])
            def gset(key, sem):
                guards[key] = (sem, c.n(sem))
            for idx in range(NIDX):
                ch, h = divmod(idx, 16)
                cc = ch
                sl = idx % 2
                g0, g1 = 2 * h, 2 * h + 1
                kqo = 8192 + sl * 256
                kto = 8704 + sl * 128
                auo = 13312
                wto = 14080 + sl * 128
                yto = 14336 + sl * 128
                def col_u(g):    return z1tok[:, cc * 128 + g: cc * 128 + g + 1]
                def col_w(g):    return z1tok[:, cc * 128 + 32 + g: cc * 128 + 33 + g]
                def col_b(g):    return z1tok[:, cc * 128 + 64 + g: cc * 128 + 65 + g]
                def col_neu(g):  return z1tok[:, 2048 + cc * 64 + g: 2048 + cc * 64 + g + 1]
                def col_euq(g):  return z1tok[:, 2048 + cc * 64 + 32 + g: 2048 + cc * 64 + 33 + g]
                def col_ebc(g):  return cols[:, 1024 + g * 16 + cc: 1024 + g * 16 + cc + 1]

                def fsync(sy, idx=idx, ch=ch, h=h, sl=sl, kqo=kqo):
                    if idx == 0:
                        c.wait(sy, f'{rowstag}done')
                    if idx < 2:
                        c.wait(sy, f'{tag}dl{sl}', B[f'dl{sl}'])
                    gwait(sy, ('kq', sl))
                    c.inc(sy.dma_start(out=actT[:, kqo:kqo + 128],
                                       in_=convk[h * 128:(h + 1) * 128, ch * 128:(ch + 1) * 128]),
                          f'{tag}dl{sl}', 16)
                    c.inc(sy.dma_start(out=actT[:, kqo + 128:kqo + 256],
                                       in_=convq[h * 128:(h + 1) * 128, ch * 128:(ch + 1) * 128]),
                          f'{tag}dl{sl}', 16)
                    gwait(sy, ('zs', sl))
                    gwait(sy, ('vt', sl))
                    for j in range(2):
                        c.inc(sy.dma_start(out=stage[:, 768 + (2 * sl + j) * 128: 768 + (2 * sl + j + 1) * 128],
                                           in_=convv[(2 * h + j) * 128:(2 * h + j + 1) * 128,
                                                     ch * 128:(ch + 1) * 128]),
                              f'{tag}dl{sl}', 16)
                        c.inc(sy.dma_start(out=stage[:, 1280 + (2 * sl + j) * 128: 1280 + (2 * sl + j + 1) * 128],
                                           in_=qkvzT[(6 * h + 4 + j) * 128:(6 * h + 5 + j) * 128,
                                                     ch * 128:(ch + 1) * 128]),
                              f'{tag}dl{sl}', 16)
                blk.sync(fsync)
                def fgp(g_, idx=idx, ch=ch, sl=sl):
                    if idx == 0:
                        c.wait(g_, f'{rowstag}done')
                        c.inc(g_.memset(actT[:, 9216:13312], 0.0), f'{tag}sz', 1)
                    if idx < 2:
                        c.wait(g_, f'{tag}dg{sl}', B[f'dg{sl}'])
                    gwait(g_, ('land', sl))
                    c.inc(g_.dma_start(out=stage[0:32, sl * 384:(sl + 1) * 384],
                                       in_=rows_d.rearrange("g (q t) -> g q t", t=S)[:, :, ch * 128:(ch + 1) * 128]),
                          f'{tag}dg{sl}', 16)
                    c.inc(g_.dma_start(out=stage[32:48, sl * 384:(sl + 1) * 384],
                                       in_=rowsK_d.rearrange("g (q t) -> g q t", t=S)[:, :, ch * 128:(ch + 1) * 128]),
                          f'{tag}dg{sl}', 16)
                blk.gpsimd(fgp)

                def ftA(t, idx=idx, ch=ch, h=h, sl=sl, kqo=kqo, kto=kto):
                    if idx == 0:
                        c.wait(t, f'{rowstag}done')
                    c.wait(t, f'{tag}dl{sl}', B[f'dl{sl}'] + 16 * 6 * (idx // 2 + 1))
                    c.wait(t, f'{tag}dg{sl}', B[f'dg{sl}'] + 16 * 2 * (idx // 2 + 1))
                    KT = actT[:, kqo:kqo + 128]
                    QT = actT[:, kqo + 128:kqo + 256]
                    gwait(t, ('bank0',))
                    gwait(t, ('bank0b',))
                    c.inc(t.transpose(psA[:, 256:384], stage[:, 1280 + 2 * sl * 128: 1280 + (2 * sl + 1) * 128], identf[:]), f'{tag}pga', 1)
                    c.inc(t.transpose(psA[:, 384:512], stage[:, 1280 + (2 * sl + 1) * 128: 1280 + (2 * sl + 2) * 128], identf[:]), f'{tag}pga', 1)
                    c.inc(t.matmul(psA[:, 0:128], KT, KT, start=True, stop=True), f'{tag}pga', 1)
                    c.inc(t.matmul(psA[:, 128:256], KT, QT, start=True, stop=True), f'{tag}pga', 1)
                    gwait(t, ('bank1',))
                    gwait(t, ('bank1b',))
                    c.inc(t.matmul(psA[:, 512:640], KT, identb[:], start=True, stop=True), f'{tag}pga', 1)
                    c.inc(t.transpose(psA[:, 640:768], stage[:, 768 + 2 * sl * 128: 768 + (2 * sl + 1) * 128], identf[:]), f'{tag}pga', 1)
                    c.inc(t.transpose(psA[:, 768:896], stage[:, 768 + (2 * sl + 1) * 128: 768 + (2 * sl + 2) * 128], identf[:]), f'{tag}pga', 1)
                    gwait(t, ('bank2',))
                    c.inc(t.matmul(psA[:, 1024:1408], selS[:, 2 * h * 128:(2 * h + 1) * 128],
                                   stage[0:48, sl * 384:(sl + 1) * 384], start=True, stop=True), f'{tag}pga', 1)
                    gwait(t, ('bank3',))
                    c.inc(t.matmul(psA[:, 1536:1920], selS[:, (2 * h + 1) * 128:(2 * h + 2) * 128],
                                   stage[0:48, sl * 384:(sl + 1) * 384], start=True, stop=True), f'{tag}pga', 1)
                blk.tensor(ftA)
                pga_n = c.n(f'{tag}pga')
                gset(('land', sl), f'{tag}pga')

                def fsA(s, idx=idx, sl=sl, kto=kto, pga_n=pga_n):
                    c.wait(s, f'{tag}pga', pga_n - 4)
                    gwait(s, ('ktok', sl))
                    c.inc(s.copy(actT[:, kto:kto + 128], psA[:, 512:640]), f'{tag}evS', 1)
                    c.wait(s, f'{tag}pga', pga_n - 7)
                    gwait(s, ('zs', sl))
                    c.inc(s.activation(out=stage[:, 1280 + 2 * sl * 128: 1280 + (2 * sl + 1) * 128],
                                       in_=psA[:, 256:384], func=AF.Silu), f'{tag}evS', 1)
                    c.inc(s.activation(out=stage[:, 1280 + (2 * sl + 1) * 128: 1280 + (2 * sl + 2) * 128],
                                       in_=psA[:, 384:512], func=AF.Silu), f'{tag}evS', 1)
                blk.scalar(fsA)
                evS_A = c.n(f'{tag}evS')
                def fvA(v, idx=idx, sl=sl, pga_n=pga_n):
                    c.wait(v, f'{tag}pga', pga_n - 2)
                    gwait(v, ('vt', sl))
                    c.inc(v.tensor_copy(stage[:, 768 + 2 * sl * 128: 768 + (2 * sl + 1) * 128],
                                        psA[:, 640:768]), f'{tag}evV', 1)
                    c.inc(v.tensor_copy(stage[:, 768 + (2 * sl + 1) * 128: 768 + (2 * sl + 2) * 128],
                                        psA[:, 768:896]), f'{tag}evV', 1)
                blk.vector(fvA)
                evV_A = c.n(f'{tag}evV')

                def ftB(t, idx=idx, sl=sl, kqo=kqo, g0=g0, g1=g1):
                    KT = actT[:, kqo:kqo + 128]
                    QT = actT[:, kqo + 128:kqo + 256]
                    gwait(t, ('bank4',))
                    gwait(t, ('bank4b',))
                    if idx < 16:
                        c.wait(t, f'{tag}sz', B['sz'] + 1)
                    gwait(t, ('s', g0))
                    gwait(t, ('s', g1))
                    S0 = actT[:, 9216 + g0 * 128: 9216 + (g0 + 1) * 128]
                    S1 = actT[:, 9216 + g1 * 128: 9216 + (g1 + 1) * 128]
                    c.inc(t.matmul(psB[:, 0:128], KT, S0, start=True, stop=True), f'{tag}pgb', 1)
                    c.inc(t.matmul(psB[:, 128:256], KT, S1, start=True, stop=True), f'{tag}pgb', 1)
                    c.inc(t.matmul(psB[:, 256:384], QT, S0, start=True, stop=True), f'{tag}pgb', 1)
                    c.inc(t.matmul(psB[:, 384:512], QT, S1, start=True, stop=True), f'{tag}pgb', 1)
                blk.tensor(ftB)
                pgb_n = c.n(f'{tag}pgb')
                gset(('kq', sl), f'{tag}pgb')

                for gi, g in enumerate((g0, g1)):
                    selps = psA[:, 1024:1408] if gi == 0 else psA[:, 1536:1920]
                    arg = stage[:, 1792:1920]
                    expb = stage[:, 1920:2048]
                    au_s = (idx * 2 + gi) % 2
                    AUB = lambda s_: actT[:, auo + s_ * 384: auo + (s_ + 1) * 384]
                    def fvM(v, idx=idx, ch=ch, g=g, gi=gi, sl=sl, selps=selps, arg=arg,
                            expb=expb, au_s=au_s, pga_n=pga_n, pgb_n=pgb_n, evS_A=evS_A):
                        c.wait(v, f'{tag}pga', pga_n)
                        gwait(v, ('argbuf',))
                        c.inc(v.scalar_tensor_tensor(out=arg, in0=selps[:, 128:256],
                                                     scalar=col_u(g), in1=maskA[:],
                                                     op0=ALU.add, op1=ALU.add), f'{tag}vm', 1)
                        c.wait(v, f'{tag}sm', c.n(f'{tag}sm') + 1)
                        gwait(v, ('aub', au_s))
                        c.inc(v.scalar_tensor_tensor(out=AUB(au_s)[:, 0:128], in0=expb,
                                                     scalar=-1.0, in1=psA[:, 0:128],
                                                     op0=ALU.mult, op1=ALU.mult), f'{tag}vm', 1)
                        c.wait(v, f'{tag}vm')
                        c.inc(v.scalar_tensor_tensor(out=arg, in0=selps[:, 0:128],
                                                     scalar=col_w(g), in1=maskB[:],
                                                     op0=ALU.add, op1=ALU.add), f'{tag}vm', 1)
                        c.wait(v, f'{tag}sm', c.n(f'{tag}sm') + 2)
                        c.inc(v.scalar_tensor_tensor(out=AUB(au_s)[:, 256:384], in0=expb,
                                                     scalar=-1.0, in1=psA[:, 0:128],
                                                     op0=ALU.mult, op1=ALU.mult), f'{tag}vm', 1)
                        c.wait(v, f'{tag}vm')
                        c.inc(v.scalar_tensor_tensor(out=arg, in0=selps[:, 256:384],
                                                     scalar=col_w(g), in1=maskW[:],
                                                     op0=ALU.add, op1=ALU.add), f'{tag}vm', 1)
                        c.wait(v, f'{tag}sm', c.n(f'{tag}sm') + 3)
                        gwait(v, ('wt', sl, gi))
                        c.inc(v.tensor_tensor(out=actT[:, 14080 + sl * 128: 14208 + sl * 128],
                                              in0=expb, in1=psA[:, 128:256], op=ALU.mult),
                              f'{tag}vm', 1)
                        c.wait(v, f'{tag}evV', evV_A)
                        gwait(v, ('xg', 2 * sl + gi))
                        xgb = obuf[:, 512 + (2 * sl + gi) * 128: 512 + (2 * sl + gi + 1) * 128]
                        c.inc(v.tensor_scalar(out=xgb,
                                              in0=stage[:, 768 + (2 * sl + gi) * 128: 768 + (2 * sl + gi + 1) * 128],
                                              scalar1=col_b(g), scalar2=None, op0=ALU.mult),
                              f'{tag}vm', 1)
                        c.wait(v, f'{tag}vm')
                        c.wait(v, f'{tag}pgb', pgb_n)
                        c.inc(v.scalar_tensor_tensor(out=AUB(au_s)[:, 128:256],
                                                     in0=psB[:, gi * 128:(gi + 1) * 128],
                                                     scalar=col_neu(g), in1=xgb,
                                                     op0=ALU.mult, op1=ALU.add), f'{tag}vm', 1)
                        c.wait(v, f'{tag}vm')
                        c.inc(v.tensor_tensor(out=cols[:, 1552 + 2 * sl + gi: 1553 + 2 * sl + gi],
                                              in0=col_w(g), in1=cols[:, g * 16 + cc: g * 16 + cc + 1],
                                              op=ALU.add), f'{tag}vm', 1)
                    blk.vector(fvM)
                    if DEV_DBG and idx in (0, 16) and gi == 0:
                        def fdbg1(sy, au_s=au_s, sl=sl, idx=idx, ch=ch):
                            o = 0 if idx == 0 else 2048
                            c.wait(sy, f'{tag}vm', c.n(f'{tag}vm'))
                            c.wait(sy, f'{tag}sm', c.n(f'{tag}sm') + 4)
                            c.wait(sy, f'{tag}dg{sl}', B[f'dg{sl}'] + 16 * 2 * (idx // 2 + 1))
                            c.inc(sy.dma_start(out=dbgb[:, o:o + 384], in_=AUB(au_s)[:, 0:384]), f'{tag}dbg', 16)
                            c.inc(sy.dma_start(out=dbgb[:, o + 384:o + 512], in_=actT[:, 14080 + sl * 128: 14208 + sl * 128]), f'{tag}dbg', 16)
                            c.inc(sy.dma_start(out=dbgf[0:48, o:o + 384], in_=stage[0:48, sl * 384:(sl + 1) * 384]), f'{tag}dbg', 16)
                            c.inc(sy.dma_start(out=dbgf[:, o + 384:o + 512], in_=stage[:, 768 + 2 * sl * 128: 768 + (2 * sl + 1) * 128]), f'{tag}dbg', 16)
                            c.inc(sy.dma_start(out=dbgf[:, o + 512:o + 640], in_=stage[:, 1280 + 2 * sl * 128: 1280 + (2 * sl + 1) * 128]), f'{tag}dbg', 16)
                            c.inc(sy.dma_start(out=dbgf[:, o + 640:o + 768], in_=z1tok[:, ch * 128: ch * 128 + 128]), f'{tag}dbg', 16)
                            c.inc(sy.dma_start(out=dbgf[:, o + 768:o + 832], in_=z1tok[:, 2048 + ch * 64: 2048 + ch * 64 + 64]), f'{tag}dbg', 16)
                            c.inc(sy.dma_start(out=dbgb[:, o + 512:o + 768], in_=actT[:, 8192 + sl * 256: 8192 + sl * 256 + 256]), f'{tag}dbg', 16)
                            c.inc(sy.dma_start(out=dbgf[:, o + 832:o + 833], in_=cols[:, 1560 + 2 * sl: 1561 + 2 * sl], allow_slow_non_contiguous=True), f'{tag}dbg', 16)
                            c.inc(sy.dma_start(out=dbgf[:, o + 833:o + 834], in_=cols[:, 0 + ch: 1 + ch], allow_slow_non_contiguous=True), f'{tag}dbg', 16)
                            c.inc(sy.dma_start(out=dbgb[:, o + 896:o + 1024], in_=actT[:, 9216:9344]), f'{tag}dbg', 16)
                            c.inc(sy.dma_start(out=dbgf[:, o + 1280:o + 1792], in_=cols[:, 0:512]), f'{tag}dbg', 16)
                            c.inc(sy.dma_start(out=dbgf[0:48, o + 1792:o + 1920], in_=selS[0:48, 2048:2176]), f'{tag}dbg', 16)
                            c.inc(sy.dma_start(out=dbgf[0:48, o + 1920:o + 2048], in_=selS[0:48, 0:128]), f'{tag}dbg', 16)
                        blk.sync(fdbg1)
                        for key in (('land', sl), ('vt', sl), ('zs', sl), ('kq', sl),
                                    ('aub', au_s), ('wt', sl, gi)):
                            gset(key, f'{tag}dbg')
                    def fsM2(s, idx=idx, gi=gi, arg=arg, expb=expb):
                        base = c.n(f'{tag}sm')
                        vm_now = c.n(f'{tag}vm')
                        c.wait(s, f'{tag}vm', vm_now - 8)
                        c.inc(s.activation(out=expb, in_=arg, func=AF.Exp), f'{tag}sm', 1)
                        c.wait(s, f'{tag}vm', vm_now - 6)
                        c.inc(s.activation(out=expb, in_=arg, func=AF.Exp), f'{tag}sm', 1)
                        c.wait(s, f'{tag}vm', vm_now - 4)
                        c.inc(s.activation(out=expb, in_=arg, func=AF.Exp), f'{tag}sm', 1)
                        c.wait(s, f'{tag}vm', vm_now)
                        gwait(s, ('scc', 2 * sl + gi))
                        c.inc(s.activation(out=cols[:, 1560 + 2 * sl + gi: 1561 + 2 * sl + gi],
                                           in_=cols[:, 1552 + 2 * sl + gi: 1553 + 2 * sl + gi],
                                           func=AF.Exp), f'{tag}sm', 1)
                    blk.scalar(fsM2)
                    gset(('argbuf',), f'{tag}sm')
                    def fsQ(s, idx=idx, g=g, gi=gi, sl=sl, pgb_n=pgb_n):
                        c.wait(s, f'{tag}pgb', pgb_n)
                        gwait(s, ('qs0', 2 * sl + gi))
                        c.inc(s.activation(out=obuf[:, (2 * sl + gi) * 128:(2 * sl + gi + 1) * 128],
                                           in_=psB[:, 256 + gi * 128: 256 + (gi + 1) * 128],
                                           func=AF.Copy, scale=col_euq(g)), f'{tag}sq', 1)
                    blk.scalar(fsQ)
                    vm_done = c.n(f'{tag}vm')
                    for lev in range(7):
                        cur = (idx * 2 + gi + lev) % 2
                        nxt = 1 - cur
                        def ftS(t, lev=lev, cur=cur, nxt=nxt, au_s=au_s, idx=idx, gi=gi,
                                vm_done=vm_done):
                            A_ = AUB(cur)[:, 0:128]
                            U_ = AUB(cur)[:, 128:256]
                            B_ = AUB(cur)[:, 256:384]
                            if lev == 0:
                                c.wait(t, f'{tag}vm', vm_done)
                            gwait(t, ('bank6',))
                            if lev < 6:
                                c.inc(t.matmul(psB[:, 1024:1280], B_, AUB(cur)[:, 0:256],
                                               start=True, stop=False), f'{tag}ps', 1)
                                c.inc(t.matmul(psB[:, 1152:1280], identb[:], U_,
                                               start=False, stop=True), f'{tag}ps', 1)
                                c.inc(t.matmul(psB[:, 1280:1408], A_, B_,
                                               start=True, stop=True), f'{tag}ps', 1)
                            else:
                                c.inc(t.matmul(psB[:, 1024:1152], B_, U_,
                                               start=True, stop=False), f'{tag}ps', 1)
                                c.inc(t.matmul(psB[:, 1024:1152],
                                               identb[:], U_, start=False, stop=True), f'{tag}ps', 1)
                        blk.tensor(ftS)
                        def fevS(eng, lev=lev, cur=cur, nxt=nxt, idx=idx, gi=gi):
                            c.wait(eng, f'{tag}ps', c.n(f'{tag}ps'))
                            if DEV_DBG and c.n(f'{tag}dbg'):
                                c.wait(eng, f'{tag}dbg', c.n(f'{tag}dbg'))
                            gwait(eng, ('aub', nxt))
                            if lev < 6:
                                if isinstance(eng, type(nc.vector)):
                                    c.inc(eng.tensor_copy(AUB(nxt)[:, 0:384], psB[:, 1024:1408]),
                                          f'{tag}evV', 1)
                                else:
                                    c.inc(eng.copy(AUB(nxt)[:, 0:384], psB[:, 1024:1408]),
                                          f'{tag}evV' if False else f'{tag}evS', 1)
                            else:
                                c.inc(eng.tensor_copy(AUB(nxt)[:, 128:256], psB[:, 1024:1152]),
                                      f'{tag}evV', 1)
                                gwait(eng, ('ubar', gi))
                                c.inc(eng.tensor_scalar(
                                    out=actT[:, 14592 + gi * 128: 14720 + gi * 128],
                                    in0=psB[:, 1024:1152],
                                    scalar1=cols[:, 1560 + 2 * sl + gi: 1561 + 2 * sl + gi],
                                    scalar2=None, op0=ALU.mult), f'{tag}evV', 1)
                        if lev % 2 == 0:
                            blk.vector(fevS)
                            gset(('bank6',), f'{tag}evV')
                        else:
                            blk.scalar(fevS)
                            gset(('bank6',), f'{tag}evS')
                        gset(('aub', cur), f'{tag}ps')
                    ufin = (idx * 2 + gi + 7) % 2
                    def ftO(t, idx=idx, gi=gi, g=g, sl=sl, ufin=ufin, kqo=kqo, kto=kto):
                        U7 = AUB(ufin)[:, 128:256]
                        WT = actT[:, 14080 + sl * 128: 14208 + sl * 128]
                        gwait(t, ('bank7a', gi))
                        c.wait(t, f'{tag}evV', c.n(f'{tag}evV'))
                        c.wait(t, f'{tag}evS', c.n(f'{tag}evS'))
                        c.inc(t.matmul(psB[:, 1536 + gi * 128: 1664 + gi * 128], WT, U7,
                                       start=True, stop=True), f'{tag}po', 1)
                    blk.tensor(ftO)
                    po_n = c.n(f'{tag}po')
                    gset(('wt', sl, gi), f'{tag}po')
                    def fvO(v, idx=idx, gi=gi, g=g, sl=sl, po_n=po_n):
                        c.wait(v, f'{tag}po', po_n)
                        c.wait(v, f'{tag}sq', c.n(f'{tag}sq'))
                        xgb = obuf[:, 512 + (2 * sl + gi) * 128: 512 + (2 * sl + gi + 1) * 128]
                        qsb = obuf[:, (2 * sl + gi) * 128:(2 * sl + gi + 1) * 128]
                        c.inc(v.tensor_tensor(out=xgb, in0=psB[:, 1536 + gi * 128: 1664 + gi * 128],
                                              in1=qsb, op=ALU.add), f'{tag}vo', 1)
                        c.wait(v, f'{tag}vo')
                        c.inc(v.tensor_tensor(out=xgb, in0=xgb,
                                              in1=stage[:, 1280 + (2 * sl + gi) * 128: 1280 + (2 * sl + gi + 1) * 128],
                                              op=ALU.mult), f'{tag}vo', 1)
                    blk.vector(fvO)
                    gset(('qs0', 2 * sl + gi), f'{tag}vo')
                    def fsO(s, idx=idx, gi=gi, sl=sl):
                        c.wait(s, f'{tag}vo', c.n(f'{tag}vo'))
                        xgb = obuf[:, 512 + (2 * sl + gi) * 128: 512 + (2 * sl + gi + 1) * 128]
                        c.inc(s.activation(out=bar[:, 4 * S + (2 * sl + gi) * 128: 4 * S + (2 * sl + gi + 1) * 128],
                                           in_=xgb, func=AF.Square,
                                           accum_out=cols[:, 1536 + 2 * sl + gi: 1537 + 2 * sl + gi]),
                              f'{tag}so', 1)
                        c.wait(s, f'{tag}so')
                        c.inc(s.activation(out=cols[:, 1540 + 2 * sl + gi: 1541 + 2 * sl + gi],
                                           in_=cols[:, 1536 + 2 * sl + gi: 1537 + 2 * sl + gi],
                                           func=AF.Sqrt, bias=smallv[:, 42:43]), f'{tag}so', 1)
                    blk.scalar(fsO)
                    def fvY(v, idx=idx, gi=gi, sl=sl, g=g):
                        c.wait(v, f'{tag}so', c.n(f'{tag}so'))
                        c.inc(v.reciprocal(cols[:, 1544 + 2 * sl + gi: 1545 + 2 * sl + gi],
                                           cols[:, 1540 + 2 * sl + gi: 1541 + 2 * sl + gi]),
                              f'{tag}vy', 1)
                        c.wait(v, f'{tag}vy')
                        xgb = obuf[:, 512 + (2 * sl + gi) * 128: 512 + (2 * sl + gi + 1) * 128]
                        gwait(v, ('ybuf', gi))
                        c.inc(v.tensor_scalar(out=bar[:, 5 * S + gi * 128: 5 * S + (gi + 1) * 128],
                                              in0=xgb, scalar1=cols[:, 1544 + 2 * sl + gi: 1545 + 2 * sl + gi],
                                              scalar2=None, op0=ALU.mult), f'{tag}vy', 1)
                    blk.vector(fvY)
                    vy_n = c.n(f'{tag}vy')
                    gset(('xg', 2 * sl + gi), f'{tag}vy')
                    def ftY(t, idx=idx, gi=gi, sl=sl, g=g, vy_n=vy_n, ufin=ufin, kto=kto):
                        c.wait(t, f'{tag}vy', vy_n)
                        gwait(t, ('bank5', gi))
                        c.inc(t.matmul(psB[:, 512 + gi * 128: 640 + gi * 128],
                                       bar[:, 5 * S + gi * 128: 5 * S + (gi + 1) * 128],
                                       identb[:], start=True, stop=True), f'{tag}py', 1)
                        gwait(t, ('bank7b', gi))
                        c.inc(t.matmul(psB[:, 1792 + gi * 128: 1920 + gi * 128],
                                       actT[:, kto:kto + 128],
                                       actT[:, 14592 + gi * 128: 14720 + gi * 128],
                                       start=True, stop=True), f'{tag}py', 1)
                    blk.tensor(ftY)
                    py_n = c.n(f'{tag}py')
                    gset(('ktok', sl), f'{tag}py')
                    gset(('aub', ufin), f'{tag}py')
                    gset(('ubar', gi), f'{tag}py')
                    gset(('scc', 2 * sl + gi), f'{tag}evV')
                    def fsY(s, idx=idx, gi=gi, sl=sl, g=g, ch=ch, py_n=py_n):
                        c.wait(s, f'{tag}py', py_n - 1)
                        gwait(s, ('yslot', gi))
                        c.inc(s.copy(actT[:, 14336 + gi * 128: 14464 + gi * 128],
                                     psB[:, 512 + gi * 128: 640 + gi * 128]), f'{tag}sy', 1)
                    blk.scalar(fsY)
                    gset(('ybuf', gi), f'{tag}sy')
                    def fvS(v, idx=idx, gi=gi, g=g, py_n=py_n):
                        c.wait(v, f'{tag}py', py_n)
                        c.wait(v, f'{tag}vs', c.n(f'{tag}vs'))
                        S_g = actT[:, 9216 + g * 128: 9216 + (g + 1) * 128]
                        c.inc(v.tensor_scalar(out=obuf[:, 1024:1152], in0=S_g,
                                              scalar1=col_ebc(g), scalar2=None,
                                              op0=ALU.mult), f'{tag}vs', 1)
                        c.wait(v, f'{tag}vs')
                        c.inc(v.tensor_tensor(out=S_g,
                                              in0=psB[:, 1792 + gi * 128: 1920 + gi * 128],
                                              in1=obuf[:, 1024:1152], op=ALU.add), f'{tag}vs', 1)
                    blk.vector(fvS)
                    if DEV_DBG and idx in (0, 16) and gi == 0:
                        def fdbg2(sy, ufin=ufin, gi=gi, sl=sl, g=g, idx=idx):
                            o = 0 if idx == 0 else 2048
                            c.wait(sy, f'{tag}vs', c.n(f'{tag}vs'))
                            c.wait(sy, f'{tag}vy', c.n(f'{tag}vy'))
                            c.inc(sy.dma_start(out=dbgb[:, o + 768:o + 896], in_=AUB(ufin)[:, 128:256]), f'{tag}dbg', 16)
                            c.inc(sy.dma_start(out=dbgf[:, o + 896:o + 1024], in_=obuf[:, 512 + (2 * sl + gi) * 128: 512 + (2 * sl + gi + 1) * 128]), f'{tag}dbg', 16)
                            c.inc(sy.dma_start(out=dbgf[:, o + 1024:o + 1152], in_=obuf[:, (2 * sl + gi) * 128:(2 * sl + gi + 1) * 128]), f'{tag}dbg', 16)
                            c.inc(sy.dma_start(out=dbgb[:, o + 1024:o + 1152], in_=actT[:, 9216 + g * 128: 9216 + (g + 1) * 128]), f'{tag}dbg', 16)
                            c.inc(sy.dma_start(out=dbgb[:, o + 1152:o + 1280], in_=actT[:, 8704 + sl * 128: 8704 + sl * 128 + 128]), f'{tag}dbg', 16)
                            c.inc(sy.dma_start(out=dbgb[:, o + 1280:o + 1408], in_=actT[:, 14592 + gi * 128: 14720 + gi * 128]), f'{tag}dbg', 16)
                        blk.sync(fdbg2)
                        for key in (('aub', ufin), ('s', g), ('xg', 2 * sl + gi),
                                    ('qs0', 2 * sl + gi), ('ktok', sl)):
                            gset(key, f'{tag}dbg')
                    if not (DEV_DBG and idx == 0 and gi == 0):
                        gset(('s', g), f'{tag}vs')
                    gset(('bank7b', gi), f'{tag}vs')
                    gset(('bank5', gi), f'{tag}sy')
                    def fsyY(sy, idx=idx, gi=gi, g=g, ch=ch):
                        c.wait(sy, f'{tag}sd{gi}', c.n(f'{tag}sd{gi}'))
                        c.wait(sy, f'{tag}sy', c.n(f'{tag}sy'))
                        c.inc(sy.dma_start(out=yT_d[g * 128:(g + 1) * 128, ch * 128:(ch + 1) * 128],
                                           in_=actT[:, 14336 + gi * 128: 14464 + gi * 128]),
                              f'{tag}sd{gi}', 16)
                    blk.sync(fsyY)
                    gset(('yslot', gi), f'{tag}sd{gi}')
                    gset(('bank7a', gi), f'{tag}vs')
                gset(('bank0',), f'{tag}vm')
                gset(('bank0b',), f'{tag}evS')
                gset(('bank1',), f'{tag}evV')
                gset(('bank1b',), f'{tag}evS')
                gset(('bank2',), f'{tag}vm')
                gset(('bank3',), f'{tag}vm')
                gset(('bank4',), f'{tag}vm')
                gset(('bank4b',), f'{tag}sq')
                gset(('zs', sl), f'{tag}vo')
                gset(('vt', sl), f'{tag}vm')
            def fend(sy):
                c.wait(sy, f'{tag}sd0')
                c.wait(sy, f'{tag}sd1')
                if c.n(f'{tag}dbg'):
                    c.wait(sy, f'{tag}dbg')
                c.inc(sy.nop(), f'{tag}done', 1)
            blk.sync(fend)


        def ph_prepA(tag):
            for i in range(32):
                sl = i % 2
                def fsy(sy, i=i, sl=sl):
                    if i >= 2:
                        c.wait(sy, f'{tag}v', i - 1)
                    c.inc(sy.dma_start(out=stage[:, sl * S:(sl + 1) * S],
                                       in_=w_out[i * 128:(i + 1) * 128, :]), f'{tag}l{sl}', 16)
                blk.sync(fsy)
                def fv(v, i=i, sl=sl):
                    c.wait(v, f'{tag}l{sl}', 16 * (i // 2 + 1))
                    c.wait(v, 'cstV')
                    if i >= 2:
                        c.wait(v, f'{tag}s{sl}', 16 * (i // 2))
                    c.inc(v.tensor_scalar(out=bar[:, sl * S:(sl + 1) * S],
                                          in0=stage[:, sl * S:(sl + 1) * S],
                                          scalar1=smallv[:, 44:45], scalar2=None,
                                          op0=ALU.mult), f'{tag}v', 1)
                blk.vector(fv)
                def fsy2(sy, i=i, sl=sl):
                    c.wait(sy, f'{tag}v', i + 1)
                    c.inc(sy.dma_start(out=woute_d[i * 128:(i + 1) * 128, :],
                                       in_=bar[:, sl * S:(sl + 1) * S]), f'{tag}s{sl}', 16)
                blk.sync(fsy2)
            def fend(sy):
                c.wait(sy, f'{tag}s0')
                c.wait(sy, f'{tag}s1')
                c.inc(sy.nop(), f'{tag}done', 1)
            blk.sync(fend)


        def ph_prepB(tag):
            def f1t(t):
                c.wait(t, 'cst')
                c.wait(t, 'prAdone')
                for gi2 in range(4):
                    c.inc(t.matmul(psA[:, gi2 * 512:(gi2 + 1) * 512], onesrowS[:],
                                   lnrow1[0:1, gi2 * 512:(gi2 + 1) * 512],
                                   start=True, stop=True), f'{tag}bc', 1)
            blk.tensor(f1t)
            def f1v(v):
                c.wait(v, f'{tag}bc', c.n(f'{tag}bc'))
                for gi2 in range(4):
                    c.inc(v.tensor_copy(obuf[:, gi2 * 512:(gi2 + 1) * 512],
                                        psA[:, gi2 * 512:(gi2 + 1) * 512]), f'{tag}ev', 1)
            blk.vector(f1v)
            ev0 = c.n(f'{tag}ev')
            b_sg = c.n(f'{tag}sg')
            for dp in range(8):
                def f2g(g, dp=dp):
                    if dp == 0:
                        c.wait(g, 'prAdone')
                    for o in range(16):
                        i2 = dp * 16 + o
                        if i2 >= 2:
                            c.wait(g, f'{tag}pe', (i2 - 1) * 8)
                            c.wait(g, f'{tag}lq{i2 % 2}', 16 * (i2 // 2))
                            c.wait(g, f'{tag}lk{i2 % 2}', 16 * (i2 // 2))
                        c.inc(g.dma_start(out=bar[:, (i2 % 2) * 256:(i2 % 2) * 256 + 256],
                                          in_=Wq[o * 128:(o + 1) * 128, dp * 256:(dp + 1) * 256]),
                              f'{tag}lq{i2 % 2}', 16)
                        c.inc(g.dma_start(out=bar[:, 512 + (i2 % 2) * 2048: 512 + (i2 % 2 + 1) * 2048],
                                          in_=Wk[o * 128:(o + 1) * 128, :]),
                              f'{tag}lk{i2 % 2}', 16)
                blk.gpsimd(f2g)
                def f2t(t, dp=dp, ev0=ev0):
                    if dp == 0:
                        c.wait(t, f'{tag}ev', ev0)
                    if dp >= 1:
                        c.wait(t, f'{tag}ev', ev0 + dp * 8)
                    for o in range(16):
                        i2 = dp * 16 + o
                        c.wait(t, f'{tag}lq{i2 % 2}', 16 * (i2 // 2 + 1))
                        c.wait(t, f'{tag}lk{i2 % 2}', 16 * (i2 // 2 + 1))
                        for db2 in range(2):
                            ps = psA if db2 == 0 else psB
                            for dg in range(4):
                                ins = t.matmul(ps[:, dg * 512:(dg + 1) * 512],
                                               bar[:, (i2 % 2) * 256 + db2 * 128:(i2 % 2) * 256 + (db2 + 1) * 128],
                                               bar[:, 512 + (i2 % 2) * 2048 + dg * 512: 512 + (i2 % 2) * 2048 + (dg + 1) * 512],
                                               start=(o == 0), stop=(o == 15))
                        c.inc(ins, f'{tag}pe', 8)
                blk.tensor(f2t)
                def f2v(v, dp=dp):
                    c.wait(v, f'{tag}pe', (dp + 1) * 16 * 8)
                    if dp >= 1:
                        c.wait(v, f'{tag}sg', b_sg + 16 * 2 * dp)
                    for db2 in range(2):
                        ps = psA if db2 == 0 else psB
                        for dg in range(4):
                            c.inc(v.scalar_tensor_tensor(
                                out=bar[:, 4608 + db2 * 2048 + dg * 512: 4608 + db2 * 2048 + (dg + 1) * 512],
                                in0=ps[:, dg * 512:(dg + 1) * 512],
                                scalar=smallv[:, 4 + dp * 2 + db2: 5 + dp * 2 + db2],
                                in1=obuf[:, dg * 512:(dg + 1) * 512],
                                op0=ALU.mult, op1=ALU.mult), f'{tag}ev', 1)
                blk.vector(f2v)
                def f2s(sy, dp=dp, ev0=ev0):
                    c.wait(sy, f'{tag}ev', ev0 + (dp + 1) * 8)
                    for db2 in range(2):
                        c.inc(sy.dma_start(out=g_d[(dp * 2 + db2) * 128:(dp * 2 + db2 + 1) * 128, :],
                                           in_=bar[:, 4608 + db2 * 2048: 4608 + (db2 + 1) * 2048]),
                              f'{tag}sg', 16)
                blk.sync(f2s)
            b_lo = [c.n(f'{tag}lo0'), c.n(f'{tag}lo1')]
            def f3g(g):
                c.wait(g, f'{tag}sg', c.n(f'{tag}sg'))
                for e in range(16):
                    c.inc(g.dma_start(out=bar[:, 8704 + e * 16: 8704 + (e + 1) * 16],
                                      in_=Amat[e * 128:(e + 1) * 128, :]), f'{tag}lA', 16)
                for cb in range(16):
                    for e in range(16):
                        i3 = cb * 16 + e
                        if i3 >= 2:
                            c.wait(g, f'{tag}pw', i3 - 1)
                            c.wait(g, f'{tag}lo{i3 % 2}', b_lo[i3 % 2] + 16 * (i3 // 2))
                        c.inc(g.dma_start(out=bar[:, (i3 % 2) * 128:(i3 % 2) * 128 + 128],
                                          in_=Wo[e * 128:(e + 1) * 128, cb * 128:(cb + 1) * 128]),
                              f'{tag}lo{i3 % 2}', 16)
            blk.gpsimd(f3g)
            def f3t(t):
                c.wait(t, f'{tag}lA', c.n(f'{tag}lA'))
                c.wait(t, f'{tag}sg', c.n(f'{tag}sg'))
                for cb in range(16):
                    if cb >= 1:
                        c.wait(t, f'{tag}evw', cb)
                    for e in range(16):
                        i3 = cb * 16 + e
                        c.wait(t, f'{tag}lo{i3 % 2}', b_lo[i3 % 2] + 16 * (i3 // 2 + 1))
                        ins = t.matmul(psA[:, 0:16],
                                       bar[:, (i3 % 2) * 128:(i3 % 2) * 128 + 128],
                                       bar[:, 8704 + e * 16: 8704 + (e + 1) * 16],
                                       start=(e == 0), stop=(e == 15))
                        c.inc(ins, f'{tag}pw', 1)
                blk_ = None
            blk.tensor(f3t)
            def f3s(s):
                for cb in range(16):
                    c.wait(s, f'{tag}pw', (cb + 1) * 16)
                    c.inc(s.copy(bar[:, 8960 + cb * 16: 8960 + (cb + 1) * 16],
                                 psA[:, 0:16]), f'{tag}evw', 1)
            blk.scalar(f3s)
            b_lv = [c.n(f'{tag}lv0'), c.n(f'{tag}lv1')]
            def f4g(g):
                c.wait(g, f'{tag}evw', 16)
                for db in range(16):
                    for h in range(16):
                        i4 = db * 16 + h
                        if i4 >= 2:
                            c.wait(g, f'{tag}pv', i4 - 1)
                            c.wait(g, f'{tag}lv{i4 % 2}', b_lv[i4 % 2] + 16 * (i4 // 2))
                        c.inc(g.dma_start(out=bar[:, (i4 % 2) * 128:(i4 % 2) * 128 + 128],
                                          in_=Wv[h * 128:(h + 1) * 128, db * 128:(db + 1) * 128]),
                              f'{tag}lv{i4 % 2}', 16)
            blk.gpsimd(f4g)
            def f4t(t):
                c.wait(t, f'{tag}evw', 16)
                for db in range(16):
                    if db >= 1:
                        c.wait(t, f'{tag}evv', db)
                    for h in range(16):
                        i4 = db * 16 + h
                        c.wait(t, f'{tag}lv{i4 % 2}', b_lv[i4 % 2] + 16 * (i4 // 2 + 1))
                        ins = t.matmul(psA[:, 512 + h * 16: 512 + (h + 1) * 16],
                                       bar[:, (i4 % 2) * 128:(i4 % 2) * 128 + 128],
                                       bar[:, 8960 + h * 16: 8960 + (h + 1) * 16],
                                       start=True, stop=True)
                        c.inc(ins, f'{tag}pv', 1)
            blk.tensor(f4t)
            b_sv = [c.n(f'{tag}sv0'), c.n(f'{tag}sv1')]
            def f4s(s):
                for db in range(16):
                    c.wait(s, f'{tag}pv', (db + 1) * 16)
                    if db >= 2:
                        c.wait(s, f'{tag}sv{db % 2}', b_sv[db % 2] + 16 * ((db - 2) // 2 + 1))
                    c.inc(s.activation(out=bar[:, 2048 + (db % 2) * 256: 2048 + (db % 2 + 1) * 256],
                                       in_=psA[:, 512:768], func=AF.Copy,
                                       scale=smallv[:, 4 + db: 5 + db]), f'{tag}evv', 1)
            blk.scalar(f4s)
            def f4sy(sy):
                for db in range(16):
                    c.wait(sy, f'{tag}evv', db + 1)
                    c.inc(sy.dma_start(out=va_d[db * 128:(db + 1) * 128, :],
                                       in_=bar[:, 2048 + (db % 2) * 256: 2048 + (db % 2 + 1) * 256]),
                          f'{tag}sv{db % 2}', 16)
            blk.sync(f4sy)
            def f5g(g):
                c.wait(g, f'{tag}pv', c.n(f'{tag}pv'))
                for db in range(16):
                    c.inc(g.dma_start(out=bar[:, 8704 + db * 16: 8704 + (db + 1) * 16],
                                      in_=Amat[db * 128:(db + 1) * 128, :]), f'{tag}lt', 16)
            blk.gpsimd(f5g)
            def f5v(v):
                c.wait(v, f'{tag}lt', c.n(f'{tag}lt'))
                c.wait(v, f'{tag}pv', c.n(f'{tag}pv'))
                for db in range(16):
                    c.inc(v.tensor_scalar(out=bar[:, 9216 + db * 16: 9216 + (db + 1) * 16],
                                          in0=bar[:, 8704 + db * 16: 8704 + (db + 1) * 16],
                                          scalar1=smallv[:, 20 + db: 21 + db], scalar2=0.1,
                                          op0=ALU.mult, op1=ALU.mult), f'{tag}vA', 1)
            blk.vector(f5v)
            wlist = [(Win, wint_d, 64, True), (Wup, wupt_d, 64, True), (Wdn, wdnt_d, 16, False)]
            b_l6 = [c.n(f'{tag}l6{j}') for j in range(4)]
            b_st = [c.n(f'{tag}st0'), c.n(f'{tag}st1')]
            it5 = 0
            for (wsrc, wdst, nfb, scaled) in wlist:
                ndb = 2048 * FF // (nfb * 128) // 128
                for fb in range(nfb):
                    for db in range(ndb):
                        it = it5
                        it5 += 1
                        sl4 = it % 4
                        def f6g(g, it=it, sl4=sl4, fb=fb, db=db, wsrc=wsrc):
                            if it == 0:
                                c.wait(g, f'{tag}vA', 16)
                            if it >= 4:
                                c.wait(g, f'{tag}pt', it - 3)
                                c.wait(g, f'{tag}l6{sl4}', b_l6[sl4] + 16 * (it // 4))
                            c.inc(g.dma_start(out=bar[:, 11520 + sl4 * 128: 11520 + (sl4 + 1) * 128],
                                              in_=wsrc[fb * 128:(fb + 1) * 128, db * 128:(db + 1) * 128]),
                                  f'{tag}lw{sl4}' if False else f'{tag}l6{sl4}', 16)
                        blk.gpsimd(f6g)
                        def f6t(t, it=it, sl4=sl4):
                            if it == 0:
                                c.wait(t, f'{tag}vA', 16)
                                c.wait(t, f'{tag}sv0', c.n(f'{tag}sv0'))
                                c.wait(t, f'{tag}sv1', c.n(f'{tag}sv1'))
                            c.wait(t, f'{tag}l6{sl4}', b_l6[sl4] + 16 * (it // 4 + 1))
                            if it >= 4:
                                c.wait(t, f'{tag}evt', it - 3)
                            c.inc(t.matmul(psA[:, sl4 * 512: sl4 * 512 + 128],
                                           bar[:, 11520 + sl4 * 128: 11520 + (sl4 + 1) * 128],
                                           identb[:], start=True, stop=True), f'{tag}pt', 1)
                        blk.tensor(f6t)
                        def f6s(s, it=it, sl4=sl4, db=db, scaled=scaled):
                            c.wait(s, f'{tag}pt', it + 1)
                            if it >= 2:
                                c.wait(s, f'{tag}st{it % 2}', b_st[it % 2] + 16 * ((it - 2) // 2 + 1))
                            if scaled:
                                c.inc(s.activation(out=bar[:, 12032 + (it % 2) * 128: 12032 + (it % 2 + 1) * 128],
                                                   in_=psA[:, sl4 * 512: sl4 * 512 + 128],
                                                   func=AF.Copy, scale=smallv[:, 20 + db: 21 + db]),
                                      f'{tag}evt', 1)
                            else:
                                c.inc(s.copy(bar[:, 12032 + (it % 2) * 128: 12032 + (it % 2 + 1) * 128],
                                             psA[:, sl4 * 512: sl4 * 512 + 128]), f'{tag}evt', 1)
                        blk.scalar(f6s)
                        def f6sy(sy, it=it, fb=fb, db=db, wdst=wdst):
                            c.wait(sy, f'{tag}evt', it + 1)
                            c.inc(sy.dma_start(out=wdst[db * 128:(db + 1) * 128, fb * 128:(fb + 1) * 128],
                                               in_=bar[:, 12032 + (it % 2) * 128: 12032 + (it % 2 + 1) * 128]),
                                  f'{tag}st{it % 2}', 16)
                        blk.sync(f6sy)
            def fend(sy):
                c.wait(sy, f'{tag}st0')
                c.wait(sy, f'{tag}st1')
                c.wait(sy, f'{tag}sv0')
                c.wait(sy, f'{tag}sv1')
                c.wait(sy, f'{tag}sg')
                c.inc(sy.nop(), f'{tag}done', 1)
            blk.sync(fend)

        def ph_wout(b, tag, dtag):
            B = {s: c.n(f'{tag}{s}') for s in
                 (['pe', 'lx'] + [f'lw{i}' for i in range(4)] + [f'ly{i}' for i in range(8)])}
            guards = {}
            def gwait(eng, key):
                g_ = guards.get(key)
                if g_:
                    c.wait(eng, g_[0], g_[1])
            def gset(key, sem):
                guards[key] = (sem, c.n(sem))
            for tp in range(8):
                def fx(sy, tp=tp):
                    if tp == 0:
                        c.wait(sy, f'{dtag}done')
                        c.wait(sy, f'{tag}lx', B['lx'])
                    gwait(sy, ('xload',))
                    for tj in range(2):
                        tt = 2 * tp + tj
                        c.inc(sy.dma_start(out=z1tok[:, tj * 2052: tj * 2052 + 2048],
                                           in_=xin[b * S + tt * 128: b * S + (tt + 1) * 128, :]),
                              f'{tag}lx', 16)
                blk.sync(fx)
                def fw(gp, tp=tp):
                    if tp == 0:
                        c.wait(gp, f'{dtag}done')
                        c.wait(gp, 'prAdone', 1)
                    for vb in range(32):
                        if tp == 0 and vb < 4:
                            c.wait(gp, f'{tag}lw{vb % 4}', B[f'lw{vb % 4}'])
                        iw = tp * 32 + vb
                        if iw >= 4:
                            c.wait(gp, f'{tag}pe', B['pe'] + iw - 3)
                        c.inc(gp.dma_start(out=bar[:, (vb % 4) * 2048:(vb % 4 + 1) * 2048],
                                           in_=woute_d[vb * 128:(vb + 1) * 128, :]),
                              f'{tag}lw{vb % 4}', 16)
                blk.gpsimd(fw)
                def fy(sy, tp=tp):
                    for vb in range(32):
                        if tp == 0 and vb < 8:
                            c.wait(sy, f'{tag}ly{vb % 8}', B[f'ly{vb % 8}'])
                        iw = tp * 32 + vb
                        if iw >= 8:
                            c.wait(sy, f'{tag}pe', B['pe'] + iw - 7)
                        c.inc(sy.dma_start(out=bar[:, 10240 + (vb % 8) * 256: 10240 + (vb % 8 + 1) * 256],
                                           in_=yT_d[vb * 128:(vb + 1) * 128, tp * 256:(tp + 1) * 256]),
                              f'{tag}ly{vb % 8}', 16)
                blk.sync(fy)
                def ft(t, tp=tp):
                    if tp == 0:
                        for nm in ('d5vs', 'd5sy', 'd5evV', 'd5evS', 'd5sq',
                                   'd5so', 'd5vo', 'd5vy', 'd5vm', 'd5sm'):
                            c.wait(t, nm)
                    gwait(t, ('psA',))
                    gwait(t, ('psB',))
                    for vb in range(32):
                        c.wait(t, f'{tag}lw{vb % 4}', B[f'lw{vb % 4}'] + 16 * (tp * 8 + vb // 4 + 1))
                        c.wait(t, f'{tag}ly{vb % 8}', B[f'ly{vb % 8}'] + 16 * (tp * 4 + vb // 8 + 1))
                        for tj in range(2):
                            ps = psA if tj == 0 else psB
                            lhs = bar[:, 10240 + (vb % 8) * 256 + tj * 128: 10240 + (vb % 8) * 256 + (tj + 1) * 128]
                            for dgi in range(4):
                                ins = t.matmul(ps[:, dgi * 512:(dgi + 1) * 512], lhs,
                                               bar[:, (vb % 4) * 2048 + dgi * 512:(vb % 4) * 2048 + (dgi + 1) * 512],
                                               start=(vb == 0), stop=(vb == 31))
                        c.inc(ins, f'{tag}pe', 1)
                blk.tensor(ft)
                pe_n = c.n(f'{tag}pe')
                for tj in range(2):
                    ps = psA if tj == 0 else psB
                    def fs(s, tp=tp, tj=tj, ps=ps, pe_n=pe_n):
                        if tp == 0 and tj == 0:
                            for nm in ('d5vm', 'd5vo', 'd5vy', 'd5vs'):
                                c.wait(s, nm)
                        c.wait(s, f'{tag}pe', pe_n)
                        for bk in range(4):
                            c.inc(s.activation(out=obuf[:, tj * 2048 + bk * 512: tj * 2048 + (bk + 1) * 512],
                                               in_=ps[:, bk * 512:(bk + 1) * 512], func=AF.Square,
                                               accum_out=cols[:, 104 + tj * 4 + bk: 105 + tj * 4 + bk]),
                                  f'{tag}sq', 1)
                    blk.scalar(fs)
                    def fv(v, tp=tp, tj=tj, ps=ps, pe_n=pe_n):
                        if tp == 0 and tj == 0:
                            for nm in ('d5sm', 'd5sq', 'd5so', 'd5sy', 'd5evS'):
                                c.wait(v, nm)
                        c.wait(v, f'{tag}pe', pe_n)
                        base = c.n(f'{tag}lnv')
                        for bk in range(4):
                            c.inc(v.reduce_sum(cols[:, 96 + tj * 4 + bk: 97 + tj * 4 + bk],
                                               ps[:, bk * 512:(bk + 1) * 512], axis=AXX),
                                  f'{tag}lnv', 1)
                        c.wait(v, f'{tag}lnv')
                        c.inc(v.tensor_tensor(out=cols[:, 96 + tj * 4: 97 + tj * 4],
                                              in0=cols[:, 96 + tj * 4: 97 + tj * 4],
                                              in1=cols[:, 97 + tj * 4: 98 + tj * 4], op=ALU.add), f'{tag}lnv', 1)
                        c.wait(v, f'{tag}lnv')
                        c.inc(v.tensor_tensor(out=cols[:, 98 + tj * 4: 99 + tj * 4],
                                              in0=cols[:, 98 + tj * 4: 99 + tj * 4],
                                              in1=cols[:, 99 + tj * 4: 100 + tj * 4], op=ALU.add), f'{tag}lnv', 1)
                        c.wait(v, f'{tag}lnv')
                        c.inc(v.tensor_tensor(out=cols[:, 96 + tj * 4: 97 + tj * 4],
                                              in0=cols[:, 96 + tj * 4: 97 + tj * 4],
                                              in1=cols[:, 98 + tj * 4: 99 + tj * 4], op=ALU.add), f'{tag}lnv', 1)
                        c.wait(v, f'{tag}sq', c.n(f'{tag}sq'))
                        for bk in range(1, 4):
                            c.wait(v, f'{tag}lnv')
                            c.inc(v.tensor_tensor(out=cols[:, 104 + tj * 4: 105 + tj * 4],
                                                  in0=cols[:, 104 + tj * 4: 105 + tj * 4],
                                                  in1=cols[:, 104 + tj * 4 + bk: 105 + tj * 4 + bk],
                                                  op=ALU.add), f'{tag}lnv', 1)
                        mu = cols[:, 112 + tj: 113 + tj]
                        var = cols[:, 114 + tj: 115 + tj]
                        musq = cols[:, 116 + tj: 117 + tj]
                        c.wait(v, f'{tag}lnv')
                        c.inc(v.tensor_scalar(out=mu, in0=cols[:, 96 + tj * 4: 97 + tj * 4],
                                              scalar1=1.0 / D, scalar2=None, op0=ALU.mult), f'{tag}lnv', 1)
                        c.wait(v, f'{tag}lnv')
                        c.inc(v.tensor_tensor(out=musq, in0=mu, in1=mu, op=ALU.mult), f'{tag}lnv', 1)
                        c.wait(v, f'{tag}lnv')
                        c.inc(v.scalar_tensor_tensor(out=var, in0=cols[:, 104 + tj * 4: 105 + tj * 4],
                                                     scalar=1.0 / D, in1=musq,
                                                     op0=ALU.mult, op1=ALU.subtract), f'{tag}lnv', 1)
                    blk.vector(fv)
                    def fs2(s, tp=tp, tj=tj):
                        c.wait(s, f'{tag}lnv', c.n(f'{tag}lnv'))
                        c.inc(s.activation(out=cols[:, 118 + tj: 119 + tj],
                                           in_=cols[:, 114 + tj: 115 + tj],
                                           func=AF.Sqrt, bias=smallv[:, 40:41]), f'{tag}sd', 1)
                    blk.scalar(fs2)
                    def fv2(v, tp=tp, tj=tj, ps=ps):
                        c.wait(v, f'{tag}sd', c.n(f'{tag}sd'))
                        rs = cols[:, 120 + tj: 121 + tj]
                        mu = cols[:, 112 + tj: 113 + tj]
                        murs = cols[:, 122 + tj: 123 + tj]
                        c.inc(v.reciprocal(rs, cols[:, 118 + tj: 119 + tj]), f'{tag}lnw', 1)
                        c.wait(v, f'{tag}lnw')
                        c.inc(v.tensor_tensor(out=murs, in0=mu, in1=rs, op=ALU.mult), f'{tag}lnw', 1)
                        c.wait(v, f'{tag}lnw')
                        c.wait(v, f'{tag}lx', B['lx'] + 16 * 2 * (tp + 1))
                        gwait(v, ('stg', tj))
                        c.inc(v.tensor_scalar(out=stage[:, tj * 2048:(tj + 1) * 2048],
                                              in0=z1tok[:, tj * 2052: tj * 2052 + 2048],
                                              scalar1=1.0, scalar2=murs,
                                              op0=ALU.mult, op1=ALU.subtract), f'{tag}lnw', 1)
                        for bk in range(4):
                            c.wait(v, f'{tag}lnw')
                            c.inc(v.scalar_tensor_tensor(out=stage[:, tj * 2048 + bk * 512: tj * 2048 + (bk + 1) * 512],
                                                         in0=ps[:, bk * 512:(bk + 1) * 512],
                                                         scalar=rs,
                                                         in1=stage[:, tj * 2048 + bk * 512: tj * 2048 + (bk + 1) * 512],
                                                         op0=ALU.mult, op1=ALU.add), f'{tag}lnw', 1)
                    blk.vector(fv2)
                    def fst(sy, tp=tp, tj=tj):
                        c.wait(sy, f'{tag}lnw', c.n(f'{tag}lnw'))
                        tt = 2 * tp + tj
                        if tp == 0:
                            c.wait(sy, f'{tag}x2' if tj == 0 else f'{tag}x2b',
                                   c.n(f'{tag}x2' if tj == 0 else f'{tag}x2b'))
                        c.inc(sy.dma_start(out=x2_d[b * S + tt * 128: b * S + (tt + 1) * 128, :],
                                           in_=stage[:, tj * 2048:(tj + 1) * 2048]),
                              f'{tag}x2' if tj == 0 else f'{tag}x2b', 16)
                    blk.sync(fst)
                    gset(('stg', tj), f'{tag}x2' if tj == 0 else f'{tag}x2b')
                    gset(('psA',) if tj == 0 else ('psB',), f'{tag}lnw')
                gset(('xload',), f'{tag}lnw')
            def fend(sy):
                c.wait(sy, f'{tag}x2')
                c.wait(sy, f'{tag}x2b')
                c.inc(sy.nop(), f'{tag}done', 1)
            blk.sync(fend)

        def ph_qg(b, tag, z2tag, wtag):
            B = {s: c.n(f'{tag}{s}') for s in ('lg0', 'lg1', 'lv0', 'pe',
                                               'pv', 'evS', 'evV', 'eva',
                                               'so0', 'so1', 'on')}
            z2tevF = c.n(f'{z2tag}tev')
            def fva(sy):
                c.wait(sy, f'{tag}lv0', B['lv0'])
                c.wait(sy, 'prBdone', 1)
                c.wait(sy, f'{wtag}pe')
                for nm in ('lw0', 'lw1', 'lw2', 'lw3'):
                    c.wait(sy, f'{wtag}{nm}')
                for d in range(16):
                    c.inc(sy.dma_start(out=bar[:, 4352 + d * 256: 4352 + (d + 1) * 256],
                                       in_=va_d[d * 128:(d + 1) * 128, :]),
                          f'{tag}lv0', 16)
            blk.sync(fva)
            def fon(g):
                c.wait(g, f'{wtag}pe')
                if 'a5' in XB:
                    c.wait(g, 'a5pt', XB['a5']['pt'])
                for st5 in range(16):
                    for h in range(16):
                        c.inc(g.memset(bar[:, st5 * 272 + h * 17 + 16: st5 * 272 + h * 17 + 17], 1.0),
                              f'{tag}on', 1)
            blk.gpsimd(fon)
            def fpv(t):
                c.wait(t, f'{z2tag}tev', z2tevF)
                for nm in ('sq', 'lnv', 'lnw'):
                    c.wait(t, f'{wtag}{nm}')
                c.wait(t, f'{tag}lv0', B['lv0'] + 256)
                for st5 in range(16):
                    if st5 >= 2:
                        c.wait(t, f'{tag}eva', B['eva'] + st5 - 1)
                    for d in range(16):
                        ins = t.matmul(psA[:, (st5 % 2) * 512:(st5 % 2) * 512 + 256],
                                       actT[:, d * S + st5 * 128: d * S + (st5 + 1) * 128],
                                       bar[:, 4352 + d * 256: 4352 + (d + 1) * 256],
                                       start=(d == 0), stop=(d == 15))
                    c.inc(ins, f'{tag}pv', 1)
            blk.tensor(fpv)
            def fpa(s):
                for st5 in range(16):
                    c.wait(s, f'{tag}pv', B['pv'] + st5 + 1)
                    if st5 == 0:
                        c.wait(s, f'{wtag}pe')
                        for nm in ('lw0', 'lw1', 'lw2', 'lw3'):
                            c.wait(s, f'{wtag}{nm}')
                        if 'a5' in XB:
                            c.wait(s, 'a5pt', XB['a5']['pt'])
                    for h in range(16):
                        ins = s.copy(bar[:, st5 * 272 + h * 17: st5 * 272 + h * 17 + 16],
                                     psA[:, (st5 % 2) * 512 + h * 16:(st5 % 2) * 512 + (h + 1) * 16])
                    c.inc(ins, f'{tag}eva', 1)
            blk.scalar(fpa)
            def flgso(sy):
                c.wait(sy, f'{tag}lg0', B['lg0'])
                c.wait(sy, f'{tag}lg1', B['lg1'])
                c.wait(sy, f'{tag}pv', B['pv'] + 16)
                for dp in range(17):
                    if dp < 16:
                        par = dp % 2
                        if dp >= 2:
                            c.wait(sy, f'{tag}pe', B['pe'] + dp - 1)
                            c.wait(sy, f'{tag}lg{par}', B[f'lg{par}'] + 256 * (dp // 2))
                        for d in range(16):
                            c.inc(sy.dma_start(
                                out=bar[:, 4352 + par * 2048 + d * 128: 4352 + par * 2048 + (d + 1) * 128],
                                in_=g_d[d * 128:(d + 1) * 128, dp * 128:(dp + 1) * 128]),
                                  f'{tag}lg{par}', 16)
                    if dp >= 1:
                        j = dp - 1
                        jp = j % 2
                        c.wait(sy, f'{tag}evS' if jp == 0 else f'{tag}evV',
                               B['evS' if jp == 0 else 'evV'] + 4 * (j // 2 + 1))
                        if j < 2:
                            c.wait(sy, f'{tag}so{jp}', B[f'so{jp}'])
                            if 'a5' in XB:
                                for nm in ('lq0', 'lq1'):
                                    c.wait(sy, f'a5{nm}', XB['a5'][nm])
                        c.inc(sy.dma_start(out=qgt_d[j * 128:(j + 1) * 128, :],
                                           in_=wring[:, jp * 2048:(jp + 1) * 2048]),
                              f'{tag}so{jp}', 16)
            blk.sync(flgso)
            def fpe(t):
                for dp in range(16):
                    par = dp % 2
                    ps = psA if par == 0 else psB
                    if dp == 0:
                        c.wait(t, f'{tag}eva', B['eva'] + 16)
                    if dp >= 2:
                        c.wait(t, f'{tag}evS' if par == 0 else f'{tag}evV',
                               B['evS' if par == 0 else 'evV'] + 4 * ((dp - 2) // 2 + 1))
                    c.wait(t, f'{tag}lg{par}', B[f'lg{par}'] + 256 * (dp // 2 + 1))
                    for d in range(16):
                        gt = bar[:, 4352 + par * 2048 + d * 128: 4352 + par * 2048 + (d + 1) * 128]
                        for gi5 in range(4):
                            ins = t.matmul(ps[:, gi5 * 512:(gi5 + 1) * 512], gt,
                                           actT[:, d * S + gi5 * 512: d * S + (gi5 + 1) * 512],
                                           start=(d == 0), stop=(d == 15))
                    c.inc(ins, f'{tag}pe', 1)
            blk.tensor(fpe)
            def fes(s):
                for dp in range(0, 16, 2):
                    c.wait(s, f'{tag}pe', B['pe'] + dp + 1)
                    if dp == 0:
                        c.wait(s, 'p2lw0')
                        c.wait(s, 'p2lw1')
                    if dp >= 2:
                        c.wait(s, f'{tag}so0', B['so0'] + 16 * (dp // 2))
                    for gi5 in range(4):
                        c.inc(s.copy(wring[:, gi5 * 512:(gi5 + 1) * 512],
                                     psA[:, gi5 * 512:(gi5 + 1) * 512]), f'{tag}evS', 1)
            blk.scalar(fes)
            def fev(v):
                for dp in range(1, 16, 2):
                    c.wait(v, f'{tag}pe', B['pe'] + dp + 1)
                    if dp == 1:
                        c.wait(v, 'p2lw0')
                        c.wait(v, 'p2lw1')
                    if dp >= 2:
                        c.wait(v, f'{tag}so1', B['so1'] + 16 * (dp // 2))
                    for gi5 in range(4):
                        c.inc(v.tensor_copy(wring[:, 2048 + gi5 * 512: 2048 + (gi5 + 1) * 512],
                                            psB[:, gi5 * 512:(gi5 + 1) * 512]), f'{tag}evV', 1)
            blk.vector(fev)
            def fend(sy):
                c.wait(sy, f'{tag}so0')
                c.wait(sy, f'{tag}so1')
                c.wait(sy, f'{tag}eva', B['eva'] + 16)
                if DEV_DBG:
                    c.wait(sy, f'{tag}on', B['on'] + 256)
                    c.inc(sy.dma_start(out=dbgb[:, 0:4096], in_=bar[:, 0:4096]),
                          f'{tag}dbg', 16)
                    c.wait(sy, f'{tag}dbg')
                c.inc(sy.nop(), f'{tag}done', 1)
            blk.sync(fend)

        def ph_attn(b, tag, qtag, z2tag, wtag):
            B = {s: c.n(f'{tag}{s}') for s in
                 ('lq0', 'lq1', 'ps', 'pt', 'ph', 'pmh', 'ex', 'pv', 'th',
                  'me', 'mx', 'sm', 'mh')}
            z2tevF = c.n(f'{z2tag}tev')
            QN = {s: c.n(f'{qtag}{s}') for s in
                  ('pe', 'lg0', 'lg1', 'evS', 'evV', 'eva', 'so0', 'so1', 'on')}
            WNsq = c.n(f'{wtag}sq')
            PRg1, PRg2 = c.n('GD1'), c.n('GD2')
            s128 = smallv[:, 45:46]
            chunks = []
            group_chunks = {}
            pvg = {}
            ex_after_group = {}
            pt_after_group = {}
            exg_ptwait = {}
            ps_after_h = {}
            csl = 0; ps_n = 0; ex_n = 0; pt_n = 0
            last_pt_user = {0: 0, 1: 0}
            for h in range(16):
                for j in range(16):
                    exg_ptwait[(h, j)] = last_pt_user[j % 2]
                    gl = []
                    gi0 = j // 4
                    for gi in range(gi0, 4):
                        if gi == gi0:
                            cstart, N = j * 128, (4 - j % 4) * 128
                        else:
                            cstart, N = gi * 512, 512
                        ps_n += 1
                        ent = (h, j, cstart, N, csl % 6, gi == gi0, csl - 5, ps_n)
                        chunks.append(ent)
                        gl.append(ent)
                        csl += 1
                    group_chunks[(h, j)] = gl
                    ex_after_group[(h, j)] = ex_n + len(gl)
                    ex_n += len(gl)
                    pvg[(h, j)] = [(i, pt_n + k + 1) for k, i in enumerate(range(j, 16))]
                    pt_n += 16 - j
                    pt_after_group[(h, j)] = pt_n
                    last_pt_user[j % 2] = pt_n
                ps_after_h[h] = ps_n
            n_chunks = len(chunks)

            def cslot(slot, n0, n1, p0=0, p1=128):
                ps = psA if slot < 4 else psB
                sb = (slot % 4) * 512
                return ps[p0:p1, sb + n0: sb + n1]

            def fpe(t):
                for i in range(16):
                    if i == 0:
                        c.wait(t, f'{z2tag}tev', z2tevF)
                        c.wait(t, f'{qtag}evS', QN['evS'])
                        c.wait(t, f'{qtag}evV', QN['evV'])
                        c.wait(t, f'{qtag}eva', QN['eva'])
                        c.wait(t, f'{qtag}on', QN['on'])
                        c.wait(t, f'{tag}sm', B['sm'])
                    if i >= 4:
                        c.wait(t, f'{tag}pv', B['pv'] + i - 3)
                    for d in range(16):
                        ins = t.matmul(psA[:, (i % 4) * 512:(i % 4) * 512 + 16],
                                       actT[:, d * S + i * 128: d * S + (i + 1) * 128],
                                       bar[:, 9216 + d * 16: 9216 + (d + 1) * 16],
                                       start=(d == 0), stop=(d == 15))
                    c.inc(ins, f'{tag}ph', 1)
                for h in range(16):
                    for j in range(16):
                        if (h, j) == (0, 0):
                            c.wait(t, f'{tag}pv', B['pv'] + 16)
                        if j == 0:
                            c.wait(t, f'{tag}lq{h % 2}', B[f'lq{h % 2}'] + 16 * (h // 2 + 1))
                        for (hh, jj, cstart, N, slot, msk, exg, psn) in group_chunks[(h, j)]:
                            if exg > 0:
                                c.wait(t, f'{tag}ex', B['ex'] + exg)
                            sAP = cslot(slot, 0, N)
                            ins = t.matmul(sAP,
                                           actT[:, h * S + j * 128: h * S + (j + 1) * 128],
                                           wring[:, (h % 2) * 2048 + cstart: (h % 2) * 2048 + cstart + N],
                                           start=True, stop=not msk)
                            if msk:
                                ins = t.matmul(cslot(slot, 0, 128), identb[:],
                                               maskUbS[:], start=False, stop=True)
                            c.inc(ins, f'{tag}ps', 1)
                        pvl = [(h, j - 1)] if j >= 1 else []
                        if j == 15:
                            pvl.append((h, 15))
                        for (ph_, pj) in pvl:
                            if pj == 0 and ph_ >= 2:
                                c.wait(t, f'{tag}sm', B['sm'] + 16 * (ph_ - 1))
                            c.wait(t, f'{tag}ex', B['ex'] + ex_after_group[(ph_, pj)])
                            base = 1536 if ph_ % 2 == 0 else 1024
                            for (i, ptn) in pvg[(ph_, pj)]:
                                ins = t.matmul(psB[:, base + i * 32: base + i * 32 + 17],
                                               bar[:, 4352 + (pj % 2) * 2048 + i * 128: 4352 + (pj % 2) * 2048 + (i + 1) * 128],
                                               bar[:, pj * 272 + ph_ * 17: pj * 272 + (ph_ + 1) * 17],
                                               start=(pj == 0 and i == 0),
                                               stop=(pj == 15 and i == 15))
                                c.inc(ins, f'{tag}pt', 1)
                for i in range(16):
                    if i == 0:
                        c.wait(t, f'{tag}ex', B['ex'] + ex_n)
                    if i >= 6:
                        c.wait(t, f'{tag}me', B['me'] + 2 * (i - 6) + 2)
                    c.wait(t, f'{tag}me', B['me'] + 2 * i + 1)
                    slot = i % 6
                    ins = t.matmul(cslot(slot, 0, 128, 0, 16),
                                   bar[:, 8704 + i * 16: 8704 + (i + 1) * 16],
                                   identb[:], start=True, stop=True)
                    c.inc(ins, f'{tag}pmh', 1)
            blk.tensor(fpe)
            def fsc(s):
                for i in range(16):
                    if i == 0:
                        c.wait(s, f'{tag}mh', B['mh'])
                    c.wait(s, f'{tag}ph', B['ph'] + i + 1)
                    c.inc(s.copy(obuf[:, i * 16:(i + 1) * 16],
                                 psA[:, (i % 4) * 512:(i % 4) * 512 + 16]), f'{tag}pv', 1)
                exd = 0
                for (h, j, cstart, N, slot, msk, exg, psn) in chunks:
                    if exd == 0:
                        c.wait(s, f'{qtag}lg0', QN['lg0'])
                        c.wait(s, f'{qtag}lg1', QN['lg1'])
                        c.wait(s, f'{qtag}pe', QN['pe'])
                    if cstart == j * 128 and exg_ptwait[(h, j)] > 0:
                        c.wait(s, f'{tag}pt', B['pt'] + exg_ptwait[(h, j)])
                    c.wait(s, f'{tag}ps', B['ps'] + psn)
                    c.inc(s.activation(
                        out=bar[:, 4352 + (j % 2) * 2048 + cstart: 4352 + (j % 2) * 2048 + cstart + N],
                        in_=cslot(slot, 0, N), func=AF.Exp, scale=s128), f'{tag}ex', 1)
                    exd += 1
                for i in range(16):
                    c.wait(s, f'{tag}sm', B['sm'] + 240 + i + 1)
                    c.inc(s.activation(out=obuf[:, 512 + i * 16: 512 + (i + 1) * 16],
                                       in_=obuf[:, 256 + i * 16: 256 + (i + 1) * 16],
                                       func=AF.Tanh), f'{tag}th', 1)
                for i in range(16):
                    c.wait(s, f'{tag}mh', B['mh'] + i + 1)
                    if i == 0:
                        c.wait(s, 'GD1', PRg1)
                        c.wait(s, 'GD2', PRg2)
                    c.inc(s.copy(bar[:, 8704 + i * 16: 8704 + (i + 1) * 16],
                                 obuf[:, 768 + i * 16: 768 + (i + 1) * 16]), f'{tag}me', 1)
                    c.wait(s, f'{tag}pmh', B['pmh'] + i + 1)
                    c.inc(s.copy(mhT_s[0:16, i * 128:(i + 1) * 128],
                                 cslot(i % 6, 0, 128, 0, 16)), f'{tag}me', 1)
            blk.scalar(fsc)
            def fve(v):
                for h in range(16):
                    base = 1536 if h % 2 == 0 else 1024
                    if h == 0:
                        c.wait(v, f'{wtag}sq', WNsq)
                        c.wait(v, f'{tag}th', B['th'])
                        c.wait(v, f'{tag}me', B['me'])
                    c.wait(v, f'{tag}pt', B['pt'] + pt_after_group[(h, 15)])
                    for i in range(16):
                        rc = obuf[:, 1040:1041]
                        c.wait(v, f'{tag}sm')
                        c.inc(v.reciprocal(rc, psB[:, base + i * 32 + 16: base + i * 32 + 17]),
                              f'{tag}mx', 1)
                        c.wait(v, f'{tag}mx')
                        if h == 0:
                            c.inc(v.tensor_scalar(out=obuf[:, 256 + i * 16: 256 + (i + 1) * 16],
                                                  in0=psB[:, base + i * 32: base + i * 32 + 16],
                                                  scalar1=rc, scalar2=None, op0=ALU.mult),
                                  f'{tag}sm', 1)
                        else:
                            c.inc(v.scalar_tensor_tensor(
                                out=obuf[:, 256 + i * 16: 256 + (i + 1) * 16],
                                in0=psB[:, base + i * 32: base + i * 32 + 16],
                                scalar=rc,
                                in1=obuf[:, 256 + i * 16: 256 + (i + 1) * 16],
                                op0=ALU.mult, op1=ALU.add), f'{tag}sm', 1)
                for i in range(16):
                    c.wait(v, f'{tag}th', B['th'] + i + 1)
                    if i == 0:
                        c.wait(v, f'{tag}pv', B['pv'] + 16)
                    c.inc(v.tensor_tensor(out=obuf[:, 768 + i * 16: 768 + (i + 1) * 16],
                                          in0=obuf[:, 512 + i * 16: 512 + (i + 1) * 16],
                                          in1=obuf[:, i * 16:(i + 1) * 16],
                                          op=ALU.mult), f'{tag}mh', 1)
            blk.vector(fve)
            def fsy(sy):
                for h in range(16):
                    if h < 2:
                        c.wait(sy, f'{tag}lq{h % 2}', B[f'lq{h % 2}'])
                        c.wait(sy, f'{qtag}so0', QN['so0'])
                        c.wait(sy, f'{qtag}so1', QN['so1'])
                        c.wait(sy, f'{tag}ex', B['ex'])
                        c.wait(sy, f'{tag}mx', B['mx'])
                    else:
                        c.wait(sy, f'{tag}ps', B['ps'] + ps_after_h[h - 2])
                        c.wait(sy, f'{tag}lq{h % 2}', B[f'lq{h % 2}'] + 16 * (h // 2))
                    c.inc(sy.dma_start(out=wring[:, (h % 2) * 2048:(h % 2 + 1) * 2048],
                                       in_=qgt_d[h * 128:(h + 1) * 128, :]),
                          f'{tag}lq{h % 2}', 16)
            blk.sync(fsy)
            def fend(sy):
                for nm in ('pt', 'ph', 'pmh', 'ps'):
                    c.wait(sy, f'{tag}{nm}', B[nm] + {'pt': pt_n, 'ph': 16,
                                                      'pmh': 16, 'ps': ps_n}[nm])
                c.wait(sy, f'{tag}me', B['me'] + 32)
                c.wait(sy, f'{tag}sm', B['sm'] + 256)
                c.wait(sy, f'{tag}ex', B['ex'] + ex_n)
                if DEV_DBG:
                    c.inc(sy.dma_start(out=dbgf[:, 0:1041], in_=obuf[:, 0:1041]),
                          f'{tag}dbg', 16)
                    c.inc(sy.dma_start(out=dbgb[0:16, 0:2048], in_=mhT_s[0:16, :]),
                          f'{tag}dbg', 16)
                    c.wait(sy, f'{tag}dbg')
                c.inc(sy.nop(), f'{tag}done', 1)
            blk.sync(fend)

        def ph_mlp(b, tag, atag, qtag):
            B = {s: c.n(f'{tag}{s}') for s in
                 ('lw0', 'lw1', 'lu0', 'lu1', 'pe', 'pu', 'sg', 'mv', 'sm_')}
            AN = {s: c.n(f'{atag}{s}') for s in
                  ('pt', 'ps', 'pmh', 'ph', 'ex', 'me', 'sm', 'mx', 'pv', 'th')}
            QS = {s: c.n(f'{qtag}{s}') for s in ('so0', 'so1')}
            OL = dict(XB.get('o6', {}))
            for fb in range(64):
                par = fb % 2
                def fgp(g, fb=fb, par=par):
                    if fb == 0:
                        c.wait(g, f'{atag}pt', AN['pt'])
                        c.wait(g, f'{tag}lw0', B['lw0'])
                        c.wait(g, f'{tag}lw1', B['lw1'])
                        c.wait(g, f'{tag}lu0', B['lu0'])
                        c.wait(g, f'{tag}lu1', B['lu1'])
                    if fb >= 2:
                        c.wait(g, f'{tag}pe', B['pe'] + fb - 1)
                        c.wait(g, f'{tag}pu', B['pu'] + fb - 1)
                    for d in range(16):
                        c.inc(g.dma_start(
                            out=bar[:, par * 4096 + d * 128: par * 4096 + (d + 1) * 128],
                            in_=wint_d[d * 128:(d + 1) * 128, fb * 128:(fb + 1) * 128]),
                              f'{tag}lw{par}', 16)
                    for d in range(16):
                        c.inc(g.dma_start(
                            out=bar[:, par * 4096 + 2048 + d * 128: par * 4096 + 2048 + (d + 1) * 128],
                            in_=wupt_d[d * 128:(d + 1) * 128, fb * 128:(fb + 1) * 128]),
                              f'{tag}lu{par}', 16)
                blk.gpsimd(fgp)
                def fpe(t, fb=fb, par=par):
                    if fb == 0:
                        c.wait(t, f'{atag}ex', AN['ex'])
                        c.wait(t, f'{atag}me', AN['me'])
                        c.wait(t, f'{atag}sm', AN['sm'])
                        c.wait(t, f'{atag}mx', AN['mx'])
                        c.wait(t, f'{atag}pv', AN['pv'])
                        c.wait(t, f'{atag}th', AN['th'])
                        c.wait(t, 'cstG', 16)
                    if fb >= 1:
                        c.wait(t, f'{tag}sg', B['sg'] + 4 * fb)
                        c.wait(t, f'{tag}mv', B['mv'] + 4 * fb)
                    c.wait(t, f'{tag}lw{par}', B[f'lw{par}'] + 256 * (fb // 2 + 1))
                    for d in range(16):
                        wt = bar[:, par * 4096 + d * 128: par * 4096 + (d + 1) * 128]
                        for gi in range(4):
                            t.matmul(psA[:, gi * 512:(gi + 1) * 512], wt,
                                     actT[:, d * S + gi * 512: d * S + (gi + 1) * 512],
                                     start=(d == 0), stop=False)
                    for gi in range(4):
                        ins = t.matmul(psA[:, gi * 512:(gi + 1) * 512],
                                       Bs[0:16, fb * 128:(fb + 1) * 128],
                                       mhT_s[0:16, gi * 512:(gi + 1) * 512],
                                       start=False, stop=True)
                    c.inc(ins, f'{tag}pe', 1)
                    c.wait(t, f'{tag}lu{par}', B[f'lu{par}'] + 256 * (fb // 2 + 1))
                    for d in range(16):
                        wt = bar[:, par * 4096 + 2048 + d * 128: par * 4096 + 2048 + (d + 1) * 128]
                        for gi in range(4):
                            ins = t.matmul(psB[:, gi * 512:(gi + 1) * 512], wt,
                                           actT[:, d * S + gi * 512: d * S + (gi + 1) * 512],
                                           start=(d == 0), stop=(d == 15))
                    c.inc(ins, f'{tag}pu', 1)
                blk.tensor(fpe)
                def fsc(s, fb=fb):
                    if fb == 0:
                        c.wait(s, f'{atag}ps', AN['ps'])
                        c.wait(s, f'{qtag}so0', QS['so0'])
                        c.wait(s, f'{qtag}so1', QS['so1'])
                    c.wait(s, f'{tag}pe', B['pe'] + fb + 1)
                    if fb >= 1:
                        c.wait(s, f'{tag}mv', B['mv'] + 4 * fb)
                    for gi in range(4):
                        c.inc(s.activation(out=wring[:, gi * 512:(gi + 1) * 512],
                                           in_=psA[:, gi * 512:(gi + 1) * 512],
                                           func=AF.Silu), f'{tag}sg', 1)
                blk.scalar(fsc)
                def fve(v, fb=fb):
                    if fb == 0:
                        c.wait(v, f'{atag}ps', AN['ps'])
                    c.wait(v, f'{tag}pu', B['pu'] + fb + 1)
                    c.wait(v, f'{tag}sg', B['sg'] + 4 * (fb + 1))
                    if fb >= 1:
                        c.wait(v, f'{tag}sm_', B['sm_'] + 16 * fb)
                    for gi in range(4):
                        c.inc(v.tensor_tensor(out=wring[:, 2048 + gi * 512: 2048 + (gi + 1) * 512],
                                              in0=psB[:, gi * 512:(gi + 1) * 512],
                                              in1=wring[:, gi * 512:(gi + 1) * 512],
                                              op=ALU.mult), f'{tag}mv', 1)
                blk.vector(fve)
                def fsy(sy, fb=fb):
                    if fb == 0 and OL:
                        c.wait(sy, 'o6lm0', OL['lm0'])
                        c.wait(sy, 'o6lm1', OL['lm1'])
                    c.wait(sy, f'{tag}mv', B['mv'] + 4 * (fb + 1))
                    c.inc(sy.dma_start(out=m_d[fb * 128:(fb + 1) * 128, :],
                                       in_=wring[:, 2048:4096]), f'{tag}sm_', 16)
                blk.sync(fsy)
            def fend(sy):
                c.wait(sy, f'{tag}sm_', B['sm_'] + 64 * 16)
                c.inc(sy.nop(), f'{tag}done', 1)
            blk.sync(fend)

        def ph_out(b, tag, mtag, z2tag, wtag, atag):
            B = {s: c.n(f'{tag}{s}') for s in
                 ('lm0', 'lm1', 'lw0', 'lw1', 'pe', 'ev', 'pt', 'ea', 'sq',
                  'lnv', 'sd', 'lnw', 'mus', 'lx', 'so', 'sob')}
            MN = {s: c.n(f'{mtag}{s}') for s in ('pe', 'pu', 'mv', 'sg', 'sm_',
                                                 'lw0', 'lw1', 'lu0', 'lu1')}
            ZN = {s: c.n(f'{z2tag}{s}') for s in ('ld0', 'ld1', 'tp', 'z1')}
            WW = {s: c.n(f'{wtag}{s}') for s in ('pe', 'lnw', 'x2', 'x2b')}
            AW = {s: c.n(f'{atag}{s}') for s in ('sm', 'mh', 'pv', 'me', 'th')}
            asm = [stage[:, 0:2048], stage[:, 2048:4096],
                   obuf[:, 0:2048], obuf[:, 2048:4096]]
            for tg in range(4):
                def flm(sy, tg=tg):
                    if tg == 0:
                        c.wait(sy, f'{mtag}pe', MN['pe'])
                        c.wait(sy, f'{mtag}pu', MN['pu'])
                        c.wait(sy, f'{tag}lm0', B['lm0'])
                        c.wait(sy, f'{tag}lm1', B['lm1'])
                        c.wait(sy, f'{mtag}sm_', MN['sm_'])
                    else:
                        c.wait(sy, f'{tag}pe', B['pe'] + 32 * tg)
                        c.wait(sy, f'{tag}lm{tg % 2}', B[f'lm{tg % 2}'] + 1024 * (tg // 2))
                    for f in range(64):
                        c.inc(sy.dma_start(
                            out=actT[:, f * 512:(f + 1) * 512],
                            in_=m_d[f * 128:(f + 1) * 128, tg * 512:(tg + 1) * 512]),
                              f'{tag}lm{tg % 2}', 16)
                blk.sync(flm)
                for dt in range(16):
                    it = tg * 16 + dt
                    def fgp(g, it=it, dt=dt, tg=tg):
                        if it == 0:
                            c.wait(g, 'prBdone', 1)
                            c.wait(g, f'{mtag}pe', MN['pe'])
                            c.wait(g, f'{mtag}pu', MN['pu'])
                            c.wait(g, f'{tag}lw0', B['lw0'])
                            c.wait(g, f'{tag}lw1', B['lw1'])
                        for fh in range(2):
                            gidx = it * 2 + fh
                            par = gidx % 2
                            if gidx >= 2:
                                c.wait(g, f'{tag}pe', B['pe'] + gidx - 1)
                                c.wait(g, f'{tag}lw{par}', B[f'lw{par}'] + 512 * (gidx // 2))
                            for f in range(fh * 32, fh * 32 + 32):
                                c.inc(g.dma_start(
                                    out=bar[:, par * 4096 + (f % 32) * 128: par * 4096 + (f % 32 + 1) * 128],
                                    in_=wdnt_d[f * 128:(f + 1) * 128, dt * 128:(dt + 1) * 128]),
                                      f'{tag}lw{par}', 16)
                    blk.gpsimd(fgp)
                    def fpe(t, it=it, dt=dt, tg=tg):
                        if it == 0:
                            c.wait(t, f'{mtag}mv', MN['mv'])
                            c.wait(t, f'{mtag}sg', MN['sg'])
                        itp = it - 4
                        if itp >= 0 and itp // 16 == it // 16:
                            self_tp(t, itp)
                        c.wait(t, f'{tag}lm{tg % 2}', B[f'lm{tg % 2}'] + 1024 * (tg // 2 + 1))
                        if it >= 4:
                            c.wait(t, f'{tag}ev', B['ev'] + it - 3)
                        sl = psA[:, (dt % 4) * 512:(dt % 4 + 1) * 512]
                        for fh in range(2):
                            gidx = it * 2 + fh
                            par = gidx % 2
                            c.wait(t, f'{tag}lw{par}', B[f'lw{par}'] + 512 * (gidx // 2 + 1))
                            for f in range(fh * 32, fh * 32 + 32):
                                ins = t.matmul(sl,
                                               bar[:, par * 4096 + (f % 32) * 128: par * 4096 + (f % 32 + 1) * 128],
                                               actT[:, f * 512:(f + 1) * 512],
                                               start=(f == 0), stop=(f == 63))
                            c.inc(ins, f'{tag}pe', 1)
                        if it % 16 == 15:
                            for itp2 in range(it - 3, it + 1):
                                self_tp(t, itp2)
                    def self_tp(t, itp):
                        c.wait(t, f'{tag}ev', B['ev'] + itp + 1)
                        dtp = itp % 16
                        for q in range(4):
                            tr = (itp * 4 + q) % 4
                            if itp * 4 + q >= 4:
                                c.wait(t, f'{tag}ea', B['ea'] + itp * 4 + q - 3)
                            ins = t.matmul(psB[:, tr * 512: tr * 512 + 128],
                                           bar[:, 10240 + (dtp % 4) * 512 + q * 128: 10240 + (dtp % 4) * 512 + (q + 1) * 128],
                                           identb[:], start=True, stop=True)
                            c.inc(ins, f'{tag}pt', 1)
                    blk.tensor(fpe)
                    def fsc(s, it=it, dt=dt):
                        if it == 0:
                            c.wait(s, f'{wtag}pe', WW['pe'])
                            c.wait(s, f'{mtag}mv', MN['mv'])
                        itl = [it - 4] if (it >= 4 and (it - 4) // 16 == it // 16) else []
                        for itp in itl:
                            dtp = itp % 16
                            tgp = itp // 16
                            for q in range(4):
                                if itp == 0 and q == 0:
                                    c.wait(s, f'{atag}sm', AW['sm'])
                                    c.wait(s, f'{atag}mh', AW['mh'])
                                    c.wait(s, f'{wtag}lnw', WW['lnw'])
                                    c.wait(s, f'{wtag}x2', WW['x2'])
                                    c.wait(s, f'{wtag}x2b', WW['x2b'])
                                    c.wait(s, f'{z2tag}ld0', ZN['ld0'])
                                    c.wait(s, f'{z2tag}ld1', ZN['ld1'])
                                    c.wait(s, f'{z2tag}z1', ZN['z1'])
                                if tgp > 0 and q == 0 and dtp == 0:
                                    c.wait(s, f'{tag}so', B['so'] + 32 * tgp)
                                    c.wait(s, f'{tag}sob', B['sob'] + 32 * tgp)
                                c.wait(s, f'{tag}pt', B['pt'] + itp * 4 + q + 1)
                                tr = (itp * 4 + q) % 4
                                c.inc(s.copy(asm[q][:, dtp * 128:(dtp + 1) * 128],
                                             psB[:, tr * 512: tr * 512 + 128]),
                                      f'{tag}ea', 1)
                        c.wait(s, f'{tag}pe', B['pe'] + 2 * (it + 1))
                        if it >= 4:
                            c.wait(s, f'{tag}pt', B['pt'] + 4 * (it - 3))
                        c.inc(s.copy(bar[:, 10240 + (dt % 4) * 512: 10240 + (dt % 4 + 1) * 512],
                                     psA[:, (dt % 4) * 512:(dt % 4 + 1) * 512]),
                              f'{tag}ev', 1)
                        if it % 16 == 15:
                            for itp in range(it - 3, it + 1):
                                dtp = itp % 16
                                for q in range(4):
                                    c.wait(s, f'{tag}pt', B['pt'] + itp * 4 + q + 1)
                                    tr = (itp * 4 + q) % 4
                                    c.inc(s.copy(asm[q][:, dtp * 128:(dtp + 1) * 128],
                                                 psB[:, tr * 512: tr * 512 + 128]),
                                          f'{tag}ea', 1)
                    blk.scalar(fsc)
                for q in range(4):
                    tt = tg * 4 + q
                    def fx(sy, tg=tg, q=q, tt=tt):
                        if tg == 0 and q < 2:
                            c.wait(sy, f'{z2tag}tp', ZN['tp'])
                            c.wait(sy, f'{z2tag}z1', ZN['z1'])
                            c.wait(sy, f'{tag}lx', B['lx'])
                        else:
                            c.wait(sy, f'{tag}lnw', c.n(f'{tag}lnw'))
                        c.inc(sy.dma_start(out=z1tok[:, (q % 2) * 2052: (q % 2) * 2052 + 2048],
                                           in_=x2_d[b * S + tt * 128:b * S + (tt + 1) * 128, :]),
                              f'{tag}lx', 16)
                    blk.sync(fx)
                    def fv1(v, tg=tg, q=q, tt=tt):
                        if tg == 0 and q == 0:
                            c.wait(v, f'{atag}me', AW['me'])
                            c.wait(v, f'{atag}th', AW['th'])
                        c.wait(v, f'{tag}ea', B['ea'] + (tg * 16 + 15) * 4 + q + 1)
                        c.wait(v, f'{tag}lx', B['lx'] + 16 * (4 * tg + q + 1))
                        mus = cols[:, 1200 + tt: 1201 + tt]
                        c.inc(v.tensor_tensor(out=mus, in0=cols[:, 32 + tt: 33 + tt],
                                              in1=cols[:, 64 + tt: 65 + tt], op=ALU.mult),
                              f'{tag}mus', 1)
                        c.wait(v, f'{tag}mus')
                        c.inc(v.scalar_tensor_tensor(out=asm[q], in0=z1tok[:, (q % 2) * 2052: (q % 2) * 2052 + 2048],
                                                     scalar=cols[:, 64 + tt: 65 + tt],
                                                     in1=asm[q], op0=ALU.mult, op1=ALU.add),
                              f'{tag}lnv', 1)
                        c.wait(v, f'{tag}lnv')
                        c.inc(v.tensor_scalar(out=asm[q], in0=asm[q], scalar1=1.0,
                                              scalar2=mus, op0=ALU.mult, op1=ALU.subtract),
                              f'{tag}lnv', 1)
                        base = 1024 + tt * 8
                        for bk in range(4):
                            c.wait(v, f'{tag}lnv')
                            c.inc(v.reduce_sum(cols[:, base + bk: base + bk + 1],
                                               asm[q][:, bk * 512:(bk + 1) * 512], axis=AXX),
                                  f'{tag}lnv', 1)
                        for st in (1, 2):
                            c.wait(v, f'{tag}lnv')
                            c.inc(v.tensor_tensor(out=cols[:, base: base + 1],
                                                  in0=cols[:, base: base + 1],
                                                  in1=cols[:, base + st: base + st + 1],
                                                  op=ALU.add), f'{tag}lnv', 1)
                    blk.vector(fv1)
                    def fs1(s, q=q, tt=tt):
                        c.wait(s, f'{tag}lnv', c.n(f'{tag}lnv'))
                        base = 1024 + tt * 8
                        for bk in range(4):
                            c.wait(s, f'{tag}sq')
                            c.inc(s.activation(out=cols[:, 1536:2048],
                                               in_=asm[q][:, bk * 512:(bk + 1) * 512],
                                               func=AF.Square,
                                               accum_out=cols[:, base + 4 + bk: base + 5 + bk]),
                                  f'{tag}sq', 1)
                    blk.scalar(fs1)
                    def fv2(v, q=q, tt=tt):
                        c.wait(v, f'{tag}sq', c.n(f'{tag}sq'))
                        base = 1024 + tt * 8
                        c.inc(v.tensor_tensor(out=cols[:, base + 3: base + 4],
                                              in0=cols[:, base: base + 1],
                                              in1=cols[:, base + 3: base + 4], op=ALU.add),
                              f'{tag}lnv', 1)
                        for bk in range(1, 4):
                            c.wait(v, f'{tag}lnv')
                            c.inc(v.tensor_tensor(out=cols[:, base + 4: base + 5],
                                                  in0=cols[:, base + 4: base + 5],
                                                  in1=cols[:, base + 4 + bk: base + 5 + bk],
                                                  op=ALU.add), f'{tag}lnv', 1)
                        mu4 = cols[:, 1264 + tt: 1265 + tt]
                        musq = cols[:, 1296 + tt: 1297 + tt]
                        var4 = cols[:, 1312 + tt: 1313 + tt]
                        c.wait(v, f'{tag}lnv')
                        c.inc(v.tensor_scalar(out=mu4, in0=cols[:, base + 3: base + 4],
                                              scalar1=1.0 / D, scalar2=None, op0=ALU.mult),
                              f'{tag}lnv', 1)
                        c.wait(v, f'{tag}lnv')
                        c.inc(v.tensor_tensor(out=musq, in0=mu4, in1=mu4, op=ALU.mult),
                              f'{tag}lnv', 1)
                        c.wait(v, f'{tag}lnv')
                        c.inc(v.scalar_tensor_tensor(out=var4, in0=cols[:, base + 4: base + 5],
                                                     scalar=1.0 / D, in1=musq,
                                                     op0=ALU.mult, op1=ALU.subtract),
                              f'{tag}lnv', 1)
                    blk.vector(fv2)
                    def fs2(s, tt=tt):
                        c.wait(s, f'{tag}lnv', c.n(f'{tag}lnv'))
                        c.inc(s.activation(out=cols[:, 1328 + tt: 1329 + tt],
                                           in_=cols[:, 1312 + tt: 1313 + tt],
                                           func=AF.Sqrt, bias=smallv[:, 40:41]),
                              f'{tag}sd', 1)
                    blk.scalar(fs2)
                    def fv3(v, q=q, tt=tt):
                        c.wait(v, f'{tag}sd', c.n(f'{tag}sd'))
                        rs4 = cols[:, 1344 + tt: 1345 + tt]
                        murs = cols[:, 1360 + tt: 1361 + tt]
                        c.inc(v.reciprocal(rs4, cols[:, 1328 + tt: 1329 + tt]), f'{tag}lnw', 1)
                        c.wait(v, f'{tag}lnw')
                        c.inc(v.tensor_tensor(out=murs, in0=cols[:, 1264 + tt: 1265 + tt],
                                              in1=rs4, op=ALU.mult), f'{tag}lnw', 1)
                        c.wait(v, f'{tag}lnw')
                        c.inc(v.tensor_scalar(out=z1tok[:, (q % 2) * 2052: (q % 2) * 2052 + 2048],
                                              in0=z1tok[:, (q % 2) * 2052: (q % 2) * 2052 + 2048],
                                              scalar1=1.0, scalar2=murs,
                                              op0=ALU.mult, op1=ALU.subtract), f'{tag}lnw', 1)
                        c.wait(v, f'{tag}lnw')
                        c.inc(v.scalar_tensor_tensor(out=asm[q], in0=asm[q], scalar=rs4,
                                                     in1=z1tok[:, (q % 2) * 2052: (q % 2) * 2052 + 2048],
                                                     op0=ALU.mult, op1=ALU.add),
                              f'{tag}lnw', 1)
                    blk.vector(fv3)
                    def fst(sy, q=q, tt=tt):
                        c.wait(sy, f'{tag}lnw', c.n(f'{tag}lnw'))
                        nm = 'so' if q % 2 == 0 else 'sob'
                        c.inc(sy.dma_start(out=out[b * S + tt * 128: b * S + (tt + 1) * 128, :],
                                           in_=asm[q]), f'{tag}{nm}', 16)
                    blk.sync(fst)
            def fend(sy):
                c.wait(sy, f'{tag}so', B['so'] + 8 * 16)
                c.wait(sy, f'{tag}sob', B['sob'] + 8 * 16)
                c.inc(sy.nop(), f'{tag}done', 1)
            blk.sync(fend)

        ph_const()
        if DEV_PHASES >= 6:
            ph_prepA('prA')
            ph_prepB('prB')
        last_done = {1: 'p2', 2: 'p2', 3: 'c3', 4: 'r4', 5: 'd5', 6: 'w6',
                     7: 'q5', 8: 'a5', 9: 'm6', 10: 'o6'}
        for b in range(DEV_NB):
            lp = last_done.get(min(DEV_PHASES, max(last_done)), 'p2')
            prev = f'{lp}done' if b > 0 else ('prBdone' if DEV_PHASES >= 6 else None)
            if DEV_PHASES >= 1:
                ph_p1(b, 'p1', xin[b * S:(b + 1) * S, :], prev)
            if DEV_PHASES >= 2:
                ph_p2(b, 'p2', 'p1')
            if DEV_PHASES >= 3:
                ph_conv(b, 'c3', 'p2')
            if DEV_PHASES >= 4:
                ph_rows(b, 'r4', 'c3')
            if DEV_PHASES >= 5:
                ph_delta(b, 'd5', 'r4')
            if DEV_PHASES >= 6:
                ph_wout(b, 'w6', 'd5')
            if DEV_PHASES >= 7:
                ph_p1(b, 'z2', x2_d[b * S:(b + 1) * S, :], 'w6done', mode='z2')
                ph_qg(b, 'q5', 'z2', 'w6')
            if DEV_PHASES >= 8:
                ph_attn(b, 'a5', 'q5', 'z2', 'w6')
                XB['a5'] = {'pt': c.n('a5pt'), 'lq0': c.n('a5lq0'),
                            'lq1': c.n('a5lq1')}
            if DEV_PHASES >= 9:
                ph_mlp(b, 'm6', 'a5', 'q5')
            if DEV_PHASES >= 10:
                ph_out(b, 'o6', 'm6', 'z2', 'w6', 'a5')
                XB['o6'] = {'lm0': c.n('o6lm0'), 'lm1': c.n('o6lm1')}
        def fin(sy):
            for nm in ('prAdone', 'prBdone', 'p2done', 'c3done', 'r4done', 'd5done',
                       'w6done', 'q5done', 'a5done', 'm6done', 'o6done'):
                if c.n(nm):
                    c.wait(sy, nm)
        blk.sync(fin)
    return nc, scratch


def _consts():
    ident = np.eye(128, dtype=np.float32)
    sel = np.zeros((48, 4096), np.float32)
    for g in range(32):
        sel[g, g * 128:(g + 1) * 128] = 1.0
        sel[32 + g // 2, g * 128:(g + 1) * 128] = 1.0
    tl = np.tril(np.ones((128, 128), np.float32), -1)
    mA = np.where(tl > 0, np.float32(0.0), np.float32(-1e30))
    mB = np.where(tl.T > 0, np.float32(0.0), np.float32(-1e30))
    tu = np.triu(np.ones((128, 128), np.float32), 0)
    mW = np.where(tu > 0, np.float32(-0.5 * np.log(128.0)), np.float32(-1e30))
    mU = np.where(np.triu(np.ones((128, 128), np.float32), 0) > 0,
                  np.float32(0.0), np.float32(-1e30))
    return dict(c_identb=ident.astype(bfnp), c_identf=ident,
                c_maskUb=mU.astype(bfnp),
                c_onesrow=np.ones((1, 128), np.float32),
                c_sel=sel, c_ones=np.ones((128, 1), bfnp),
                c_maskA=mA.astype(np.float32), c_maskB=mB.astype(np.float32),
                c_maskW=mW.astype(np.float32))


def _in_map(core, inputs):
    x = inputs['L0_attn_hidden_states']
    m = dict(
        xin=np.ascontiguousarray(x[NB * core:NB * core + NB].reshape(NB * S, D)),
        w_qkvz=inputs['L0_attn_w_qkvz'],
        w_ba=inputs['L0_attn_w_ba'],
        convw=np.ascontiguousarray(inputs['L0_attn_conv_w'].reshape(8192, 4)),
        w_out=inputs['L0_attn_w_out'],
        Wq=inputs['L0_ffn_Wq'], Wk=inputs['L0_ffn_Wk'], Wv=inputs['L0_ffn_Wv'],
        Wo=inputs['L0_ffn_Wo'],
        Amat=inputs['L0_ffn_A'], Bmat=inputs['L0_ffn_B'],
        Win=inputs['L0_ffn_W_in_base'], Wup=inputs['L0_ffn_W_up_base'],
        Wdn=inputs['L0_ffn_W_down'],
        dtb_col=np.ascontiguousarray(inputs['L0_attn_dt_bias'].reshape(32, 1)),
        alog_col=np.ascontiguousarray(inputs['L0_attn_A_log'].reshape(32, 1)),
        normw_col=np.ascontiguousarray(inputs['L0_attn_norm_w'].reshape(128, 1)),
        ln1w_col=np.ascontiguousarray(inputs['L0_ffn_ln1_w'].reshape(16, 128).T),
        ln2w_col=np.ascontiguousarray(inputs['L0_ffn_ln2_w'].reshape(16, 128).T),
        ln1w_row=np.ascontiguousarray(inputs['L0_ffn_ln1_w'].reshape(1, D)),
    )
    m.update(_consts())
    return m


def kernel(**inputs):
    inputs = {k: np.asarray(v) for k, v in inputs.items()}
    nc, _ = build()
    in_maps = [_in_map(cr, inputs) for cr in range(DEV_CORES)]
    res = run_bass_kernel_spmd(nc, in_maps, core_ids=list(range(DEV_CORES)))
    outs = [np.asarray(res.results[cr]['out']).reshape(NB, S, D)
            for cr in range(DEV_CORES)]
    return np.concatenate(outs, axis=0)
